# version 13
# baseline (speedup 1.0000x reference)
import numpy as np
import concourse.bass as bass
import concourse.mybir as mybir

F32 = mybir.dt.float32
BF16 = mybir.dt.bfloat16
I32 = mybir.dt.int32
U32 = mybir.dt.uint32
I16 = mybir.dt.int16
ALU = mybir.AluOpType
AF = mybir.ActivationFunctionType
AX = mybir.AxisListType

ENGINES = ("sync", "tensor", "vector", "scalar", "gpsimd")


class Ctr:
    def __init__(self, sem, name):
        self.sem = sem
        self.name = name
        self.n = 0


class Prog:
    def __init__(self, nc):
        self.nc = nc
        self.ops = {e: [] for e in ENGINES}
        self._ctrs = []
        self._stack = None

    def ctr(self, sem, name):
        c = Ctr(sem, name)
        self._ctrs.append(c)
        return c

    def do(self, eng, fn, wait=(), inc=None, inc_k=1):
        waits = []
        def add(w):
            if isinstance(w, list):
                for x in w:
                    add(x)
            elif isinstance(w, tuple):
                waits.append((w[0], w[1]))
            else:
                waits.append((w, w.n))
        for w in wait:
            add(w)
        if inc is not None:
            inc.n += inc_k
        self.ops[eng].append((waits, fn, inc, inc_k))

    def emit(self, block):
        def mk(eng_name):
            def emit_engine(e):
                last_wait = {}
                for waits, fn, inc, inc_k in self.ops[eng_name]:
                    for c, n in waits:
                        if n > 0 and last_wait.get(c, -1) < n:
                            e.wait_ge(c.sem, n)
                            last_wait[c] = n
                    inst = fn(e) if fn is not None else None
                    if inc is not None:
                        if inst is None:
                            inst = e.drain()
                        inst.then_inc(inc.sem, inc_k)
            return emit_engine
        for eng_name in ENGINES:
            if self.ops[eng_name]:
                getattr(block, eng_name)(mk(eng_name))



D = 64


class Attn:
    def __init__(self, nc, P, ctx, cfg, semctx=None):
        self.nc, self.P, self.cfg = nc, P, cfg
        T, DM, HPC = cfg["T"], cfg["DM"], cfg["HPC"]
        self.T, self.DM, self.HPC = T, DM, HPC
        self.QT = 512
        self.TCH = 512
        self.SHIFT = self.QT - 128
        self.SW_W = T + self.SHIFT + 1
        self.NA = T // 128
        self.NDK = DM // 128
        e = ctx.enter_context
        s = lambda *a: e(nc.sbuf_tensor(*a))
        self.XS = 128
        self.xs_f = s("xs_f", [128, 2, (self.XS // 128) * DM], F32)
        self.xs_h = s("xs_h", [128, 2, (self.XS // 128) * DM], BF16)
        self.xs_l = s("xs_l", [128, 2, (self.XS // 128) * DM], BF16)
        self.xfm_h = s("xfm_h", [128, 2, self.TCH], BF16)
        self.xfm_l = s("xfm_l", [128, 2, self.TCH], BF16)
        self.w_f = s("w_f", [128, self.NDK, 128], F32)
        self.w_h = s("w_h", [128, self.NDK, 384], BF16)
        self.w_l = s("w_l", [128, self.NDK, 384], BF16)
        self.wp_f = s("wp_f", [128, DM], F32)
        self.wp_h = s("wp_h", [128, DM], BF16)
        self.wp_l = s("wp_l", [128, DM], BF16)
        self.q_h = s("q_h", [128, T], BF16)
        self.q_l = s("q_l", [128, T], BF16)
        self.k_h = s("k_h", [128, T], BF16)
        self.k_l = s("k_l", [128, T], BF16)
        self.v_tm = s("v_tm", [128, self.NA, 132], F32)
        self.vfm_sb = s("vfm_sb", [128, 2, self.TCH], F32)
        self.SL = s("SL", [128, self.SW_W], F32)
        self.Smask = s("Smask", [128, self.SW_W], BF16)
        self.Sw = s("Sw", [128, 2, self.SW_W], F32)
        self.dec = s("dec", [1, max(HPC, 2)], F32)
        self.decb = s("decb", [128, max(HPC, 2)], F32)
        self.ones1 = s("ones1", [1, 128], F32)
        self.ident = s("ident", [128, 128], F32)
        self.c1 = s("c1", [1, 1], F32)
        self.c1b = s("c1b", [128, 1], F32)
        self.c0125 = s("c0125", [128, 1], F32)
        self.cm1 = s("cm1", [128, 1], F32)
        self.pw = s("pw", [128, 2, 2, self.QT], F32)
        self.rr = s("rr", [1, self.QT], F32)
        self.bc_sb = s("bc_sb", [64, self.QT], F32)
        self.yf = s("yf", [128, self.QT], F32)
        self.y_h = s("y_h", [128, T], BF16)
        self.y_l = s("y_l", [128, T], BF16)
        self.po_sb = s("po_sb", [128, 2, DM], F32)
        self.ps = [e(nc.psum_tensor(f"ps{i}", [128, 512], F32)) for i in range(8)]
        se = (semctx or ctx).enter_context
        c = lambda name: P.ctr(se(nc.semaphore(name)), name)
        self.c_x = [c("a_x0"), c("a_x1")]; self.c_xo = [c("a_xo0"), c("a_xo1")]
        self.c_prv = c("a_prv"); self.c_prg = c("a_prg"); self.c_prs = c("a_prs"); self.c_prt = c("a_prt")
        self.c_xd = c("a_xd")
        self.c_wL = c("a_wL"); self.c_wG = c("a_wG"); self.c_wS = c("a_wS")
        self.c_fm = [c("a_fm0"), c("a_fm1")]; self.c_qkv = c("a_qkv")
        self.c_evS = c("a_evS"); self.c_evV = c("a_evV")
        self.c_vtT = c("a_vtT"); self.c_vtV = c("a_vtV"); self.c_sw = c("a_sw"); self.c_swS = c("a_swS")
        self.c_s = c("a_s"); self.c_exp = c("a_exp"); self.c_pw = c("a_pw")
        self.c_avn = c("a_avn")
        self.c_epV = c("a_epV"); self.c_epT = c("a_epT"); self.c_epS = c("a_epS")
        self.c_po = c("a_po"); self.c_pev = c("a_pev"); self.c_pst = [c("a_pst0"), c("a_pst1")]
        self._ep_last_n = 0
        self._exp_last_n = 0
        self._avn_last_n = 0
        self._pev_end = 0
        self._po_end = 0
        self._fm_prev_hp = (self.c_qkv, 0)
        self._flash_end = [(self.c_epV, 0), (self.c_epS, 0), (self.c_epT, 0)]

    def srecip(self, out, in_, ctrS, wait=(), nP=128):
        P = self.P
        P.do("scalar", lambda e: e.activation(out, in_, AF.Ln), wait=list(wait), inc=ctrS)
        P.do("scalar", lambda e, nP=nP: e.activation(out, out, AF.Exp, scale=self.cm1[0:nP, :]),
             wait=[(ctrS, ctrS.n)], inc=ctrS)

    def emit_xsplit(self, x_dram, xhb, xlb):
        P, T, DM, XS = self.P, self.T, self.DM, self.XS
        na = XS // 128
        for i, t0 in enumerate(range(0, T, XS)):
            par = i % 2
            cx, cxo = self.c_x[par], self.c_xo[par]
            a0 = t0 // 128
            sl = x_dram.rearrange("(a p) d -> p a d", p=128)[:, a0:a0 + na, :]
            w = [(cxo, cxo.n)] if i >= 2 else []
            P.do("sync", lambda e, sl=sl, par=par: e.dma_start(out=self.xs_f[:, par, :].rearrange("p (a d) -> p a d", d=self.DM), in_=sl),
                 wait=w, inc=cx, inc_k=16)
            xn = (cx, cx.n)
            P.do("vector", lambda e, par=par: e.tensor_copy(self.xs_h[:, par, :], self.xs_f[:, par, :]),
                 wait=[xn], inc=self.c_prv)
            P.do("vector", lambda e, par=par: e.tensor_tensor(self.xs_l[:, par, :], self.xs_f[:, par, :], self.xs_h[:, par, :], op=ALU.subtract),
                 wait=[(self.c_prv, self.c_prv.n)], inc=self.c_prv)
            pn = (self.c_prv, self.c_prv.n)
            hb = xhb.rearrange("(a p) d -> p a d", p=128)[:, a0:a0 + na, :]
            lb = xlb.rearrange("(a p) d -> p a d", p=128)[:, a0:a0 + na, :]
            P.do("gpsimd", lambda e, hb=hb, par=par: e.dma_start(out=hb, in_=self.xs_h[:, par, :].rearrange("p (a d) -> p a d", d=self.DM)),
                 wait=[pn], inc=cxo, inc_k=16)
            P.do("gpsimd", lambda e, lb=lb, par=par: e.dma_start(out=lb, in_=self.xs_l[:, par, :].rearrange("p (a d) -> p a d", d=self.DM)),
                 inc=cxo, inc_k=16)
        self.xsplit_done = [(self.c_xo[0], self.c_xo[0].n), (self.c_xo[1], self.c_xo[1].n)]

    def emit_tables(self, draw):
        P, HPC, W, SHIFT = self.P, self.HPC, self.SW_W, self.SHIFT
        P.do("vector", lambda e: e.memset(self.ones1[:, :], 1.0), inc=self.c_prv)
        P.do("vector", lambda e: e.memset(self.c1[:, :], 1.0), inc=self.c_prv)
        P.do("vector", lambda e: e.memset(self.c1b[:, :], 1.0), inc=self.c_prv)
        P.do("vector", lambda e: e.memset(self.c0125[:, :], 0.125), inc=self.c_prv)
        P.do("vector", lambda e: e.memset(self.cm1[:, :], -1.0), inc=self.c_prv)
        consts_n = (self.c_prv, self.c_prv.n)
        P.do("gpsimd", lambda e: e.iota(self.ident[:, :], pattern=[[1, 128]], base=0,
                                        channel_multiplier=-1, allow_small_or_imprecise_dtypes=True),
             inc=self.c_prg)
        idn = (self.c_prg, self.c_prg.n)
        P.do("vector", lambda e: e.tensor_scalar(self.ident[:, :], self.ident[:, :], 0.0, None, op0=ALU.is_equal),
             wait=[idn], inc=self.c_prv)
        self.ident_ready = (self.c_prv, self.c_prv.n)
        P.do("gpsimd", lambda e: e.iota(self.SL[:, :], pattern=[[1, W]], base=-SHIFT,
                                        channel_multiplier=-1, allow_small_or_imprecise_dtypes=True),
             inc=self.c_prg)
        sln = (self.c_prg, self.c_prg.n)
        P.do("vector", lambda e: e.tensor_scalar(self.Smask[:, :], self.SL[:, :], 0.0, None, op0=ALU.is_ge),
             wait=[sln], inc=self.c_prv)
        mkn = (self.c_prv, self.c_prv.n)
        P.do("scalar", lambda e: e.activation(self.SL[:, :], self.SL[:, :], AF.Relu),
             wait=[mkn], inc=self.c_prs)
        P.do("scalar", lambda e: e.activation(self.SL[:, :], self.SL[:, :], AF.Ln, bias=self.c1b[:, :], scale=1.0),
             wait=[(self.c_prs, self.c_prs.n)], inc=self.c_prs)
        self.SL_done = (self.c_prs, self.c_prs.n)
        P.do("gpsimd", lambda e, draw=draw: e.dma_start(out=self.dec[0:1, 0:self.HPC], in_=draw),
             inc=self.c_xd, inc_k=16)
        dn = (self.c_xd, self.c_xd.n)
        P.do("scalar", lambda e: e.activation(self.dec[0:1, 0:self.HPC], self.dec[0:1, 0:self.HPC], AF.Exp),
             wait=[dn, consts_n], inc=self.c_prs)
        P.do("scalar", lambda e: e.activation(self.dec[0:1, 0:self.HPC], self.dec[0:1, 0:self.HPC], AF.Ln, bias=self.c1[0:1, 0:1], scale=1.0),
             wait=[(self.c_prs, self.c_prs.n)], inc=self.c_prs)
        decn = (self.c_prs, self.c_prs.n)
        P.do("tensor", lambda e: e.matmul(self.ps[7][:, 0:self.HPC], self.ones1[:, :], self.dec[0:1, 0:self.HPC], start=True, stop=True),
             wait=[decn, consts_n], inc=self.c_prt)
        bn = (self.c_prt, self.c_prt.n)
        P.do("vector", lambda e: e.tensor_copy(self.decb[:, 0:self.HPC], self.ps[7][:, 0:self.HPC]),
             wait=[bn], inc=self.c_prv)
        self.tables_done = [(self.c_prv, self.c_prv.n), self.SL_done]

    def emit_hp(self, hp, io):
        P, T, DM = self.P, self.T, self.DM
        TCH, QT, SHIFT = self.TCH, self.QT, self.SHIFT
        NDK, NA = self.NDK, self.NA
        prev_flash = self._flash_end
        prev_pev = (self.c_pev, self._pev_end)
        prev_po = (self.c_po, self._po_end)

        wq, wk, wv, wp = io["wq"], io["wk"], io["wv"], io["wp"]
        for j, w in enumerate((wq, wk, wv)):
            sl = w.rearrange("(a p) c -> p a c", p=128)[:, :, 128 * hp:128 * (hp + 1)]
            wprev = (self.c_wS, self.c_wS.n)
            P.do("sync", lambda e, sl=sl: e.dma_start(out=self.w_f[:, :, :], in_=sl),
                 wait=[prev_po, wprev], inc=self.c_wL, inc_k=16)
            P.do("gpsimd", lambda e, sl=sl, j=j: e.dma_start(out=self.w_h[:, :, 128 * j:128 * (j + 1)], in_=sl),
                 wait=[prev_po, wprev], inc=self.c_wG, inc_k=16)
            wn = [(self.c_wL, self.c_wL.n), (self.c_wG, self.c_wG.n)]
            P.do("vector", lambda e, j=j: e.tensor_tensor(self.w_l[:, :, 128 * j:128 * (j + 1)], self.w_f[:, :, :], self.w_h[:, :, 128 * j:128 * (j + 1)], op=ALU.subtract),
                 wait=wn, inc=self.c_wS)
        wprev = (self.c_wS, self.c_wS.n)
        slp = wp.rearrange("(a p) c -> p a c", p=128)[:, hp, :]
        P.do("sync", lambda e, slp=slp: e.dma_start(out=self.wp_f[:, :], in_=slp),
             wait=[prev_po, wprev], inc=self.c_wL, inc_k=16)
        P.do("gpsimd", lambda e, slp=slp: e.dma_start(out=self.wp_h[:, :], in_=slp),
             wait=[prev_po, wprev], inc=self.c_wG, inc_k=16)
        wn2 = [(self.c_wL, self.c_wL.n), (self.c_wG, self.c_wG.n)]
        P.do("vector", lambda e: e.tensor_tensor(self.wp_l[:, :], self.wp_f[:, :], self.wp_h[:, :], op=ALU.subtract),
             wait=wn2, inc=self.c_wS)
        w_ready = [(self.c_wS, self.c_wS.n), (self.c_wG, self.c_wG.n)]

        P.do("vector", lambda e: e.memset(self.v_tm[:, :, 64:65], 1.0), wait=[prev_flash], inc=self.c_vtV)
        P.do("vector", lambda e: e.memset(self.v_tm[:, :, 130:131], 1.0), wait=[(self.c_vtV, self.c_vtV.n)], inc=self.c_vtV)

        xhb, xlb = io["xhb"], io["xlb"]
        ev_after, qkv_after, vt_after = {}, {}, {}
        fm_used = {}
        pstr_last = {0: (self.c_vtV, 0), 1: (self.tables_done if hp == 0 else (self.c_vtV, 0))}
        for it, t0 in enumerate(range(0, T, TCH)):
            par = it % 2
            psq, psk, psv = (self.ps[0], self.ps[1], self.ps[2]) if par == 0 else (self.ps[4], self.ps[5], self.ps[6])
            for dk in range(NDK):
                g = it * NDK + dk
                fpar = g % 2
                cfm = self.c_fm[fpar]
                for src, dst in ((xhb, self.xfm_h), (xlb, self.xfm_l)):
                    sl = src[t0:t0 + TCH, 128 * dk:128 * (dk + 1)]
                    w = list(self.xsplit_done) + [self._fm_prev_hp]
                    if g >= 2:
                        w.append((self.c_qkv, fm_used[g - 2]))
                    P.do("sync", lambda e, sl=sl, dst=dst, fpar=fpar: e.dma_start(out=dst[:, fpar, :], in_=sl, transpose=True),
                         wait=w, inc=cfm, inc_k=16)
                fmn = (cfm, cfm.n)
                for j, ps in enumerate((psq, psk, psv)):
                    for pi, (wt, xt) in enumerate(((self.w_h, self.xfm_h), (self.w_h, self.xfm_l), (self.w_l, self.xfm_h))):
                        w = [w_ready, fmn, prev_pev]
                        if it >= 2:
                            w.append(ev_after[it - 2])
                        first_of_dk = (pi == 0 and j == 0)
                        last = (j == 2 and pi == 2)
                        P.do("tensor", lambda e, wt=wt, xt=xt, ps=ps, dk=dk, j=j, pi=pi, fpar=fpar: e.matmul(
                            ps[:, :], wt[:, dk, 128 * j:128 * (j + 1)], xt[:, fpar, :],
                            start=(dk == 0 and pi == 0), stop=(dk == NDK - 1 and pi == 2)),
                            wait=(w if first_of_dk else []), inc=(self.c_qkv if last else None))
                fm_used[g] = self.c_qkv.n
            qkv_after[it] = self.c_qkv.n
            qn = (self.c_qkv, self.c_qkv.n)
            for (ps, hi, lo) in ((psq, self.q_h, self.q_l), (psk, self.k_h, self.k_l)):
                P.do("scalar", lambda e, ps=ps, hi=hi, t0=t0: e.copy(hi[:, t0:t0 + TCH], ps[:, :]),
                     wait=[qn], inc=self.c_evS)
                hin = (self.c_evS, self.c_evS.n)
                P.do("vector", lambda e, ps=ps, hi=hi, lo=lo, t0=t0: e.tensor_tensor(
                    lo[:, t0:t0 + TCH], ps[:, :], hi[:, t0:t0 + TCH], op=ALU.subtract),
                    wait=[hin], inc=self.c_evV)
            w = [qn]
            if it >= 2:
                w.append((self.c_vtV, vt_after[it - 2]))
            P.do("vector", lambda e, psv=psv, par=par: e.tensor_copy(self.vfm_sb[:, par, :], psv[:, :]),
                 wait=w, inc=self.c_evV)
            ev_after[it] = [(self.c_evS, self.c_evS.n), (self.c_evV, self.c_evV.n)]
            vev = (self.c_evV, self.c_evV.n)
            for c in range(TCH // 128):
                cg = t0 // 128 + c
                pp = c % 2
                pstr = self.ps[3] if pp == 0 else self.ps[7]
                P.do("tensor", lambda e, c=c, par=par, pstr=pstr: e.transpose(
                    pstr[:, 0:128], self.vfm_sb[:, par, 128 * c:128 * (c + 1)], self.ident[:, :]),
                    wait=[vev, pstr_last[pp], self.ident_ready], inc=self.c_vtT)
                tn = (self.c_vtT, self.c_vtT.n)
                P.do("vector", lambda e, cg=cg, pstr=pstr: e.tensor_copy(self.v_tm[:, cg, 0:64], pstr[0:128, 0:64]),
                     wait=[tn], inc=self.c_vtV)
                P.do("vector", lambda e, cg=cg, pstr=pstr: e.tensor_copy(self.v_tm[:, cg, 66:130], pstr[0:128, 64:128]),
                     wait=[(self.c_vtV, self.c_vtV.n)], inc=self.c_vtV)
                pstr_last[pp] = (self.c_vtV, self.c_vtV.n)
            vt_after[it] = self.c_vtV.n
            if it == min(1, T // TCH - 1):
                for h2 in range(2):
                    h = 2 * hp + h2
                    P.do("vector", lambda e, h2=h2, h=h: e.tensor_scalar(
                        self.Sw[:, h2, :], self.SL[:, :], self.decb[:, h:h + 1], 1.0, op0=ALU.mult, op1=ALU.add),
                        wait=[self.tables_done, prev_flash], inc=self.c_sw)
                    self.srecip(self.Sw[:, h2, :], self.Sw[:, h2, :], self.c_swS, wait=[(self.c_sw, self.c_sw.n)])
                    P.do("vector", lambda e, h2=h2: e.tensor_tensor(self.Sw[:, h2, :], self.Sw[:, h2, :], self.Smask[:, :], op=ALU.mult),
                         wait=[(self.c_swS, self.c_swS.n)], inc=self.c_sw)
                sw_ready = (self.c_sw, self.c_sw.n)
        self._fm_prev_hp = (self.c_qkv, self.c_qkv.n)
        qkv_all = (self.c_vtV, self.c_vtV.n)
        qk_ready = [(self.c_evS, self.c_evS.n), (self.c_evV, self.c_evV.n)]
        if self.cfg.get("qkv_only"):
            self.attn_done = [qk_ready, qkv_all]
            return

        exp_after, avn_after = {}, {}
        bc_copy_n = {}
        ep_by_par = getattr(self, "_ep_by_par", {0: 0, 1: 0})
        for qi, q0 in enumerate(range(0, T, QT)):
            qpar = qi % 2
            psav = [self.ps[4 + 2 * qpar + h2] for h2 in range(2)]
            psbc = [self.ps[4 + 2 * (1 - qpar) + h2] for h2 in range(2)]
            cmax = min(T, q0 + QT) // 128 if self.cfg.get("causal", True) else T // 128
            ep_prev = (self.c_epV, ep_by_par[qpar])

            def emit_s(c):
                par = c % 2
                for h2 in range(2):
                    pss = self.ps[2 * par + h2]
                    b0 = 64 * h2
                    w = [qkv_all, qk_ready, sw_ready]
                    if (q0, c - 2, h2) in exp_after:
                        w.append((self.c_exp, exp_after[(q0, c - 2, h2)]))
                    else:
                        w.append((self.c_exp, self._exp_last_n))
                    for pi, (kk, qq) in enumerate(((self.k_h, self.q_h), (self.k_h, self.q_l), (self.k_l, self.q_h))):
                        P.do("tensor", lambda e, pss=pss, kk=kk, qq=qq, b0=b0, c=c, q0=q0, pi=pi: e.matmul(
                            pss[:, :], kk[b0:b0 + 64, 128 * c:128 * (c + 1)], qq[b0:b0 + 64, q0:q0 + QT],
                            start=(pi == 0), stop=(pi == 2), tile_position=(b0, 0)),
                            wait=(w if pi == 0 else []), inc=(self.c_s if pi == 2 else None))
                    sn = (self.c_s, self.c_s.n)
                    w = [sn]
                    if (q0, c - 2, h2) in avn_after:
                        w.append((self.c_avn, avn_after[(q0, c - 2, h2)]))
                    else:
                        w.append((self.c_avn, self._avn_last_n))
                    P.do("scalar", lambda e, pss=pss, h2=h2, par=par: e.activation(
                        self.pw[:, h2, par, :], pss[:, :], AF.Exp, scale=self.c0125[:, :]),
                        wait=w, inc=self.c_exp)
                    exp_after[(q0, c, h2)] = self.c_exp.n
                    en = (self.c_exp, self.c_exp.n)
                    u0 = SHIFT + q0 - 128 * c
                    P.do("vector", lambda e, h2=h2, par=par, u0=u0: e.tensor_tensor(
                        self.pw[:, h2, par, :], self.pw[:, h2, par, :], self.Sw[:, h2, u0:u0 + QT], op=ALU.mult),
                        wait=[en], inc=self.c_pw)

            def emit_av(c):
                par = c % 2
                for h2 in range(2):
                    pn = (self.c_pw, pw_n[(c, h2)])
                    w = [pn]
                    if c == 0:
                        w.append(ep_prev)
                        if (q0, h2) in bc_prev:
                            w.append((self.c_epS, bc_prev[(q0, h2)]))
                    P.do("tensor", lambda e, h2=h2, par=par, c=c, q0=q0, cmax=cmax, psav=psav: e.matmul(
                        psav[h2][0:65, :], self.v_tm[:, c, 66 * h2:66 * h2 + 65], self.pw[:, h2, par, :],
                        start=(c == 0), stop=(c == cmax - 1)),
                        wait=w, inc=self.c_avn)
                    avn_after[(q0, c, h2)] = self.c_avn.n

            pw_n = {}
            bc_prev = getattr(self, "_bc_prev", {})
            for c in range(cmax):
                sbase = self.c_pw.n
                emit_s(c)
                pw_n[(c, 0)] = sbase + 1
                pw_n[(c, 1)] = sbase + 2
                if c >= 1:
                    emit_av(c - 1)
            emit_av(cmax - 1)
            bn_prev = None
            for h2 in range(2):
                avn = (self.c_avn, avn_after[(q0, cmax - 1, h2)])
                w = [avn] + ([bn_prev] if bn_prev else [])
                self.srecip(self.rr[0:1, :], psav[h2][64:65, :], self.c_epS, wait=w, nP=1)
                rn = (self.c_epS, self.c_epS.n)
                P.do("tensor", lambda e, h2=h2, psbc=psbc: e.matmul(psbc[h2][0:64, :], self.ones1[0:1, 0:64], self.rr[0:1, :], start=True, stop=True),
                     wait=[rn], inc=self.c_epT)
                bn = (self.c_epT, self.c_epT.n)
                bn_prev = bn
                P.do("scalar", lambda e, h2=h2, psbc=psbc: e.copy(self.bc_sb[:, :], psbc[h2][0:64, :]),
                     wait=[bn], inc=self.c_epS)
                self._bc_prev = getattr(self, "_bc_prev", {})
                self._bc_prev[(q0 + QT, h2)] = self.c_epS.n
                cn = (self.c_epS, self.c_epS.n)
                b0 = 64 * h2
                P.do("vector", lambda e, h2=h2, b0=b0, psav=psav: e.tensor_tensor(self.yf[b0:b0 + 64, :], psav[h2][0:64, :], self.bc_sb[:, :], op=ALU.mult),
                     wait=[cn], inc=self.c_epV)
                yn = (self.c_epV, self.c_epV.n)
                P.do("scalar", lambda e, b0=b0, q0=q0: e.copy(self.y_h[b0:b0 + 64, q0:q0 + QT], self.yf[b0:b0 + 64, :]),
                     wait=[yn, prev_po], inc=self.c_epS)
                hn = (self.c_epS, self.c_epS.n)
                P.do("vector", lambda e, b0=b0, q0=q0: e.tensor_tensor(
                    self.y_l[b0:b0 + 64, q0:q0 + QT], self.yf[b0:b0 + 64, :], self.y_h[b0:b0 + 64, q0:q0 + QT], op=ALU.subtract),
                    wait=[hn], inc=self.c_epV)
            ep_by_par[qpar] = self.c_epV.n
            self._exp_last_n = self.c_exp.n
            self._avn_last_n = self.c_avn.n
        self._ep_by_par = ep_by_par
        self._bc_prev = {(0, h2): self._bc_prev[(T, h2)] for h2 in range(2)}
        self._flash_end = [(self.c_epV, self.c_epV.n), (self.c_epS, self.c_epS.n), (self.c_epT, self.c_epT.n)]
        flash_done = self._flash_end

        if self.cfg.get("no_proj"):
            self.attn_done = [flash_done]
            return
        part = io["part"]
        pev_after = {}
        pst_start = [self.c_pst[0].n, self.c_pst[1].n]
        pst_prev_total = getattr(self, "_pst_totals", [(self.c_pst[0], 0), (self.c_pst[1], 0)])
        for tc in range(NA):
            par = tc % 2
            for hi_, h0 in enumerate(range(0, DM, 512)):
                w512 = min(512, DM - h0)
                pspo = self.ps[2 * par + hi_]
                for pi, (yy, ww) in enumerate(((self.y_h, self.wp_h), (self.y_l, self.wp_h), (self.y_h, self.wp_l))):
                    w = [flash_done]
                    if tc >= 2:
                        w.append((self.c_pev, pev_after[tc - 2]))
                    P.do("tensor", lambda e, pspo=pspo, yy=yy, ww=ww, tc=tc, h0=h0, w512=w512, pi=pi: e.matmul(
                        pspo[:, 0:w512], yy[:, 128 * tc:128 * (tc + 1)], ww[:, h0:h0 + w512],
                        start=(pi == 0), stop=(pi == 2)),
                        wait=(w if pi == 0 else []), inc=(self.c_po if pi == 2 else None))
                pon = (self.c_po, self.c_po.n)
                wv_ = [pon]
                if tc >= 2:
                    wv_.append((self.c_pst[par], pst_start[par] + 16 * (tc // 2)))
                else:
                    wv_.extend(pst_prev_total)
                P.do("vector", lambda e, pspo=pspo, par=par, h0=h0, w512=w512: e.tensor_copy(
                    self.po_sb[:, par, h0:h0 + w512], pspo[:, 0:w512]),
                    wait=wv_, inc=self.c_pev)
            pev_after[tc] = self.c_pev.n
            pvn = (self.c_pev, self.c_pev.n)
            dst = part.rearrange("(a p) d -> p a d", p=128)[:, tc, :]
            w = [pvn]
            if hp > 0:
                w.extend(pst_prev_total)
            if hp == 0:
                P.do("gpsimd", lambda e, dst=dst, par=par: e.dma_start(out=dst, in_=self.po_sb[:, par, :]),
                     wait=w, inc=self.c_pst[par], inc_k=16)
            else:
                P.do("gpsimd", lambda e, dst=dst, par=par: e.dma_start(out=dst, in_=self.po_sb[:, par, :], accum_op=ALU.add),
                     wait=w, inc=self.c_pst[par], inc_k=16)
        self._pst_totals = [(self.c_pst[0], self.c_pst[0].n), (self.c_pst[1], self.c_pst[1].n)]
        self._pev_end = self.c_pev.n
        self._po_end = self.c_po.n
        self.attn_done = self._pst_totals


class Moe:

    def __init__(self, nc, P, ctx, cfg, semctx=None):
        self.nc, self.P, self.cfg = nc, P, cfg
        TC = cfg["TC"]
        DM = cfg["DM"]
        FF = cfg["FF"]
        self.E = cfg["E"]
        self.CAP = cfg["CAP"]
        self.NCORE = cfg["NCORE"]
        self.TC, self.DM, self.FF = TC, DM, FF
        self.NA = TC // 128
        self.NDK = DM // 128
        self.NFB = (FF + 127) // 128
        self.GCH = cfg.get("GCH", 512)
        self.NB = self.CAP // 128
        self.NCOL = 2 * self.NB + 1
        self.NREC = 2 * self.NA
        self.NCH = self.NCORE * self.NREC
        e = ctx.enter_context
        s = lambda *a: e(nc.sbuf_tensor(*a))
        self.t_x = s("m_x", [128, 2, DM], F32)
        self.t_p1 = s("m_p1", [128, 2, DM], F32)
        self.t_ln = s("m_ln", [128, 3, DM], F32)
        self.t_lh = s("m_lh", [128, 3, DM], BF16)
        self.t_ll = s("m_ll", [128, 3, DM], BF16)
        self.stat = s("m_stat", [128, 8], F32)
        self.eps = s("m_eps", [128, 1], F32)
        self.hofft = s("m_hofft", [1, 1], U32)
        self.e0t = s("m_e0t", [1, 1], F32)
        self.e0col = s("m_e0col", [128, 1], F32)
        self.cm05 = s("m_cm05", [128, 1], F32)
        self.c1e9 = s("m_c1e9", [128, 1], F32)
        self.rinv = s("m_rinv", [128, 1], F32)
        self.mcm1 = s("m_mcm1", [128, 1], F32)
        self.rw_f = s("m_rwf", [128, self.NDK, 16], F32)
        self.rw_h = s("m_rwh", [128, self.NDK, 16], BF16)
        self.rw_l = s("m_rwl", [128, self.NDK, 16], BF16)
        self.rb_b = s("m_rbb", [128, 16], F32)
        self.lg = s("m_lg", [128, 16], F32)
        self.prob = s("m_prob", [128, 16], F32)
        self.m8 = s("m_m8", [128, 8], F32)
        self.i8 = s("m_i8", [128, 8], U32)
        self.idxf = s("m_idxf", [128, self.NA, 2], F32)
        self.valn = s("m_valn", [128, self.NA, 2], F32)
        self.posl = s("m_posl", [128, self.NA, 2], F32)
        self.gsc = s("m_gsc", [128, self.NA, 2], F32)
        self.cgi = s("m_cgi", [128, self.NA, 2], U32)
        self.iota16 = s("m_io16", [128, 16], F32)
        self.iota41 = s("m_io41", [128, self.NCOL], F32)
        self.ltri = s("m_ltri", [128, 128], BF16)
        self.ones128 = s("m_ones128", [128, 1], BF16)
        self.runoff = s("m_runoff", [1, 16], F32)
        self.ohk = s("m_ohk", [128, 2, 16], F32)
        self.ohsum_h = s("m_ohsum", [128, 16], BF16)
        self.posb = s("m_posb", [128, 16], F32)
        self.runb = s("m_runb", [128, 16], F32)
        self.xfm_f = s("m_xfmf", [128, DM // 128, 128], F32)
        self.identm = s("m_identm", [128, 128], F32)
        self.cnt_sb = s("m_cnt", [1, 16], F32)
        self.allcnt = s("m_allcnt", [8, 16], F32)
        self.msk = s("m_msk", [8, 1], F32)
        self.off16 = s("m_off16", [1, 16], F32)
        self.offb = s("m_offb", [128, 16], F32)
        self.recs = s("m_recs", [128, self.NCH], F32)
        self.evi = s("m_evi", [128, self.NCH], I32)
        self.evt = s("m_evt", [128, self.NCH], I32)
        self.ef = s("m_ef", [128, self.NCH], F32)
        self.pf = s("m_pf", [128, self.NCH], F32)
        self.colf = s("m_colf", [128, self.NCH], F32)
        self.rrow = s("m_rrow", [128, self.NCH], F32)
        self.recw = s("m_recw", [128, self.NREC], F32)
        self.io128r = s("m_io128r", [128, 128], F32)
        self.tidt = s("m_tid", [128, self.NCH], F32)
        self.oer = s("m_oer", [128, 2, 128], F32)
        self.oeb = s("m_oeb", [128, 2, self.NCOL], F32)
        self.tbl = s("m_tbl", [128, self.NCOL], F32)
        self.tbli = s("m_tbli", [128, self.NCOL], U32)
        self.xg_h = s("m_xgh", [128, max(self.GCH // 256, 1), DM], BF16)
        self.xg_l = s("m_xgl", [128, max(self.GCH // 256, 1), DM], BF16)
        self.xfm_h = s("m_xfmh", [128, 2, self.NDK, self.GCH], BF16)
        self.xfm_l = s("m_xfml", [128, 2, self.NDK, self.GCH], BF16)
        self.b32 = s("m_b32", [128, 3, 2 * self.NA], F32)
        self.bi32 = s("m_bi32", [128, 3, 2 * self.NA], I32)
        self.wsplit_f = s("m_wsf", [128, 2, FF], F32)
        self.wsplit_h = s("m_wsh", [128, 2, FF], BF16)
        self.wsplit_l = s("m_wsl", [128, 2, FF], BF16)
        self.ones1r = s("m_ones1r", [1, 128], F32)
        self.gw_h = s("m_gwh", [128, self.NDK, 2, 256], BF16)
        self.gw_l = s("m_gwl", [128, self.NDK, 2, 256], BF16)
        self.dw_h = s("m_dwh", [128, 2, DM], BF16)
        self.dw_l = s("m_dwl", [128, 2, DM], BF16)
        self.sg = s("m_sg", [128, self.GCH], F32)
        self.hh = s("m_hh", [128, self.NFB, self.GCH], BF16)
        self.hl = s("m_hl", [128, self.NFB, self.GCH], BF16)
        self.hf = s("m_hf", [128, self.GCH], F32)
        self.eo = s("m_eo", [128, self.GCH // 128, DM], F32)
        self.cmb = s("m_cmb", [128, 2, DM], F32)
        self.fin = s("m_fin", [128, 2, DM], F32)
        self.psm = [e(nc.psum_tensor(f"mps{i}", [128, 512], F32)) for i in range(8)]
        se = (semctx or ctx).enter_context
        c = lambda name: P.ctr(se(nc.semaphore(name)), name)
        self.c_ld = [c("m_ld0"), c("m_ld1")]
        self.c_gd = [c("m_gd0"), c("m_gd1")]
        self.c_v = c("m_v"); self.c_s = c("m_s"); self.c_t = c("m_t"); self.c_g = c("m_g")
        self.c_cc = c("m_cc")
        self.c_st = [c("m_st0"), c("m_st1")]
        self.c_ws = [c("m_ws0"), c("m_ws1")]
        self.c_fm = [c("m_fm0"), c("m_fm1")]
        self.c_wL = c("m_wL"); self.c_wG = c("m_wG"); self.c_wS = c("m_wS")
        self.c_mm = c("m_mm"); self.c_ev = c("m_ev"); self.c_evs = c("m_evs")
        self.c_out = c("m_out")
        self._gu_mm, self._dw_mm, self._eo_ev = {}, {}, {}
        self.GCHB = self.GCH // 128

    def dsplit(self, src_ap_rows, n_rows_chunks, hb, lb, wait_root):
        P = self.P
        for ch in range(n_rows_chunks):
            src, hdst, ldst = src_ap_rows(ch)
            par = ch % 2
            wld = [wait_root] if ch < 2 else [(self.c_st[par], self.c_st[par].n)]
            P.do("sync", lambda e, src=src, par=par: e.dma_start(out=self.wsplit_f[:, par, :src.shape[-1]], in_=src),
                 wait=wld, inc=self.c_ld[par], inc_k=16)
            ldn = (self.c_ld[par], self.c_ld[par].n)
            P.do("vector", lambda e, par=par, W=src.shape[-1]: e.tensor_copy(self.wsplit_h[:, par, 0:W], self.wsplit_f[:, par, 0:W]),
                 wait=[ldn, (self.c_v, self.c_v.n)], inc=self.c_v)
            P.do("vector", lambda e, par=par, W=src.shape[-1]: e.tensor_tensor(
                self.wsplit_l[:, par, 0:W], self.wsplit_f[:, par, 0:W], self.wsplit_h[:, par, 0:W], op=ALU.subtract),
                wait=[(self.c_v, self.c_v.n)], inc=self.c_v)
            vn = (self.c_v, self.c_v.n)
            P.do("gpsimd", lambda e, hdst=hdst, par=par, W=src.shape[-1]: e.dma_start(out=hdst, in_=self.wsplit_h[:, par, 0:W]),
                 wait=[vn], inc=self.c_st[par], inc_k=16)
            P.do("gpsimd", lambda e, ldst=ldst, par=par, W=src.shape[-1]: e.dma_start(out=ldst, in_=self.wsplit_l[:, par, 0:W]),
                 inc=self.c_st[par], inc_k=16)

    def emit_consts(self, io, attn_all):
        P = self.P
        W = [attn_all]
        P.do("vector", lambda e: e.memset(self.eps[:, :], 1e-5), wait=W, inc=self.c_v)
        P.do("vector", lambda e: e.memset(self.cm05[:, :], -0.5), inc=self.c_v)
        P.do("vector", lambda e: e.memset(self.c1e9[:, :], 1e-9), inc=self.c_v)
        P.do("vector", lambda e: e.memset(self.rinv[:, :], 1.0 / self.DM), inc=self.c_v)
        P.do("vector", lambda e: e.memset(self.mcm1[:, :], -1.0), inc=self.c_v)
        P.do("vector", lambda e: e.memset(self.runoff[:, :], 0.0), inc=self.c_v)
        P.do("gpsimd", lambda e: e.iota(self.identm[:, :], pattern=[[1, 128]], base=0, channel_multiplier=-1,
                                        allow_small_or_imprecise_dtypes=True), wait=W, inc=self.c_g)
        gn0 = (self.c_g, self.c_g.n)
        P.do("vector", lambda e: e.tensor_scalar(self.identm[:, :], self.identm[:, :], 0.0, None, op0=ALU.is_equal),
             wait=[gn0], inc=self.c_v)
        P.do("gpsimd", lambda e: e.iota(self.iota16[:, :], pattern=[[1, 16]], base=0, channel_multiplier=0,
                                        allow_small_or_imprecise_dtypes=True), inc=self.c_g)
        P.do("gpsimd", lambda e: e.iota(self.iota41[:, :], pattern=[[1, self.NCOL]], base=0, channel_multiplier=0,
                                        allow_small_or_imprecise_dtypes=True), inc=self.c_g)
        P.do("gpsimd", lambda e: e.iota(self.io128r[:, :], pattern=[[1, 128]], base=0, channel_multiplier=0,
                                        allow_small_or_imprecise_dtypes=True), inc=self.c_g)
        P.do("gpsimd", lambda e: e.iota(self.ltri[:, :], pattern=[[1, 128]], base=0, channel_multiplier=-1,
                                        allow_small_or_imprecise_dtypes=True), inc=self.c_g)
        gn = (self.c_g, self.c_g.n)
        P.do("vector", lambda e: e.tensor_scalar(self.ltri[:, :], self.ltri[:, :], 0.0, None, op0=ALU.is_gt),
             wait=[gn], inc=self.c_v)
        P.do("vector", lambda e: e.memset(self.ones128[:, :], 1.0), inc=self.c_v)
        P.do("vector", lambda e: e.memset(self.ones1r[:, :], 1.0), inc=self.c_v)
        P.do("gpsimd", lambda e: e.iota(self.tidt[:, :], pattern=[[128, self.NCH // 2], [0, 2]], base=0,
                                        channel_multiplier=1, allow_small_or_imprecise_dtypes=True),
             inc=self.c_g)
        P.do("sync", lambda e, m=io["msk"]: e.dma_start(out=self.msk[:, :], in_=m), wait=W, inc=self.c_ld[0], inc_k=16)
        P.do("sync", lambda e, m=io["e0in"]: e.dma_start(out=self.e0t[:, :], in_=m), inc=self.c_ld[0], inc_k=16)
        e0n = (self.c_ld[0], self.c_ld[0].n)
        rw = io["router_w"].rearrange("(a p) c -> p a c", p=128)
        P.do("sync", lambda e, rw=rw: e.dma_start(out=self.rw_f[:, :, :], in_=rw), inc=self.c_ld[0], inc_k=16)
        P.do("gpsimd", lambda e, rw=rw: e.dma_start(out=self.rw_h[:, :, :], in_=rw), wait=W, inc=self.c_gd[0], inc_k=16)
        rn = [(self.c_ld[0], self.c_ld[0].n), (self.c_gd[0], self.c_gd[0].n)]
        P.do("vector", lambda e: e.tensor_tensor(self.rw_l[:, :, :], self.rw_f[:, :, :], self.rw_h[:, :, :], op=ALU.subtract),
             wait=rn, inc=self.c_v)
        P.do("sync", lambda e, rb=io["router_b"]: e.dma_start(out=self.lg[0:1, :], in_=rb), inc=self.c_ld[0], inc_k=16)
        bn = (self.c_ld[0], self.c_ld[0].n)
        P.do("tensor", lambda e: e.matmul(self.psm[0][0:128, 0:16], self.ones1r[0:1, :], self.lg[0:1, :], start=True, stop=True),
             wait=[bn, attn_all, (self.c_v, self.c_v.n)], inc=self.c_t)
        tn = (self.c_t, self.c_t.n)
        P.do("vector", lambda e: e.tensor_copy(self.rb_b[:, :], self.psm[0][0:128, 0:16]), wait=[tn], inc=self.c_v)
        P.do("tensor", lambda e: e.matmul(self.psm[1][0:128, 0:1], self.ones1r[0:1, :], self.e0t[0:1, :], start=True, stop=True),
             wait=[e0n], inc=self.c_t)
        t2 = (self.c_t, self.c_t.n)
        P.do("vector", lambda e: e.tensor_copy(self.e0col[:, :], self.psm[1][0:128, 0:1]), wait=[t2], inc=self.c_v)
        self.consts_done = [(self.c_v, self.c_v.n), (self.c_g, self.c_g.n)]

    def emit_ag0_ln_router(self, io, attn_all):
        P, DM, NA = self.P, self.DM, self.NA
        groups = [[2 * i, 2 * i + 1] for i in range(self.NCORE // 2)]
        TBR = 2 * self.TC
        NCHK = 8
        rpc = TBR // NCHK
        tpc = rpc // 128
        ar_base = self.c_cc.n
        order = []
        for i in range(NCHK // 2):
            order += [i, NCHK // 2 + i]
        for ch in order:
            P.do("gpsimd", lambda e, ch=ch, i=io["part"], o=io["partsum"]: e.collective_compute(
                "AllReduce", mybir.AluOpType.add, replica_groups=groups,
                ins=[i[ch * rpc:(ch + 1) * rpc, :]], outs=[o[ch * rpc:(ch + 1) * rpc, :]]),
                wait=[attn_all], inc=self.c_cc)
        ar_tile = lambda a: (self.c_cc, ar_base + min(2 * (a // tpc) + 2, NCHK))
        P.do("sync", lambda e, h=io["hoff"]: e.dma_start(out=self.hofft[:, :], in_=h),
             wait=[attn_all], inc=self.c_ld[0], inc_k=16)
        hn = (self.c_ld[0], self.c_ld[0].n)
        regs = {}
        P.do("sync", lambda e: regs.__setitem__('h', e.value_load(self.hofft[0:1, 0:1])) or None,
             wait=[hn], inc=None)
        xsl = io["x_own"].rearrange("(a p) d -> p a d", p=128)
        p1 = io["partsum"].rearrange("(a p) d -> p a d", p=128)
        xlnf = io["xln_f"].rearrange("(a p) d -> p a d", p=128)
        xlh = io["xln_h"].rearrange("(a p) d -> p a d", p=128)
        xll = io["xln_l"].rearrange("(a p) d -> p a d", p=128)
        cd = self.consts_done
        self.ln_tile_marks = []
        sq_hist = {}
        for a in range(NA):
            par = a % 3
            xpar = a % 2
            spar = a % 2
            wld = [attn_all] if a < 2 else [sq_hist[a - 2]]
            P.do("sync", lambda e, a=a, xpar=xpar, xsl=xsl: e.dma_start(out=self.t_x[:, xpar, :], in_=xsl[:, a, :]),
                 wait=wld + [ar_tile(a)], inc=self.c_ld[spar], inc_k=16)
            P.do("sync", lambda e, a=a, xpar=xpar, p1=p1: e.dma_start(
                out=self.t_p1[:, xpar, :], in_=p1[:, bass.ds(regs['h'] + a, 1), :].rearrange("p a d -> p (a d)")),
                inc=self.c_ld[spar], inc_k=16)
            ldn = [(self.c_ld[spar], self.c_ld[spar].n)]
            slotw = [] if a < 3 else [self.ln_tile_marks[a - 3]]
            P.do("vector", lambda e, par=par, xpar=xpar: e.tensor_tensor(self.t_ln[:, par, :], self.t_x[:, xpar, :], self.t_p1[:, xpar, :], op=ALU.add),
                 wait=ldn + cd + slotw + [(self.c_v, self.c_v.n)], inc=self.c_v)
            sn = (self.c_v, self.c_v.n)
            P.do("vector", lambda e, par=par: e.reduce_sum(self.stat[:, 0:1], self.t_ln[:, par, :], axis=AX.X),
                 wait=[sn], inc=self.c_v)
            P.do("scalar", lambda e, par=par, xpar=xpar: e.activation(self.t_x[:, xpar, :], self.t_ln[:, par, :], AF.Square, accum_out=self.stat[:, 1:2]),
                 wait=[sn, (self.c_s, self.c_s.n)], inc=self.c_s)
            sqn = (self.c_s, self.c_s.n)
            sq_hist[a] = sqn
            P.do("vector", lambda e: e.tensor_scalar(self.stat[:, 2:3], self.stat[:, 0:1], self.rinv[:, :], None, op0=ALU.mult),
                 wait=[(self.c_v, self.c_v.n)], inc=self.c_v)
            P.do("vector", lambda e: e.tensor_scalar(self.stat[:, 3:4], self.stat[:, 1:2], self.rinv[:, :], None, op0=ALU.mult),
                 wait=[sqn, (self.c_v, self.c_v.n)], inc=self.c_v)
            P.do("vector", lambda e: e.tensor_tensor(self.stat[:, 4:5], self.stat[:, 2:3], self.stat[:, 2:3], op=ALU.mult),
                 wait=[(self.c_v, self.c_v.n)], inc=self.c_v)
            P.do("vector", lambda e: e.tensor_tensor(self.stat[:, 4:5], self.stat[:, 3:4], self.stat[:, 4:5], op=ALU.subtract),
                 wait=[(self.c_v, self.c_v.n)], inc=self.c_v)
            vn = (self.c_v, self.c_v.n)
            P.do("scalar", lambda e: e.activation(self.stat[:, 5:6], self.stat[:, 4:5], AF.Ln, bias=self.eps[:, :]),
                 wait=[vn, (self.c_s, self.c_s.n)], inc=self.c_s)
            P.do("scalar", lambda e: e.activation(self.stat[:, 5:6], self.stat[:, 5:6], AF.Exp, scale=self.cm05[:, :]),
                 wait=[(self.c_s, self.c_s.n)], inc=self.c_s)
            rsn = (self.c_s, self.c_s.n)
            P.do("vector", lambda e, par=par: e.tensor_scalar(self.t_ln[:, par, :], self.t_ln[:, par, :], self.stat[:, 2:3], self.stat[:, 5:6], op0=ALU.subtract, op1=ALU.mult),
                 wait=[rsn, (self.c_v, self.c_v.n)], inc=self.c_v)
            lnn = (self.c_v, self.c_v.n)
            P.do("vector", lambda e, par=par: e.tensor_copy(self.t_lh[:, par, :], self.t_ln[:, par, :]),
                 wait=[lnn], inc=self.c_v)
            P.do("vector", lambda e, par=par: e.tensor_tensor(self.t_ll[:, par, :], self.t_ln[:, par, :], self.t_lh[:, par, :], op=ALU.subtract),
                 wait=[(self.c_v, self.c_v.n)], inc=self.c_v)
            hln = (self.c_v, self.c_v.n)
            P.do("gpsimd", lambda e, a=a, par=par, xlnf=xlnf: e.dma_start(out=xlnf[:, a, :], in_=self.t_ln[:, par, :]),
                 wait=[hln], inc=self.c_st[spar], inc_k=16)
            P.do("gpsimd", lambda e, a=a, par=par, xlh=xlh: e.dma_start(out=xlh[:, a, :], in_=self.t_lh[:, par, :]),
                 inc=self.c_st[spar], inc_k=16)
            P.do("gpsimd", lambda e, a=a, par=par, xll=xll: e.dma_start(out=xll[:, a, :], in_=self.t_ll[:, par, :]),
                 inc=self.c_st[spar], inc_k=16)
            self.ln_tile_marks.append((self.c_st[spar], self.c_st[spar].n))
            for dk in range(self.NDK):
                P.do("tensor", lambda e, dk=dk, par=par: e.transpose(
                    self.psm[6][:, 0:128], self.t_ln[:, par, 128 * dk:128 * (dk + 1)], self.identm[:, :]),
                    wait=[lnn, (self.c_v, self.c_v.n), (self.c_mm, self.c_mm.n)], inc=self.c_t)
                ttn = (self.c_t, self.c_t.n)
                P.do("vector", lambda e, dk=dk: e.tensor_copy(self.xfm_f[:, dk, :], self.psm[6][:, 0:128]),
                     wait=[ttn], inc=self.c_v)
            fmn = (self.c_v, self.c_v.n)
            for dk in range(self.NDK):
                P.do("tensor", lambda e, dk=dk: e.matmul(
                    self.psm[7][:, 0:16], self.xfm_f[:, dk, :], self.rw_f[:, dk, :],
                    start=(dk == 0), stop=(dk == self.NDK - 1)),
                    wait=([fmn] if dk == 0 else []), inc=(self.c_mm if dk == self.NDK - 1 else None))
            mmn = (self.c_mm, self.c_mm.n)
            P.do("vector", lambda e: e.tensor_tensor(self.lg[:, :], self.psm[7][:, 0:16], self.rb_b[:, :], op=ALU.add),
                 wait=[mmn, (self.c_v, self.c_v.n)], inc=self.c_v)
            lgn = (self.c_v, self.c_v.n)
            P.do("vector", lambda e: e.reduce_max(self.stat[:, 6:7], self.lg[:, :], axis=AX.X),
                 wait=[lgn], inc=self.c_v)
            P.do("vector", lambda e: e.tensor_scalar(self.stat[:, 7:8], self.stat[:, 6:7], -1.0, None, op0=ALU.mult),
                 wait=[(self.c_v, self.c_v.n)], inc=self.c_v)
            mxn = (self.c_v, self.c_v.n)
            P.do("scalar", lambda e: e.activation(self.prob[:, :], self.lg[:, :], AF.Exp, bias=self.stat[:, 7:8], accum_out=self.stat[:, 6:7]),
                 wait=[mxn, (self.c_s, self.c_s.n)], inc=self.c_s)
            expn = (self.c_s, self.c_s.n)
            self.srecip(self.stat[:, 7:8], self.stat[:, 6:7], self.c_s, wait=[expn])
            rsn2 = (self.c_s, self.c_s.n)
            P.do("vector", lambda e: e.max(self.m8[:, :], self.prob[:, :]), wait=[expn, (self.c_v, self.c_v.n)], inc=self.c_v)
            P.do("vector", lambda e: e.drain(), wait=[], inc=self.c_v)
            P.do("vector", lambda e: e.max_index(self.i8[:, :], self.m8[:, :], self.prob[:, :]),
                 wait=[(self.c_v, self.c_v.n)], inc=self.c_v)
            tpn = (self.c_v, self.c_v.n)
            P.do("vector", lambda e, a=a: e.tensor_scalar(self.valn[:, a, :], self.m8[:, 0:2], self.stat[:, 7:8], None, op0=ALU.mult),
                 wait=[tpn, rsn2], inc=self.c_v)
            P.do("vector", lambda e, a=a: e.tensor_tensor(self.stat[:, 6:7], self.valn[:, a, 0:1], self.valn[:, a, 1:2], op=ALU.add),
                 wait=[(self.c_v, self.c_v.n)], inc=self.c_v)
            P.do("vector", lambda e: e.tensor_tensor(self.stat[:, 6:7], self.stat[:, 6:7], self.c1e9[:, :], op=ALU.add),
                 wait=[(self.c_v, self.c_v.n)], inc=self.c_v)
            dnn = (self.c_v, self.c_v.n)
            self.srecip(self.stat[:, 6:7], self.stat[:, 6:7], self.c_s, wait=[dnn, (self.c_s, self.c_s.n)])
            rdn = (self.c_s, self.c_s.n)
            P.do("vector", lambda e, a=a: e.tensor_scalar(self.valn[:, a, :], self.valn[:, a, :], self.stat[:, 6:7], None, op0=ALU.mult),
                 wait=[rdn, (self.c_v, self.c_v.n)], inc=self.c_v)
            P.do("vector", lambda e, a=a: e.tensor_copy(self.idxf[:, a, :], self.i8[:, 0:2]),
                 wait=[(self.c_v, self.c_v.n)], inc=self.c_v)
            for k in range(2):
                P.do("vector", lambda e, a=a, k=k: e.tensor_scalar(
                    self.ohk[:, k, :], self.iota16[:, :], self.idxf[:, a, k:k + 1], None, op0=ALU.is_equal),
                    wait=[(self.c_v, self.c_v.n)], inc=self.c_v)
            P.do("vector", lambda e: e.tensor_tensor(self.ohsum_h[:, :], self.ohk[:, 0, :], self.ohk[:, 1, :], op=ALU.add),
                 wait=[(self.c_v, self.c_v.n)], inc=self.c_v)
            ohn = (self.c_v, self.c_v.n)
            P.do("tensor", lambda e: e.matmul(self.psm[7][:, 0:16], self.ltri[:, :], self.ohsum_h[:, :], start=True, stop=True),
                 wait=[ohn, (self.c_mm, self.c_mm.n)], inc=self.c_mm)
            P.do("tensor", lambda e: e.matmul(self.psm[6][0:1, 0:16], self.ones128[:, :], self.ohsum_h[:, :], start=True, stop=True),
                 inc=self.c_mm)
            exn = (self.c_mm, self.c_mm.n)
            P.do("tensor", lambda e: e.matmul(self.psm[5][:, 0:16], self.ones1r[0:1, :], self.runoff[0:1, :], start=True, stop=True),
                 wait=[(self.c_v, self.c_v.n)], inc=self.c_mm)
            bcn = (self.c_mm, self.c_mm.n)
            P.do("vector", lambda e: e.tensor_copy(self.runb[:, :], self.psm[5][:, 0:16]),
                 wait=[bcn, (self.c_v, self.c_v.n)], inc=self.c_v)
            P.do("vector", lambda e: e.tensor_tensor(self.posb[:, :], self.psm[7][:, 0:16], self.runb[:, :], op=ALU.add),
                 wait=[exn, (self.c_v, self.c_v.n)], inc=self.c_v)
            pbn = (self.c_v, self.c_v.n)
            for k in range(2):
                P.do("vector", lambda e, k=k: e.tensor_tensor(self.prob[:, :], self.ohk[:, k, :], self.posb[:, :], op=ALU.mult),
                     wait=[pbn, (self.c_v, self.c_v.n)], inc=self.c_v)
                P.do("vector", lambda e, a=a, k=k: e.reduce_sum(self.posl[:, a, k:k + 1], self.prob[:, :], axis=AX.X),
                     wait=[(self.c_v, self.c_v.n)], inc=self.c_v)
            P.do("vector", lambda e: e.tensor_tensor(self.runoff[0:1, :], self.runoff[0:1, :], self.psm[6][0:1, 0:16], op=ALU.add),
                 wait=[(self.c_v, self.c_v.n)], inc=self.c_v)
        self.ln_done = [(self.c_v, self.c_v.n), (self.c_s, self.c_s.n),
                        (self.c_st[0], self.c_st[0].n), (self.c_st[1], self.c_st[1].n)]

    def emit_gates_records(self, io):
        P, NA, CAP, E = self.P, self.NA, self.CAP, self.E
        ln = self.ln_done
        P.do("gpsimd", lambda e, c=io["cnt_d"]: e.dma_start(out=c, in_=self.runoff[0:1, :]),
             wait=ln, inc=self.c_gd[0], inc_k=16)
        P.do("gpsimd", lambda e, i=io["cnt_d"], o=io["agc"]: e.collective_compute(
            "AllGather", mybir.AluOpType.bypass, replica_groups=[list(range(self.NCORE))], ins=[i], outs=[o]),
            wait=[(self.c_gd[0], self.c_gd[0].n)], inc=self.c_cc)
        agcn = (self.c_cc, self.c_cc.n)
        cnt_ap = io["agc"].rearrange("(r f) -> r f", r=self.NCORE)
        P.do("sync", lambda e, cnt_ap=cnt_ap: e.dma_start(out=self.allcnt[0:self.NCORE, :], in_=cnt_ap),
             wait=[agcn], inc=self.c_ld[0], inc_k=16)
        cn = (self.c_ld[0], self.c_ld[0].n)
        P.do("tensor", lambda e: e.matmul(self.psm[4][0:1, 0:16], self.msk[0:self.NCORE, :], self.allcnt[0:self.NCORE, :], start=True, stop=True),
             wait=[cn, (self.c_mm, self.c_mm.n)], inc=self.c_mm)
        on = (self.c_mm, self.c_mm.n)
        P.do("vector", lambda e: e.tensor_copy(self.off16[0:1, :], self.psm[4][0:1, 0:16]),
             wait=[on, (self.c_v, self.c_v.n)], inc=self.c_v)
        o2 = (self.c_v, self.c_v.n)
        P.do("tensor", lambda e: e.matmul(self.psm[4][:, 0:16], self.ones1r[0:1, :], self.off16[0:1, :], start=True, stop=True),
             wait=[o2], inc=self.c_mm)
        ob = (self.c_mm, self.c_mm.n)
        P.do("vector", lambda e: e.tensor_copy(self.offb[:, :], self.psm[4][:, 0:16]),
             wait=[ob, (self.c_v, self.c_v.n)], inc=self.c_v)
        obn = (self.c_v, self.c_v.n)
        P.do("gpsimd", lambda e, i=io["xln_h"], o=io["ag1h"]: e.collective_compute(
            "AllGather", mybir.AluOpType.bypass, replica_groups=[list(range(self.NCORE))], ins=[i], outs=[o]),
            wait=ln, inc=self.c_cc)
        P.do("gpsimd", lambda e, i=io["xln_l"], o=io["ag1l"]: e.collective_compute(
            "AllGather", mybir.AluOpType.bypass, replica_groups=[list(range(self.NCORE))], ins=[i], outs=[o]),
            inc=self.c_cc)
        for a in range(NA):
            for k in range(2):
                P.do("vector", lambda e, a=a, k=k: e.tensor_scalar(
                    self.ohk[:, k, :], self.iota16[:, :], self.idxf[:, a, k:k + 1], None, op0=ALU.is_equal),
                    wait=[obn, (self.c_v, self.c_v.n)], inc=self.c_v)
                P.do("vector", lambda e, k=k: e.tensor_tensor(self.prob[:, :], self.ohk[:, k, :], self.offb[:, :], op=ALU.mult),
                     wait=[(self.c_v, self.c_v.n)], inc=self.c_v)
                P.do("vector", lambda e: e.reduce_sum(self.stat[:, 0:1], self.prob[:, :], axis=AX.X),
                     wait=[(self.c_v, self.c_v.n)], inc=self.c_v)
                P.do("vector", lambda e, a=a, k=k: e.tensor_tensor(self.posl[:, a, k:k + 1], self.posl[:, a, k:k + 1], self.stat[:, 0:1], op=ALU.add),
                     wait=[(self.c_v, self.c_v.n)], inc=self.c_v)
        GCH, R = self.GCH, self.NCORE
        NCHE = CAP // GCH
        ZROW = float(R * 2 * CAP)
        fl = lambda t: t.ap().rearrange("p a k -> p (a k)")
        idf, psf, vlf = fl(self.idxf), fl(self.posl), fl(self.valn)
        gsf, cgf = fl(self.gsc), fl(self.cgi)
        b0, b1, b2 = self.b32[:, 0, :], self.b32[:, 1, :], self.b32[:, 2, :]
        i0, i1 = self.bi32[:, 0, :], self.bi32[:, 1, :]
        shG = GCH.bit_length() - 1
        D = lambda fn, w=None: P.do("vector", fn, wait=([(self.c_v, self.c_v.n)] if w is None else w), inc=self.c_v)
        D(lambda e: e.tensor_scalar(b0, psf, float(CAP), None, op0=ALU.is_lt))
        D(lambda e: e.tensor_tensor(gsf, vlf, b0, op=ALU.mult))
        D(lambda e: e.tensor_copy(i0, idf))
        D(lambda e: e.tensor_scalar(i1, i0, 1, None, op0=ALU.bitwise_and))
        D(lambda e: e.tensor_copy(b1, i1))
        D(lambda e: e.scalar_tensor_tensor(b1, b1, float(NCHE * R * GCH), psf, op0=ALU.mult, op1=ALU.add))
        D(lambda e: e.tensor_scalar(i1, i0, 1, None, op0=ALU.arith_shift_right))
        D(lambda e: e.tensor_copy(b2, i1))
        D(lambda e: e.scalar_tensor_tensor(b2, b2, float(GCH), b1, op0=ALU.mult, op1=ALU.add))
        D(lambda e: e.tensor_copy(i1, psf))
        D(lambda e: e.tensor_scalar(i1, i1, shG, None, op0=ALU.arith_shift_right))
        D(lambda e: e.tensor_copy(b1, i1))
        D(lambda e: e.scalar_tensor_tensor(b1, b1, float((R - 1) * GCH), b2, op0=ALU.mult, op1=ALU.add))
        D(lambda e: e.tensor_tensor(b2, b1, b0, op=ALU.mult))
        D(lambda e: e.tensor_scalar(b1, b0, 1.0, -ZROW, op0=ALU.subtract, op1=ALU.mult))
        D(lambda e: e.tensor_tensor(b2, b2, b1, op=ALU.add))
        D(lambda e: e.tensor_copy(cgf, b2))
        D(lambda e: e.scalar_tensor_tensor(b1, idf, 32768.0, psf, op0=ALU.mult, op1=ALU.add))
        D(lambda e: e.tensor_tensor(b1, b1, b0, op=ALU.mult))
        D(lambda e: e.tensor_scalar(b2, b0, 1.0, -(31.0 * 32768.0), op0=ALU.subtract, op1=ALU.mult))
        D(lambda e: e.tensor_tensor(self.recw[:, :], b1, b2, op=ALU.add))
        self.gates_done = (self.c_v, self.c_v.n)
        recv = io["rec_d"].rearrange("(p j) -> p j", p=128)
        P.do("gpsimd", lambda e, recv=recv: e.dma_start(out=recv, in_=self.recw[:, :]),
             wait=[self.gates_done], inc=self.c_gd[0], inc_k=16)
        pn = (self.c_gd[0], self.c_gd[0].n)
        P.do("gpsimd", lambda e, i=io["rec_d"], o=io["ag1r"]: e.collective_compute(
            "AllGather", mybir.AluOpType.bypass, replica_groups=[list(range(self.NCORE))], ins=[i], outs=[o]),
            wait=[pn], inc=self.c_cc)
        self.ag1_done = (self.c_cc, self.c_cc.n)

    def emit_table(self, io):
        P, CAP, E, NR = self.P, self.CAP, self.E, self.NCORE
        NB = self.NB
        rv = io["ag1r"].rearrange("(r p j) -> p r j", p=128, j=self.NREC)
        P.do("sync", lambda e, rv=rv: e.dma_start(
            out=self.recs.ap().rearrange("p (r j) -> p r j", j=self.NREC), in_=rv),
            wait=[self.ag1_done], inc=self.c_ld[1], inc_k=16)
        rn = (self.c_ld[1], self.c_ld[1].n)
        RJ = self.NCH
        V = lambda n: (self.c_v, self.c_v.n)
        P.do("vector", lambda e, RJ=RJ: e.tensor_copy(self.evi[:, 0:RJ], self.recs[:, 0:RJ]),
             wait=[rn, self.gates_done, (self.c_v, self.c_v.n)], inc=self.c_v)
        P.do("vector", lambda e, RJ=RJ: e.tensor_scalar(self.evt[:, 0:RJ], self.evi[:, 0:RJ], 15, None, op0=ALU.arith_shift_right),
             wait=[V(0)], inc=self.c_v)
        P.do("vector", lambda e, RJ=RJ: e.tensor_copy(self.ef[:, 0:RJ], self.evt[:, 0:RJ]),
             wait=[V(0)], inc=self.c_v)
        P.do("vector", lambda e, RJ=RJ: e.tensor_scalar(self.evt[:, 0:RJ], self.evi[:, 0:RJ], 32767, None, op0=ALU.bitwise_and),
             wait=[V(0)], inc=self.c_v)
        P.do("vector", lambda e, RJ=RJ: e.tensor_copy(self.pf[:, 0:RJ], self.evt[:, 0:RJ]),
             wait=[V(0)], inc=self.c_v)
        P.do("vector", lambda e, RJ=RJ: e.tensor_scalar(self.ef[:, 0:RJ], self.ef[:, 0:RJ], self.e0col[:, :], None, op0=ALU.subtract),
             wait=[V(0)], inc=self.c_v)
        P.do("vector", lambda e, RJ=RJ: e.tensor_scalar(self.colf[:, 0:RJ], self.ef[:, 0:RJ], 0.0, None, op0=ALU.is_ge),
             wait=[V(0)], inc=self.c_v)
        P.do("vector", lambda e, RJ=RJ: e.tensor_scalar(self.rrow[:, 0:RJ], self.ef[:, 0:RJ], 1.0, None, op0=ALU.is_le),
             wait=[V(0)], inc=self.c_v)
        P.do("vector", lambda e, RJ=RJ: e.tensor_tensor(self.colf[:, 0:RJ], self.colf[:, 0:RJ], self.rrow[:, 0:RJ], op=ALU.mult),
             wait=[V(0)], inc=self.c_v)
        inrn = (self.c_v, self.c_v.n)
        P.do("vector", lambda e, RJ=RJ: e.tensor_scalar(self.evt[:, 0:RJ], self.evt[:, 0:RJ], 7, None, op0=ALU.arith_shift_right),
             wait=[V(0)], inc=self.c_v)
        P.do("vector", lambda e, RJ=RJ: e.tensor_copy(self.rrow[:, 0:RJ], self.evt[:, 0:RJ]),
             wait=[V(0)], inc=self.c_v)
        P.do("vector", lambda e, RJ=RJ, NB=NB: e.scalar_tensor_tensor(
            self.ef[:, 0:RJ], self.ef[:, 0:RJ], float(NB), self.rrow[:, 0:RJ], op0=ALU.mult, op1=ALU.add),
            wait=[V(0)], inc=self.c_v)
        P.do("vector", lambda e, RJ=RJ: e.tensor_tensor(self.ef[:, 0:RJ], self.ef[:, 0:RJ], self.colf[:, 0:RJ], op=ALU.mult),
             wait=[V(0)], inc=self.c_v)
        P.do("vector", lambda e, RJ=RJ, NB=NB: e.tensor_scalar(self.colf[:, 0:RJ], self.colf[:, 0:RJ], 1.0, -2.0 * NB, op0=ALU.subtract, op1=ALU.mult),
             wait=[V(0)], inc=self.c_v)
        P.do("vector", lambda e, RJ=RJ: e.tensor_tensor(self.colf[:, 0:RJ], self.colf[:, 0:RJ], self.ef[:, 0:RJ], op=ALU.add),
             wait=[V(0)], inc=self.c_v)
        P.do("vector", lambda e, RJ=RJ: e.tensor_copy(self.evi[:, 0:RJ], self.pf[:, 0:RJ]),
             wait=[V(0)], inc=self.c_v)
        P.do("vector", lambda e, RJ=RJ: e.tensor_scalar(self.evi[:, 0:RJ], self.evi[:, 0:RJ], 127, None, op0=ALU.bitwise_and),
             wait=[V(0)], inc=self.c_v)
        P.do("vector", lambda e, RJ=RJ: e.tensor_copy(self.rrow[:, 0:RJ], self.evi[:, 0:RJ]),
             wait=[V(0)], inc=self.c_v)
        decn = (self.c_v, self.c_v.n)
        mm_after = {}
        for j in range(RJ):
            jp = j % 2
            w = [decn, (self.c_g, self.c_g.n), (self.c_ev, self.c_ev.n)] if j == 0 else []
            if j >= 2:
                w = [(self.c_mm, mm_after[j - 2])]
            P.do("vector", lambda e, j=j, jp=jp: e.tensor_scalar(self.oer[:, jp, :], self.io128r[:, :], self.rrow[:, j:j + 1], None, op0=ALU.is_equal),
                 wait=w + [(self.c_v, self.c_v.n)], inc=self.c_v)
            P.do("vector", lambda e, j=j, jp=jp: e.tensor_scalar(self.oeb[:, jp, :], self.iota41[:, :], self.colf[:, j:j + 1], None, op0=ALU.is_equal),
                 wait=[(self.c_v, self.c_v.n)], inc=self.c_v)
            P.do("vector", lambda e, j=j, jp=jp: e.tensor_scalar(self.oeb[:, jp, :], self.oeb[:, jp, :], self.tidt[:, j:j + 1], None, op0=ALU.mult),
                 wait=[(self.c_v, self.c_v.n)], inc=self.c_v)
            vn = (self.c_v, self.c_v.n)
            P.do("tensor", lambda e, j=j, jp=jp, RJ=RJ: e.matmul(self.psm[7][:, 0:self.NCOL], self.oer[:, jp, :], self.oeb[:, jp, :],
                                                                start=(j == 0), stop=(j == RJ - 1)),
                 wait=[vn], inc=self.c_mm)
            mm_after[j] = self.c_mm.n
        tn = (self.c_mm, self.c_mm.n)
        P.do("vector", lambda e: e.tensor_copy(self.tbl[:, :], self.psm[7][:, 0:self.NCOL]),
             wait=[tn, (self.c_v, self.c_v.n)], inc=self.c_v)
        P.do("vector", lambda e: e.tensor_copy(self.tbli[:, :], self.tbl[:, :]),
             wait=[(self.c_v, self.c_v.n)], inc=self.c_v)
        self.table_done = (self.c_v, self.c_v.n)

    def srecip(self, out, in_, ctrS, wait=(), nP=128):
        P = self.P
        P.do("scalar", lambda e: e.activation(out, in_, AF.Ln), wait=list(wait), inc=ctrS)
        P.do("scalar", lambda e, nP=nP: e.activation(out, out, AF.Exp, scale=self.mcm1[0:nP, :]),
             wait=[(ctrS, ctrS.n)], inc=ctrS)

    def dsplit_one(self, srcf, hid, lod, wait_root, fast=False):
        P = self.P
        R, Wd = srcf.shape
        sv = srcf.rearrange("(c p) w -> p c w", p=128)
        hv = hid.rearrange("(c p) w -> p c w", p=128)
        lv = lod.rearrange("(c p) w -> p c w", p=128)
        seng, stc = ("sync", self.c_ws) if fast else ("gpsimd", self.c_st)
        for ch in range(R // 128):
            for w0 in range(0, Wd, self.FF):
                ww = min(self.FF, Wd - w0)
                k = self._ws_k
                self._ws_k += 1
                fpar = k % 2
                par = k % 2
                wld = [wait_root] if k - 2 not in self._ws_vhist else [self._ws_vhist[k - 2]]
                P.do("sync", lambda e, sv=sv, ch=ch, fpar=fpar, w0=w0, ww=ww: e.dma_start(
                    out=self.wsplit_f[:, fpar, 0:ww], in_=sv[:, ch, w0:w0 + ww]),
                    wait=wld, inc=self.c_ld[fpar], inc_k=16)
                ldn = (self.c_ld[fpar], self.c_ld[fpar].n)
                P.do("vector", lambda e, par=par, fpar=fpar, ww=ww: e.tensor_copy(self.wsplit_h[:, par, 0:ww], self.wsplit_f[:, fpar, 0:ww]),
                     wait=[ldn, (self.c_v, self.c_v.n)] + ([self._ws_hist[k - 2]] if k - 2 in self._ws_hist else []), inc=self.c_v)
                P.do("vector", lambda e, par=par, fpar=fpar, ww=ww: e.tensor_tensor(self.wsplit_l[:, par, 0:ww], self.wsplit_f[:, fpar, 0:ww], self.wsplit_h[:, par, 0:ww], op=ALU.subtract),
                     wait=[(self.c_v, self.c_v.n)], inc=self.c_v)
                vn = (self.c_v, self.c_v.n)
                self._ws_vhist[k] = vn
                P.do(seng, lambda e, hv=hv, ch=ch, par=par, w0=w0, ww=ww: e.dma_start(out=hv[:, ch, w0:w0 + ww], in_=self.wsplit_h[:, par, 0:ww]),
                     wait=[vn], inc=stc[par % 2], inc_k=16)
                P.do(seng, lambda e, lv=lv, ch=ch, par=par, w0=w0, ww=ww: e.dma_start(out=lv[:, ch, w0:w0 + ww], in_=self.wsplit_l[:, par, 0:ww]),
                     inc=stc[par % 2], inc_k=16)
                self._ws_hist[k] = (stc[par % 2], stc[par % 2].n)

    def emit_wsplits(self, wsplits, wait_root, fast=False):
        stc = self.c_ws if fast else self.c_st
        self._ws_k = getattr(self, "_ws_k", 0)
        self._ws_hist = getattr(self, "_ws_hist", {})
        self._ws_vhist = getattr(self, "_ws_vhist", {})
        for (srcf, hid, lod) in wsplits:
            self.dsplit_one(srcf, hid, lod, wait_root, fast)
        return [(stc[0], stc[0].n), (stc[1], stc[1].n)]

    def emit_mlp(self, tag, nchunks, load_x, wgu_h, wgu_l, wd_hd, wd_ld, sink, extra_root, wsdone):
        P, DM, FF, GCH = self.P, self.DM, self.FF, self.GCH
        NDK, NFB = self.NDK, self.NFB
        guv = wgu_h.rearrange("(c p) w -> p c w", p=128)
        gulv = wgu_l.rearrange("(c p) w -> p c w", p=128)
        dhv = wd_hd.rearrange("(c p) d -> p c d", p=128)
        dlv = wd_ld.rearrange("(c p) d -> p c d", p=128)
        prev_gu_done = getattr(self, "_mlp_gu_done", 0)
        prev_eo_done = getattr(self, "_mlp_eo_done", [(self.c_st[0], 0), (self.c_st[1], 0)])
        self._eo_prev_n = getattr(self, "_eo_prev_n", 0)
        self._gu_hist = getattr(self, "_gu_hist", {})
        self._pidx = getattr(self, "_pidx", 0)
        for ci in range(nchunks):
            px = self._pidx % 2
            xw = load_x(ci, [(self.c_mm, self._gu_hist.get(self._pidx - 2, 0))] + list(extra_root), px)
            split_n = {}
            silu_n = {}
            for fb in range(NFB):
                wpar = fb % 2
                wld = (list(wsdone) + [(self.c_mm, prev_gu_done)]) if (ci == 0 and fb < 2) else [(self.c_mm, self._gu_mm.get((tag, ci, fb - 2), prev_gu_done))]
                P.do("sync", lambda e, fb=fb, wpar=wpar: e.dma_start(out=self.gw_h[:, :, wpar, :], in_=guv[:, :, 256 * fb:256 * (fb + 1)]),
                     wait=wld, inc=self.c_ld[wpar], inc_k=16)
                P.do("sync", lambda e, fb=fb, wpar=wpar: e.dma_start(out=self.gw_l[:, :, wpar, :], in_=gulv[:, :, 256 * fb:256 * (fb + 1)]),
                     wait=wld, inc=self.c_ld[wpar], inc_k=16)
                wn = [(self.c_ld[wpar], self.c_ld[wpar].n)]
                psg, psu = self.psm[2 * wpar], self.psm[2 * wpar + 1]
                pw = []
                if (tag, ci, fb - 2) in silu_n:
                    pw = silu_n[(tag, ci, fb - 2)]
                elif fb < 2:
                    pw = list(getattr(self, "_mlp_last_silu", []))
                    pw.append((self.c_ev, self._eo_prev_n))
                for dk in range(NDK):
                    for pi in range(3):
                        wt = self.gw_h if pi < 2 else self.gw_l
                        xt = self.xfm_h if pi != 1 else self.xfm_l
                        first = (dk == 0 and pi == 0)
                        P.do("tensor", lambda e, wt=wt, xt=xt, dk=dk, pi=pi, wpar=wpar, psg=psg, px=px: e.matmul(
                            psg[:, 0:self.GCH], wt[:, dk, wpar, 0:128], xt[:, px, dk, :],
                            start=(dk == 0 and pi == 0), stop=(dk == NDK - 1 and pi == 2)),
                            wait=(wn + xw + pw if first else []), inc=None)
                        P.do("tensor", lambda e, wt=wt, xt=xt, dk=dk, pi=pi, wpar=wpar, psu=psu, px=px: e.matmul(
                            psu[:, 0:self.GCH], wt[:, dk, wpar, 128:256], xt[:, px, dk, :],
                            start=(dk == 0 and pi == 0), stop=(dk == NDK - 1 and pi == 2)),
                            wait=[], inc=(self.c_mm if (dk == NDK - 1 and pi == 2) else None))
                self._gu_mm[(tag, ci, fb)] = self.c_mm.n
                gun = (self.c_mm, self.c_mm.n)
                wsl = [gun, (self.c_s, self.c_s.n)]
                if (tag, ci, fb - 1) in silu_n:
                    wsl.append(silu_n[(tag, ci, fb - 1)][1])
                if fb == 0:
                    wsl.append((self.c_mm, getattr(self, "_down_prev", 0)))
                    wsl += list(getattr(self, "_mlp_last_silu", []))
                P.do("scalar", lambda e, psg=psg: e.activation(self.sg[:, :], psg[:, 0:self.GCH], AF.Silu),
                     wait=wsl, inc=self.c_s)
                sgn = (self.c_s, self.c_s.n)
                P.do("vector", lambda e, psu=psu: e.tensor_tensor(self.hf[:, :], self.sg[:, :], psu[:, 0:self.GCH], op=ALU.mult),
                     wait=[sgn, (self.c_v, self.c_v.n)], inc=self.c_v)
                P.do("vector", lambda e, fb=fb: e.tensor_copy(self.hh[:, fb, :], self.hf[:, :]),
                     wait=[(self.c_v, self.c_v.n)], inc=self.c_v)
                P.do("vector", lambda e, fb=fb: e.tensor_tensor(self.hl[:, fb, :], self.hf[:, :], self.hh[:, fb, :], op=ALU.subtract),
                     wait=[(self.c_v, self.c_v.n)], inc=self.c_v)
                split_n[fb] = (self.c_v, self.c_v.n)
                silu_n[(tag, ci, fb)] = [sgn, (self.c_v, self.c_v.n)]
            self._mlp_last_silu = silu_n[(tag, ci, NFB - 2)] + silu_n[(tag, ci, NFB - 1)]
            gu_done_ci = self.c_mm.n
            halves = [(hi, h0, min(512, DM - h0)) for hi, h0 in enumerate(range(0, DM, 512))]
            nh = len(halves)
            for fb in range(NFB):
                dpar = fb % 2
                wldd = [split_n[NFB - 1], (self.c_mm, getattr(self, "_down_prev", 0))] if fb < 2 else [(self.c_mm, self._dw_mm.get((tag, ci, fb - 2), 0))]
                P.do("sync", lambda e, fb=fb, dpar=dpar: e.dma_start(out=self.dw_h[:, dpar, :], in_=dhv[:, fb, :]),
                     wait=wldd, inc=self.c_ld[dpar], inc_k=16)
                P.do("sync", lambda e, fb=fb, dpar=dpar: e.dma_start(out=self.dw_l[:, dpar, :], in_=dlv[:, fb, :]),
                     wait=wldd, inc=self.c_ld[dpar], inc_k=16)
                dwn = [(self.c_ld[dpar], self.c_ld[dpar].n)]
                for tc in range(GCH // 128):
                    for (hi, h0, w512) in halves:
                        pso = self.psm[nh * tc + hi]
                        for pi in range(3):
                            ht = self.hh if pi != 1 else self.hl
                            wt = self.dw_h if pi < 2 else self.dw_l
                            first = (pi == 0 and tc == 0 and hi == 0)
                            w = []
                            if first:
                                w = dwn + [split_n[fb]]
                                if fb == 0:
                                    w.append((self.c_ev, self._eo_ev.get((tag, ci - 1), self._eo_prev_n)))
                            P.do("tensor", lambda e, ht=ht, wt=wt, fb=fb, tc=tc, h0=h0, w512=w512, dpar=dpar, pso=pso, pi=pi: e.matmul(
                                pso[:, 0:w512], ht[:, fb, 128 * tc:128 * (tc + 1)], wt[:, dpar, h0:h0 + w512],
                                start=(fb == 0 and pi == 0), stop=(fb == NFB - 1 and pi == 2)),
                                wait=w,
                                inc=(self.c_mm if (pi == 2 and tc == GCH // 128 - 1 and hi == nh - 1) else None))
                self._dw_mm[(tag, ci, fb)] = self.c_mm.n
            dn = (self.c_mm, self.c_mm.n)
            self._down_prev = self.c_mm.n
            for tc in range(GCH // 128):
                for (hi, h0, w512) in halves:
                    pso = self.psm[nh * tc + hi]
                    P.do("vector", lambda e, tc=tc, h0=h0, w512=w512, pso=pso: e.tensor_copy(self.eo[:, tc, h0:h0 + w512], pso[:, 0:w512]),
                         wait=[dn, prev_eo_done, (self.c_v, self.c_v.n)], inc=self.c_ev)
            self._eo_ev[(tag, ci)] = self.c_ev.n
            self._eo_prev_n = self.c_ev.n
            evn = (self.c_ev, self.c_ev.n)
            prev_eo_done = sink(ci, [evn])
            prev_gu_done = gu_done_ci
            self._gu_hist[self._pidx] = gu_done_ci
            self._pidx += 1
        self._mlp_gu_done = prev_gu_done
        self._mlp_eo_done = prev_eo_done

    def emit_expert_x(self, io, ci, ex, waits, px):
        P, GCH, NB = self.P, self.GCH, self.CAP // 128
        from bass_rust import IndirectOffsetOnAxis
        w0 = waits + [self.table_done]
        hb = io["xg_hb"].rearrange("(a p) d -> p a d", p=128)
        lb = io["xg_lb"].rearrange("(a p) d -> p a d", p=128)
        nslot = max(GCH // 256, 1)
        for half in range(max(GCH // (128 * nslot), 1)):
            for gs in range(nslot):
                g4 = half * nslot + gs
                col = ex * NB + (GCH * ci) // 128 + g4
                for src, dst in ((io["ag1h"], self.xg_h), (io["ag1l"], self.xg_l)):
                    P.do("gpsimd", lambda e, src=src, dst=dst, gs=gs, col=col: e.indirect_dma_start(
                        dst[:, gs, :], None, src,
                        IndirectOffsetOnAxis(self.tbli[:, col:col + 1], 0)),
                        wait=(w0 if g4 == 0 else []) + [(self.c_st[0], self._xg_st)], inc=self.c_gd[0], inc_k=16)
            gn = (self.c_gd[0], self.c_gd[0].n)
            P.do("gpsimd", lambda e, hb=hb, half=half: e.dma_start(
                out=hb[:, half * nslot:(half + 1) * nslot, :], in_=self.xg_h[:, :, :]),
                wait=[gn, (self.c_fm[0], getattr(self, "_xg_fm", 0))], inc=self.c_st[0], inc_k=16)
            P.do("gpsimd", lambda e, lb=lb, half=half: e.dma_start(
                out=lb[:, half * nslot:(half + 1) * nslot, :], in_=self.xg_l[:, :, :]), inc=self.c_st[0], inc_k=16)
            self._xg_st = self.c_st[0].n
        bn = (self.c_st[0], self.c_st[0].n)
        for dk in range(self.NDK):
            for src, dst in ((io["xg_hb"], self.xfm_h), (io["xg_lb"], self.xfm_l)):
                P.do("sync", lambda e, src=src, dst=dst, dk=dk, px=px: e.dma_start(
                    out=dst[:, px, dk, :], in_=src[0:GCH, 128 * dk:128 * (dk + 1)], transpose=True),
                    wait=([bn] + waits if dk == 0 else []), inc=self.c_fm[0], inc_k=16)
        self._xg_fm = self.c_fm[0].n
        return [(self.c_fm[0], self.c_fm[0].n)]

    def emit_shared_x(self, io, ci, waits, px):
        P, GCH = self.P, self.GCH
        t0 = GCH * ci
        tpc = GCH // 128
        lnw = [self.ln_tile_marks[tpc * (ci + 1) - 1]]
        if tpc * (ci + 1) - 2 >= 0:
            lnw.append(self.ln_tile_marks[tpc * (ci + 1) - 2])
        for dk in range(self.NDK):
            for src, dst in ((io["xln_h_t"], self.xfm_h), (io["xln_l_t"], self.xfm_l)):
                P.do("sync", lambda e, src=src, dst=dst, dk=dk, t0=t0, px=px: e.dma_start(
                    out=dst[:, px, dk, :], in_=src[t0:t0 + GCH, 128 * dk:128 * (dk + 1)], transpose=True),
                    wait=(waits + lnw if dk == 0 else []), inc=self.c_fm[1], inc_k=16)
        return [(self.c_fm[1], self.c_fm[1].n)]

    def emit_moe(self, io, attn_all):
        P = self.P
        self._xg_st = self.c_st[0].n
        self.wsplit_ex_done = self.emit_wsplits(io["wsplits_ex"], attn_all)
        shov = io["sho_d"].rearrange("(a p) d -> p a d", p=128)
        def sink_sh(ci, wait):
            for tc in range(self.GCHB):
                P.do("sync", lambda e, ci=ci, tc=tc, shov=shov: e.dma_start(
                    out=shov[:, self.GCHB * ci + tc, :], in_=self.eo[:, tc, :]),
                    wait=(wait if tc == 0 else []), inc=self.c_ws[1], inc_k=16)
            return [(self.c_ws[1], self.c_ws[1].n)]
        self.emit_mlp("sh", self.TC // self.GCH, lambda ci, w, px: self.emit_shared_x(io, ci, w, px),
                      io["sgu_h"], io["sgu_l"], io["sd_h"], io["sd_l"], sink_sh, [], self.wsplit_sh_done)
        sh_done = [(self.c_ws[1], self.c_ws[1].n)]
        self.emit_table(io)
        eov = io["eo_d"].rearrange("(a p) d -> p a d", p=128)
        NCHE = self.CAP // self.GCH
        for ex in range(2):
            def sink_ex(ci, wait, ex=ex):
                for tc in range(self.GCHB):
                    row = (ex * self.CAP + self.GCH * ci) // 128 + tc
                    P.do("gpsimd", lambda e, row=row, tc=tc, eov=eov: e.dma_start(
                        out=eov[:, row, :], in_=self.eo[:, tc, :]),
                        wait=(wait if tc == 0 else []), inc=self.c_st[1], inc_k=16)
                snn = (self.c_st[1], self.c_st[1].n)
                r0 = ex * self.CAP + ci * self.GCH
                P.do("gpsimd", lambda e, r0=r0, i=io["eo_d"], o=io["ag2_parts"][ex * NCHE + ci]: e.collective_compute(
                    "AllGather", mybir.AluOpType.bypass, replica_groups=[list(range(self.NCORE))],
                    ins=[i[r0:r0 + self.GCH, :]], outs=[o]),
                    wait=[snn], inc=self.c_cc)
                return [snn]
            self.emit_mlp(f"ex{ex}", self.CAP // self.GCH,
                          lambda ci, w, px, ex=ex: self.emit_expert_x(io, ci, ex, w, px),
                          io["gu_h"][ex], io["gu_l"][ex], io["d_h"][ex], io["d_l"][ex], sink_ex,
                          [self.table_done], self.wsplit_ex_done)
        ag2n = (self.c_cc, self.c_cc.n)
        P.do("vector", lambda e: e.memset(self.fin[0:1, 0, :], 0.0), wait=[attn_all], inc=self.c_v)
        zn = (self.c_v, self.c_v.n)
        P.do("gpsimd", lambda e, z=io["ag2_zero"]: e.dma_start(out=z, in_=self.fin[0:1, 0, :]),
             wait=[zn], inc=self.c_gd[1], inc_k=16)
        zdn = (self.c_gd[1], self.c_gd[1].n)
        from bass_rust import IndirectOffsetOnAxis
        xlnf = io["xln_f"].rearrange("(a p) d -> p a d", p=128)
        outv = io["out"].rearrange("(a p) d -> p a d", p=128)
        gts = self.gates_done
        st_hist = {}
        cmb_hist = {}
        for a in range(self.NA):
            par = a % 2
            wld = [self.ln_done] if a < 2 else [(self.c_out, st_hist[a - 2])]
            P.do("sync", lambda e, a=a, par=par, xlnf=xlnf: e.dma_start(out=self.t_x[:, par, :], in_=xlnf[:, a, :]),
                 wait=wld, inc=self.c_ld[par], inc_k=16)
            P.do("sync", lambda e, a=a, par=par, shov=shov: e.dma_start(out=self.t_p1[:, par, :], in_=shov[:, a, :]),
                 wait=sh_done, inc=self.c_ld[par], inc_k=16)
            ldn = (self.c_ld[par], self.c_ld[par].n)
            for k in range(2):
                P.do("gpsimd", lambda e, a=a, k=k, par=par, src=io["ag2_full"]: e.indirect_dma_start(
                    self.cmb[:, k, :], None, src, IndirectOffsetOnAxis(self.cgi[:, a, k:k + 1], 0)),
                    wait=[ag2n, zdn, gts] + ([] if a < 1 else [(self.c_v, cmb_hist.get(a - 1, 0))]),
                    inc=self.c_gd[par], inc_k=16)
            gn = (self.c_gd[par], self.c_gd[par].n)
            P.do("vector", lambda e, par=par: e.tensor_tensor(self.fin[:, par, :], self.t_x[:, par, :], self.t_p1[:, par, :], op=ALU.add),
                 wait=[ldn, gn, (self.c_v, self.c_v.n)], inc=self.c_v)
            P.do("vector", lambda e, par=par: e.tensor_tensor(self.fin[:, par, :], self.fin[:, par, :], self.t_x[:, par, :], op=ALU.add),
                 wait=[(self.c_v, self.c_v.n)], inc=self.c_v)
            for k in range(2):
                P.do("vector", lambda e, a=a, k=k, par=par: e.tensor_scalar(
                    self.cmb[:, k, :], self.cmb[:, k, :], self.gsc[:, a, k:k + 1], None, op0=ALU.mult),
                    wait=[(self.c_v, self.c_v.n)], inc=self.c_v)
                P.do("vector", lambda e, k=k, par=par: e.tensor_tensor(self.fin[:, par, :], self.fin[:, par, :], self.cmb[:, k, :], op=ALU.add),
                     wait=[(self.c_v, self.c_v.n)], inc=self.c_v)
            sn = (self.c_v, self.c_v.n)
            cmb_hist[a] = self.c_v.n
            P.do("vector", lambda e, par=par: e.reduce_sum(self.stat[:, 0:1], self.fin[:, par, :], axis=AX.X),
                 wait=[sn], inc=self.c_v)
            P.do("scalar", lambda e, par=par: e.activation(self.t_x[:, par, :], self.fin[:, par, :], AF.Square, accum_out=self.stat[:, 1:2]),
                 wait=[sn, (self.c_s, self.c_s.n)], inc=self.c_s)
            sqn = (self.c_s, self.c_s.n)
            P.do("vector", lambda e: e.tensor_scalar(self.stat[:, 2:3], self.stat[:, 0:1], self.rinv[:, :], None, op0=ALU.mult),
                 wait=[(self.c_v, self.c_v.n)], inc=self.c_v)
            P.do("vector", lambda e: e.tensor_scalar(self.stat[:, 3:4], self.stat[:, 1:2], self.rinv[:, :], None, op0=ALU.mult),
                 wait=[sqn, (self.c_v, self.c_v.n)], inc=self.c_v)
            P.do("vector", lambda e: e.tensor_tensor(self.stat[:, 4:5], self.stat[:, 2:3], self.stat[:, 2:3], op=ALU.mult),
                 wait=[(self.c_v, self.c_v.n)], inc=self.c_v)
            P.do("vector", lambda e: e.tensor_tensor(self.stat[:, 4:5], self.stat[:, 3:4], self.stat[:, 4:5], op=ALU.subtract),
                 wait=[(self.c_v, self.c_v.n)], inc=self.c_v)
            vn = (self.c_v, self.c_v.n)
            P.do("scalar", lambda e: e.activation(self.stat[:, 5:6], self.stat[:, 4:5], AF.Ln, bias=self.eps[:, :]),
                 wait=[vn, (self.c_s, self.c_s.n)], inc=self.c_s)
            P.do("scalar", lambda e: e.activation(self.stat[:, 5:6], self.stat[:, 5:6], AF.Exp, scale=self.cm05[:, :]),
                 wait=[(self.c_s, self.c_s.n)], inc=self.c_s)
            rsn = (self.c_s, self.c_s.n)
            P.do("vector", lambda e, par=par: e.tensor_scalar(self.fin[:, par, :], self.fin[:, par, :], self.stat[:, 2:3], self.stat[:, 5:6], op0=ALU.subtract, op1=ALU.mult),
                 wait=[rsn, (self.c_v, self.c_v.n)], inc=self.c_v)
            fn = (self.c_v, self.c_v.n)
            P.do("gpsimd", lambda e, a=a, par=par, outv=outv: e.dma_start(out=outv[:, a, :], in_=self.fin[:, par, :]),
                 wait=[fn], inc=self.c_out, inc_k=16)
            st_hist[a] = self.c_out.n
        P.do("gpsimd", lambda e: None, wait=[(self.c_out, self.c_out.n)])


def build_program(cfg):
    from contextlib import ExitStack
    nc = bass.Bass("TRN2", target_bir_lowering=False)
    TB, TC, DM, HPC = cfg["TB"], cfg["TC"], cfg["DM"], cfg["HPC"]
    FF, E, CAP, NCORE = cfg["FF"], cfg["E"], cfg["CAP"], cfg["NCORE"]
    d = lambda n, sh, dt, k=None: nc.dram_tensor(n, sh, dt, kind=k) if k else nc.dram_tensor(n, sh, dt)
    x_b = d("x_b", [TB, DM], F32, "ExternalInput").ap()
    x_own = d("x_own", [TC, DM], F32, "ExternalInput").ap()
    wq = d("wq", [DM, HPC * 64], F32, "ExternalInput").ap()
    wk = d("wk", [DM, HPC * 64], F32, "ExternalInput").ap()
    wv = d("wv", [DM, HPC * 64], F32, "ExternalInput").ap()
    wp = d("wp", [HPC * 64, DM], F32, "ExternalInput").ap()
    draw = d("draw", [1, HPC], F32, "ExternalInput").ap()
    router_w = d("router_w", [DM, E], F32, "ExternalInput").ap()
    router_b = d("router_b", [1, E], F32, "ExternalInput").ap()
    msk = d("msk", [NCORE, 1], F32, "ExternalInput").ap()
    hoff = d("hoff", [1, 1], U32, "ExternalInput").ap()
    e0in = d("e0in", [1, 1], F32, "ExternalInput").ap()
    wgu = d("wgu", [2 * DM, 2 * FF], F32, "ExternalInput").ap()
    wdn = d("wdn", [2 * FF, DM], F32, "ExternalInput").ap()
    sgu = d("sgu", [DM, 2 * FF], F32, "ExternalInput").ap()
    sdn = d("sdn", [FF, DM], F32, "ExternalInput").ap()
    out = d("out", [TC, DM], F32, "ExternalOutput").ap()
    xhb = d("xhb", [TB, DM], BF16).ap()
    xlb = d("xlb", [TB, DM], BF16).ap()
    part = d("part", [TB, DM], F32).ap()
    partsum = d("partsum", [TB, DM], F32).ap()
    xln_f = d("xln_f", [TC, DM], F32).ap()
    xln_h = d("xln_h", [TC, DM], BF16).ap()
    xln_l = d("xln_l", [TC, DM], BF16).ap()
    cnt_d = d("cnt_d", [1, E], F32).ap()
    agc = d("agc", [NCORE * E], F32).ap()
    rec_d = d("rec_d", [2 * TC], F32).ap()
    ag1r = d("ag1r", [NCORE * 2 * TC], F32).ap()
    ag1h = d("ag1h", [NCORE * TC, DM], BF16).ap()
    ag1l = d("ag1l", [NCORE * TC, DM], BF16).ap()
    GCH = cfg.get("GCH", 512)
    xg_hb = d("xg_hb", [GCH, DM], BF16).ap()
    xg_lb = d("xg_lb", [GCH, DM], BF16).ap()
    gu_h = d("gu_h", [2 * DM, 2 * FF], BF16).ap()
    gu_l = d("gu_l", [2 * DM, 2 * FF], BF16).ap()
    d_h = d("d_h", [2 * FF, DM], BF16).ap()
    d_l = d("d_l", [2 * FF, DM], BF16).ap()
    sgu_h = d("sgu_h", [DM, 2 * FF], BF16).ap()
    sgu_l = d("sgu_l", [DM, 2 * FF], BF16).ap()
    sd_h = d("sd_h", [FF, DM], BF16).ap()
    sd_l = d("sd_l", [FF, DM], BF16).ap()
    sho_d = d("sho_d", [TC, DM], F32).ap()
    eo_d = d("eo_d", [2 * CAP, DM], F32).ap()
    ag2 = d("ag2", [NCORE * 2 * CAP + 1, DM], F32).ap()
    P = Prog(nc)
    with ExitStack() as semctx:
        with nc.Block() as block:
            with ExitStack() as actx:
                A = Attn(nc, P, actx, {"T": TB, "DM": DM, "HPC": HPC, "causal": True}, semctx)
                A.emit_xsplit(x_b, xhb, xlb)
                A.emit_tables(draw)
                ioA = {"wq": wq, "wk": wk, "wv": wv, "wp": wp, "xhb": xhb, "xlb": xlb, "part": part}
                for hp in range(HPC // 2):
                    A.emit_hp(hp, ioA)
            attn_all = [(c, c.n) for c in P._ctrs]
            with ExitStack() as mctx:
                M = Moe(nc, P, mctx, {"TC": TC, "DM": DM, "FF": FF, "E": E, "CAP": CAP,
                                      "NCORE": NCORE, "GCH": GCH}, semctx)
                ioM = {
                    "x_own": x_own, "part": part, "partsum": partsum, "hoff": hoff,
                    "xln_f": xln_f, "xln_h": xln_h, "xln_l": xln_l,
                    "router_w": router_w, "router_b": router_b, "msk": msk,
                    "cnt_d": cnt_d, "agc": agc, "rec_d": rec_d, "ag1r": ag1r,
                    "ag1h": ag1h, "ag1l": ag1l, "xg_hb": xg_hb, "xg_lb": xg_lb,
                    "e0in": e0in,
                    "wsplits_ex": [(wgu, gu_h, gu_l), (wdn, d_h, d_l)],
                    "wsplits_sh": [(sgu, sgu_h, sgu_l), (sdn, sd_h, sd_l)],
                    "gu_h": [gu_h[DM * ex:DM * (ex + 1), :] for ex in range(2)],
                    "gu_l": [gu_l[DM * ex:DM * (ex + 1), :] for ex in range(2)],
                    "d_h": [d_h[FF * ex:FF * (ex + 1), :] for ex in range(2)],
                    "d_l": [d_l[FF * ex:FF * (ex + 1), :] for ex in range(2)],
                    "sgu_h": sgu_h, "sgu_l": sgu_l, "sd_h": sd_h, "sd_l": sd_l,
                    "sho_d": sho_d, "eo_d": eo_d,
                    "ag2_parts": [ag2[j * NCORE * cfg["GCH"]:(j + 1) * NCORE * cfg["GCH"], :]
                                  for j in range(2 * (CAP // cfg["GCH"]))],
                    "ag2_zero": ag2[NCORE * 2 * CAP:NCORE * 2 * CAP + 1, :],
                    "ag2_full": ag2, "out": out,
                    "xln_h_t": xln_h, "xln_l_t": xln_l,
                }
                M.emit_consts(ioM, attn_all)
                M.wsplit_sh_done = M.emit_wsplits(ioM["wsplits_sh"], attn_all, fast=True)
                M.emit_ag0_ln_router(ioM, attn_all)
                M.emit_gates_records(ioM)
                M.emit_moe(ioM, attn_all)
                if cfg.get("debug"):
                    dbg = {}
                    fin_w = [(c, c.n) for c in P._ctrs]
                    def dump_dram(name, src_ap):
                        t = nc.dram_tensor("o_" + name, list(src_ap.shape), src_ap.dtype, kind="ExternalOutput").ap()
                        P.do("gpsimd", lambda e, t=t, s=src_ap: e.dma_start(out=t, in_=s), wait=fin_w, inc=M.c_g, inc_k=16)
                    def dump_sb(name, t_sb):
                        t = nc.dram_tensor("o_" + name, list(t_sb.shape), t_sb.dtype, kind="ExternalOutput").ap()
                        P.do("gpsimd", lambda e, t=t, s=t_sb: e.dma_start(out=t, in_=s[:]), wait=fin_w, inc=M.c_g, inc_k=16)
                    dump_dram("xln", xln_f)
                    dump_dram("psum", partsum)
                    dump_dram("sho", sho_d)
                    dump_dram("eo", eo_d)
                    dump_dram("agc", agc)
                    dump_dram("ag1r", ag1r)
                    dump_sb("idxf", M.idxf.ap())
                    dump_sb("valn", M.valn.ap())
                    dump_sb("posl", M.posl.ap())
                    dump_sb("gsc", M.gsc.ap())
                    dump_sb("cgi", M.cgi.ap())
                    dump_sb("tbl", M.tbl.ap())
                    dump_sb("recs", M.recs.ap())
                    dump_sb("ef", M.ef.ap())
                    dump_sb("pf", M.pf.ap())
                    dump_sb("colf", M.colf.ap())
                    dump_sb("rrow", M.rrow.ap())
                    dump_sb("tid", M.tidt.ap())
                    dump_sb("hh", M.hh.ap())
                    dump_sb("hl", M.hl.ap())
                    P.do("gpsimd", lambda e: None, wait=[(M.c_g, M.c_g.n)])
            P.emit(block)
    return nc


_CFG = {"TB": 4096, "TC": 2048, "DM": 1024, "HPC": 8, "FF": 1408, "E": 16,
        "CAP": 2560, "NCORE": 8, "GCH": 512}


def _interleave_gu(wg, wu):
    blocks = []
    for j in range(wg.shape[1] // 128):
        blocks.append(wg[:, 128 * j:128 * (j + 1)])
        blocks.append(wu[:, 128 * j:128 * (j + 1)])
    return np.concatenate(blocks, axis=1)


def kernel(L0_attn_x, L0_attn_w_attn, L0_attn_w_proj, L0_attn_decay_raw,
           L0_ffn_router_w, L0_ffn_router_b, L0_ffn_w_gate, L0_ffn_w_up,
           L0_ffn_w_down, L0_ffn_sw_gate, L0_ffn_sw_up, L0_ffn_sw_down):
    from concourse.bass_utils import run_bass_kernel_spmd
    cfg = _CFG
    TB, TC, DM, HPC, NCORE = cfg["TB"], cfg["TC"], cfg["DM"], cfg["HPC"], cfg["NCORE"]
    x = np.asarray(L0_attn_x)
    wa = np.asarray(L0_attn_w_attn)
    wpj = np.asarray(L0_attn_w_proj)
    draw = np.asarray(L0_attn_decay_raw)
    rw = np.asarray(L0_ffn_router_w)
    rb = np.asarray(L0_ffn_router_b)
    wg = np.asarray(L0_ffn_w_gate)
    wu = np.asarray(L0_ffn_w_up)
    wd = np.asarray(L0_ffn_w_down)
    sg_ = np.asarray(L0_ffn_sw_gate)
    su_ = np.asarray(L0_ffn_sw_up)
    sd_ = np.asarray(L0_ffn_sw_down)
    nc = build_program(cfg)
    sgu = np.ascontiguousarray(_interleave_gu(sg_, su_))
    in_maps = []
    for c in range(NCORE):
        b, half = c // 2, c % 2
        heads = [half * HPC + i for i in range(HPC)]
        wq = np.concatenate([wa[:, 64 * h:64 * h + 64] for h in heads], axis=1)
        wk = np.concatenate([wa[:, DM + 64 * h:DM + 64 * h + 64] for h in heads], axis=1)
        wv = np.concatenate([wa[:, 2 * DM + 64 * h:2 * DM + 64 * h + 64] for h in heads], axis=1)
        wpc = np.concatenate([wpj[64 * h:64 * h + 64, :] for h in heads], axis=0)
        e0, e1 = 2 * c, 2 * c + 1
        wgu = np.concatenate([_interleave_gu(wg[e0], wu[e0]), _interleave_gu(wg[e1], wu[e1])], axis=0)
        wdn = np.concatenate([wd[e0], wd[e1]], axis=0)
        in_maps.append({
            "x_b": np.ascontiguousarray(x[b]),
            "x_own": np.ascontiguousarray(x[b][half * TC:(half + 1) * TC]),
            "wq": np.ascontiguousarray(wq), "wk": np.ascontiguousarray(wk),
            "wv": np.ascontiguousarray(wv), "wp": np.ascontiguousarray(wpc),
            "draw": np.ascontiguousarray(draw[heads]).reshape(1, HPC),
            "router_w": np.ascontiguousarray(rw),
            "router_b": np.ascontiguousarray(rb).reshape(1, cfg["E"]),
            "msk": (np.arange(NCORE) < c).astype(np.float32).reshape(NCORE, 1),
            "hoff": np.array([[half * (TC // 128)]], dtype=np.uint32),
            "e0in": np.array([[2.0 * c]], dtype=np.float32),
            "wgu": np.ascontiguousarray(wgu), "wdn": np.ascontiguousarray(wdn),
            "sgu": sgu, "sdn": np.ascontiguousarray(sd_),
        })
    res = run_bass_kernel_spmd(nc, in_maps, core_ids=list(range(NCORE)))
    out = np.concatenate([np.asarray(res.results[c]["out"]) for c in range(NCORE)], axis=0)
    return out.reshape(x.shape[0], TB, DM).astype(np.float32)


# version 14 (final)
# speedup vs baseline: 1.0566x; 1.0566x over previous
import numpy as np
import concourse.bass as bass
import concourse.mybir as mybir

F32 = mybir.dt.float32
BF16 = mybir.dt.bfloat16
I32 = mybir.dt.int32
U32 = mybir.dt.uint32
I16 = mybir.dt.int16
ALU = mybir.AluOpType
AF = mybir.ActivationFunctionType
AX = mybir.AxisListType

ENGINES = ("sync", "tensor", "vector", "scalar", "gpsimd")


class Ctr:
    def __init__(self, sem, name):
        self.sem = sem
        self.name = name
        self.n = 0


class Prog:
    def __init__(self, nc):
        self.nc = nc
        self.ops = {e: [] for e in ENGINES}
        self._ctrs = []
        self._stack = None

    def ctr(self, sem, name):
        c = Ctr(sem, name)
        self._ctrs.append(c)
        return c

    def do(self, eng, fn, wait=(), inc=None, inc_k=1):
        waits = []
        def add(w):
            if isinstance(w, list):
                for x in w:
                    add(x)
            elif isinstance(w, tuple):
                waits.append((w[0], w[1]))
            else:
                waits.append((w, w.n))
        for w in wait:
            add(w)
        if inc is not None:
            inc.n += inc_k
        self.ops[eng].append((waits, fn, inc, inc_k))

    def emit(self, block):
        def mk(eng_name):
            def emit_engine(e):
                last_wait = {}
                for waits, fn, inc, inc_k in self.ops[eng_name]:
                    for c, n in waits:
                        if n > 0 and last_wait.get(c, -1) < n:
                            e.wait_ge(c.sem, n)
                            last_wait[c] = n
                    inst = fn(e) if fn is not None else None
                    if inc is not None:
                        if inst is None:
                            inst = e.drain()
                        inst.then_inc(inc.sem, inc_k)
            return emit_engine
        for eng_name in ENGINES:
            if self.ops[eng_name]:
                getattr(block, eng_name)(mk(eng_name))



D = 64


class Attn:
    def __init__(self, nc, P, ctx, cfg, semctx=None):
        self.nc, self.P, self.cfg = nc, P, cfg
        T, DM, HPC = cfg["T"], cfg["DM"], cfg["HPC"]
        self.T, self.DM, self.HPC = T, DM, HPC
        self.QT = 512
        self.TCH = 512
        self.SHIFT = self.QT - 128
        self.SW_W = T + self.SHIFT + 1
        self.NA = T // 128
        self.NDK = DM // 128
        e = ctx.enter_context
        s = lambda *a: e(nc.sbuf_tensor(*a))
        self.XS = 128
        self.xs_f = s("xs_f", [128, 2, (self.XS // 128) * DM], F32)
        self.xs_h = s("xs_h", [128, 2, (self.XS // 128) * DM], BF16)
        self.xs_l = s("xs_l", [128, 2, (self.XS // 128) * DM], BF16)
        self.xfm_h = s("xfm_h", [128, 2, self.TCH], BF16)
        self.xfm_l = s("xfm_l", [128, 2, self.TCH], BF16)
        self.w_f = s("w_f", [128, self.NDK, 128], F32)
        self.w_h = s("w_h", [128, self.NDK, 384], BF16)
        self.w_l = s("w_l", [128, self.NDK, 384], BF16)
        self.wp_f = s("wp_f", [128, DM], F32)
        self.wp_h = s("wp_h", [128, DM], BF16)
        self.wp_l = s("wp_l", [128, DM], BF16)
        self.q_h = s("q_h", [128, T], BF16)
        self.q_l = s("q_l", [128, T], BF16)
        self.k_h = s("k_h", [128, T], BF16)
        self.k_l = s("k_l", [128, T], BF16)
        self.v_tm = s("v_tm", [128, self.NA, 132], F32)
        self.vfm_sb = s("vfm_sb", [128, 2, self.TCH], F32)
        self.SL = s("SL", [128, self.SW_W], F32)
        self.Smask = s("Smask", [128, self.SW_W], BF16)
        self.Sw = s("Sw", [128, 2, self.SW_W], F32)
        self.dec = s("dec", [1, max(HPC, 2)], F32)
        self.decb = s("decb", [128, max(HPC, 2)], F32)
        self.ones1 = s("ones1", [1, 128], F32)
        self.ident = s("ident", [128, 128], F32)
        self.c1 = s("c1", [1, 1], F32)
        self.c1b = s("c1b", [128, 1], F32)
        self.c0125 = s("c0125", [128, 1], F32)
        self.cm1 = s("cm1", [128, 1], F32)
        self.pw = s("pw", [128, 2, 2, self.QT], F32)
        self.rr = s("rr", [1, self.QT], F32)
        self.bc_sb = s("bc_sb", [64, self.QT], F32)
        self.yf = s("yf", [128, self.QT], F32)
        self.y_h = s("y_h", [128, T], BF16)
        self.y_l = s("y_l", [128, T], BF16)
        self.po_sb = s("po_sb", [128, 2, DM], F32)
        self.ps = [e(nc.psum_tensor(f"ps{i}", [128, 512], F32)) for i in range(8)]
        se = (semctx or ctx).enter_context
        c = lambda name: P.ctr(se(nc.semaphore(name)), name)
        self.c_x = [c("a_x0"), c("a_x1")]; self.c_xo = [c("a_xo0"), c("a_xo1")]
        self.c_prv = c("a_prv"); self.c_prg = c("a_prg"); self.c_prs = c("a_prs"); self.c_prt = c("a_prt")
        self.c_xd = c("a_xd")
        self.c_wL = c("a_wL"); self.c_wG = c("a_wG"); self.c_wS = c("a_wS")
        self.c_fm = [c("a_fm0"), c("a_fm1")]; self.c_qkv = c("a_qkv")
        self.c_evS = c("a_evS"); self.c_evV = c("a_evV")
        self.c_vtT = c("a_vtT"); self.c_vtV = c("a_vtV"); self.c_sw = c("a_sw"); self.c_swS = c("a_swS")
        self.c_s = c("a_s"); self.c_exp = c("a_exp"); self.c_pw = c("a_pw")
        self.c_avn = c("a_avn")
        self.c_epV = c("a_epV"); self.c_epT = c("a_epT"); self.c_epS = c("a_epS")
        self.c_po = c("a_po"); self.c_pev = c("a_pev"); self.c_pst = [c("a_pst0"), c("a_pst1")]
        self._ep_last_n = 0
        self._exp_last_n = 0
        self._avn_last_n = 0
        self._pev_end = 0
        self._po_end = 0
        self._fm_prev_hp = (self.c_qkv, 0)
        self._flash_end = [(self.c_epV, 0), (self.c_epS, 0), (self.c_epT, 0)]

    def srecip(self, out, in_, ctrS, wait=(), nP=128):
        P = self.P
        P.do("scalar", lambda e: e.activation(out, in_, AF.Ln), wait=list(wait), inc=ctrS)
        P.do("scalar", lambda e, nP=nP: e.activation(out, out, AF.Exp, scale=self.cm1[0:nP, :]),
             wait=[(ctrS, ctrS.n)], inc=ctrS)

    def emit_xsplit(self, x_dram, xhb, xlb):
        P, T, DM, XS = self.P, self.T, self.DM, self.XS
        na = XS // 128
        for i, t0 in enumerate(range(0, T, XS)):
            par = i % 2
            cx, cxo = self.c_x[par], self.c_xo[par]
            a0 = t0 // 128
            sl = x_dram.rearrange("(a p) d -> p a d", p=128)[:, a0:a0 + na, :]
            w = [(cxo, cxo.n)] if i >= 2 else []
            P.do("sync", lambda e, sl=sl, par=par: e.dma_start(out=self.xs_f[:, par, :].rearrange("p (a d) -> p a d", d=self.DM), in_=sl),
                 wait=w, inc=cx, inc_k=16)
            xn = (cx, cx.n)
            P.do("vector", lambda e, par=par: e.tensor_copy(self.xs_h[:, par, :], self.xs_f[:, par, :]),
                 wait=[xn], inc=self.c_prv)
            P.do("vector", lambda e, par=par: e.tensor_tensor(self.xs_l[:, par, :], self.xs_f[:, par, :], self.xs_h[:, par, :], op=ALU.subtract),
                 wait=[(self.c_prv, self.c_prv.n)], inc=self.c_prv)
            pn = (self.c_prv, self.c_prv.n)
            hb = xhb.rearrange("(a p) d -> p a d", p=128)[:, a0:a0 + na, :]
            lb = xlb.rearrange("(a p) d -> p a d", p=128)[:, a0:a0 + na, :]
            P.do("gpsimd", lambda e, hb=hb, par=par: e.dma_start(out=hb, in_=self.xs_h[:, par, :].rearrange("p (a d) -> p a d", d=self.DM)),
                 wait=[pn], inc=cxo, inc_k=16)
            P.do("gpsimd", lambda e, lb=lb, par=par: e.dma_start(out=lb, in_=self.xs_l[:, par, :].rearrange("p (a d) -> p a d", d=self.DM)),
                 inc=cxo, inc_k=16)
        self.xsplit_done = [(self.c_xo[0], self.c_xo[0].n), (self.c_xo[1], self.c_xo[1].n)]

    def emit_tables(self, draw):
        P, HPC, W, SHIFT = self.P, self.HPC, self.SW_W, self.SHIFT
        P.do("vector", lambda e: e.memset(self.ones1[:, :], 1.0), inc=self.c_prv)
        P.do("vector", lambda e: e.memset(self.c1[:, :], 1.0), inc=self.c_prv)
        P.do("vector", lambda e: e.memset(self.c1b[:, :], 1.0), inc=self.c_prv)
        P.do("vector", lambda e: e.memset(self.c0125[:, :], 0.125), inc=self.c_prv)
        P.do("vector", lambda e: e.memset(self.cm1[:, :], -1.0), inc=self.c_prv)
        consts_n = (self.c_prv, self.c_prv.n)
        P.do("gpsimd", lambda e: e.iota(self.ident[:, :], pattern=[[1, 128]], base=0,
                                        channel_multiplier=-1, allow_small_or_imprecise_dtypes=True),
             inc=self.c_prg)
        idn = (self.c_prg, self.c_prg.n)
        P.do("vector", lambda e: e.tensor_scalar(self.ident[:, :], self.ident[:, :], 0.0, None, op0=ALU.is_equal),
             wait=[idn], inc=self.c_prv)
        self.ident_ready = (self.c_prv, self.c_prv.n)
        P.do("gpsimd", lambda e: e.iota(self.SL[:, :], pattern=[[1, W]], base=-SHIFT,
                                        channel_multiplier=-1, allow_small_or_imprecise_dtypes=True),
             inc=self.c_prg)
        sln = (self.c_prg, self.c_prg.n)
        P.do("vector", lambda e: e.tensor_scalar(self.Smask[:, :], self.SL[:, :], 0.0, None, op0=ALU.is_ge),
             wait=[sln], inc=self.c_prv)
        mkn = (self.c_prv, self.c_prv.n)
        P.do("scalar", lambda e: e.activation(self.SL[:, :], self.SL[:, :], AF.Relu),
             wait=[mkn], inc=self.c_prs)
        P.do("scalar", lambda e: e.activation(self.SL[:, :], self.SL[:, :], AF.Ln, bias=self.c1b[:, :], scale=1.0),
             wait=[(self.c_prs, self.c_prs.n)], inc=self.c_prs)
        self.SL_done = (self.c_prs, self.c_prs.n)
        P.do("gpsimd", lambda e, draw=draw: e.dma_start(out=self.dec[0:1, 0:self.HPC], in_=draw),
             inc=self.c_xd, inc_k=16)
        dn = (self.c_xd, self.c_xd.n)
        P.do("scalar", lambda e: e.activation(self.dec[0:1, 0:self.HPC], self.dec[0:1, 0:self.HPC], AF.Exp),
             wait=[dn, consts_n], inc=self.c_prs)
        P.do("scalar", lambda e: e.activation(self.dec[0:1, 0:self.HPC], self.dec[0:1, 0:self.HPC], AF.Ln, bias=self.c1[0:1, 0:1], scale=1.0),
             wait=[(self.c_prs, self.c_prs.n)], inc=self.c_prs)
        decn = (self.c_prs, self.c_prs.n)
        P.do("tensor", lambda e: e.matmul(self.ps[7][:, 0:self.HPC], self.ones1[:, :], self.dec[0:1, 0:self.HPC], start=True, stop=True),
             wait=[decn, consts_n], inc=self.c_prt)
        bn = (self.c_prt, self.c_prt.n)
        P.do("vector", lambda e: e.tensor_copy(self.decb[:, 0:self.HPC], self.ps[7][:, 0:self.HPC]),
             wait=[bn], inc=self.c_prv)
        self.tables_done = [(self.c_prv, self.c_prv.n), self.SL_done]

    def emit_hp(self, hp, io):
        P, T, DM = self.P, self.T, self.DM
        TCH, QT, SHIFT = self.TCH, self.QT, self.SHIFT
        NDK, NA = self.NDK, self.NA
        prev_flash = self._flash_end
        prev_pev = (self.c_pev, self._pev_end)
        prev_po = (self.c_po, self._po_end)

        wq, wk, wv, wp = io["wq"], io["wk"], io["wv"], io["wp"]
        for j, w in enumerate((wq, wk, wv)):
            sl = w.rearrange("(a p) c -> p a c", p=128)[:, :, 128 * hp:128 * (hp + 1)]
            wprev = (self.c_wS, self.c_wS.n)
            P.do("sync", lambda e, sl=sl: e.dma_start(out=self.w_f[:, :, :], in_=sl),
                 wait=[prev_po, wprev], inc=self.c_wL, inc_k=16)
            P.do("gpsimd", lambda e, sl=sl, j=j: e.dma_start(out=self.w_h[:, :, 128 * j:128 * (j + 1)], in_=sl),
                 wait=[prev_po, wprev], inc=self.c_wG, inc_k=16)
            wn = [(self.c_wL, self.c_wL.n), (self.c_wG, self.c_wG.n)]
            P.do("vector", lambda e, j=j: e.tensor_tensor(self.w_l[:, :, 128 * j:128 * (j + 1)], self.w_f[:, :, :], self.w_h[:, :, 128 * j:128 * (j + 1)], op=ALU.subtract),
                 wait=wn, inc=self.c_wS)
        wprev = (self.c_wS, self.c_wS.n)
        slp = wp.rearrange("(a p) c -> p a c", p=128)[:, hp, :]
        P.do("sync", lambda e, slp=slp: e.dma_start(out=self.wp_f[:, :], in_=slp),
             wait=[prev_po, wprev], inc=self.c_wL, inc_k=16)
        P.do("gpsimd", lambda e, slp=slp: e.dma_start(out=self.wp_h[:, :], in_=slp),
             wait=[prev_po, wprev], inc=self.c_wG, inc_k=16)
        wn2 = [(self.c_wL, self.c_wL.n), (self.c_wG, self.c_wG.n)]
        P.do("vector", lambda e: e.tensor_tensor(self.wp_l[:, :], self.wp_f[:, :], self.wp_h[:, :], op=ALU.subtract),
             wait=wn2, inc=self.c_wS)
        w_ready = [(self.c_wS, self.c_wS.n), (self.c_wG, self.c_wG.n)]

        P.do("vector", lambda e: e.memset(self.v_tm[:, :, 64:65], 1.0), wait=[prev_flash], inc=self.c_vtV)
        P.do("vector", lambda e: e.memset(self.v_tm[:, :, 130:131], 1.0), wait=[(self.c_vtV, self.c_vtV.n)], inc=self.c_vtV)

        xhb, xlb = io["xhb"], io["xlb"]
        ev_after, qkv_after, vt_after = {}, {}, {}
        fm_used = {}
        pstr_last = {0: (self.c_vtV, 0), 1: (self.tables_done if hp == 0 else (self.c_vtV, 0))}
        for it, t0 in enumerate(range(0, T, TCH)):
            par = it % 2
            psq, psk, psv = (self.ps[0], self.ps[1], self.ps[2]) if par == 0 else (self.ps[4], self.ps[5], self.ps[6])
            for dk in range(NDK):
                g = it * NDK + dk
                fpar = g % 2
                cfm = self.c_fm[fpar]
                for src, dst in ((xhb, self.xfm_h), (xlb, self.xfm_l)):
                    sl = src[t0:t0 + TCH, 128 * dk:128 * (dk + 1)]
                    w = list(self.xsplit_done) + [self._fm_prev_hp]
                    if g >= 2:
                        w.append((self.c_qkv, fm_used[g - 2]))
                    P.do("sync", lambda e, sl=sl, dst=dst, fpar=fpar: e.dma_start(out=dst[:, fpar, :], in_=sl, transpose=True),
                         wait=w, inc=cfm, inc_k=16)
                fmn = (cfm, cfm.n)
                for j, ps in enumerate((psq, psk, psv)):
                    for pi, (wt, xt) in enumerate(((self.w_h, self.xfm_h), (self.w_h, self.xfm_l), (self.w_l, self.xfm_h))):
                        w = [w_ready, fmn, prev_pev]
                        if it >= 2:
                            w.append(ev_after[it - 2])
                        first_of_dk = (pi == 0 and j == 0)
                        last = (j == 2 and pi == 2)
                        P.do("tensor", lambda e, wt=wt, xt=xt, ps=ps, dk=dk, j=j, pi=pi, fpar=fpar: e.matmul(
                            ps[:, :], wt[:, dk, 128 * j:128 * (j + 1)], xt[:, fpar, :],
                            start=(dk == 0 and pi == 0), stop=(dk == NDK - 1 and pi == 2)),
                            wait=(w if first_of_dk else []), inc=(self.c_qkv if last else None))
                fm_used[g] = self.c_qkv.n
            qkv_after[it] = self.c_qkv.n
            qn = (self.c_qkv, self.c_qkv.n)
            for (ps, hi, lo) in ((psq, self.q_h, self.q_l), (psk, self.k_h, self.k_l)):
                P.do("scalar", lambda e, ps=ps, hi=hi, t0=t0: e.copy(hi[:, t0:t0 + TCH], ps[:, :]),
                     wait=[qn], inc=self.c_evS)
                hin = (self.c_evS, self.c_evS.n)
                P.do("vector", lambda e, ps=ps, hi=hi, lo=lo, t0=t0: e.tensor_tensor(
                    lo[:, t0:t0 + TCH], ps[:, :], hi[:, t0:t0 + TCH], op=ALU.subtract),
                    wait=[hin], inc=self.c_evV)
            w = [qn]
            if it >= 2:
                w.append((self.c_vtV, vt_after[it - 2]))
            P.do("vector", lambda e, psv=psv, par=par: e.tensor_copy(self.vfm_sb[:, par, :], psv[:, :]),
                 wait=w, inc=self.c_evV)
            ev_after[it] = [(self.c_evS, self.c_evS.n), (self.c_evV, self.c_evV.n)]
            vev = (self.c_evV, self.c_evV.n)
            for c in range(TCH // 128):
                cg = t0 // 128 + c
                pp = c % 2
                pstr = self.ps[3] if pp == 0 else self.ps[7]
                P.do("tensor", lambda e, c=c, par=par, pstr=pstr: e.transpose(
                    pstr[:, 0:128], self.vfm_sb[:, par, 128 * c:128 * (c + 1)], self.ident[:, :]),
                    wait=[vev, pstr_last[pp], self.ident_ready], inc=self.c_vtT)
                tn = (self.c_vtT, self.c_vtT.n)
                P.do("vector", lambda e, cg=cg, pstr=pstr: e.tensor_copy(self.v_tm[:, cg, 0:64], pstr[0:128, 0:64]),
                     wait=[tn], inc=self.c_vtV)
                P.do("vector", lambda e, cg=cg, pstr=pstr: e.tensor_copy(self.v_tm[:, cg, 66:130], pstr[0:128, 64:128]),
                     wait=[(self.c_vtV, self.c_vtV.n)], inc=self.c_vtV)
                pstr_last[pp] = (self.c_vtV, self.c_vtV.n)
            vt_after[it] = self.c_vtV.n
            if it == min(1, T // TCH - 1):
                for h2 in range(2):
                    h = 2 * hp + h2
                    P.do("vector", lambda e, h2=h2, h=h: e.tensor_scalar(
                        self.Sw[:, h2, :], self.SL[:, :], self.decb[:, h:h + 1], 1.0, op0=ALU.mult, op1=ALU.add),
                        wait=[self.tables_done, prev_flash], inc=self.c_sw)
                    self.srecip(self.Sw[:, h2, :], self.Sw[:, h2, :], self.c_swS, wait=[(self.c_sw, self.c_sw.n)])
                    P.do("vector", lambda e, h2=h2: e.tensor_tensor(self.Sw[:, h2, :], self.Sw[:, h2, :], self.Smask[:, :], op=ALU.mult),
                         wait=[(self.c_swS, self.c_swS.n)], inc=self.c_sw)
                sw_ready = (self.c_sw, self.c_sw.n)
        self._fm_prev_hp = (self.c_qkv, self.c_qkv.n)
        qkv_all = (self.c_vtV, self.c_vtV.n)
        qk_ready = [(self.c_evS, self.c_evS.n), (self.c_evV, self.c_evV.n)]
        if self.cfg.get("qkv_only"):
            self.attn_done = [qk_ready, qkv_all]
            return

        exp_after, avn_after = {}, {}
        bc_copy_n = {}
        ep_by_par = getattr(self, "_ep_by_par", {0: 0, 1: 0})
        for qi, q0 in enumerate(range(0, T, QT)):
            qpar = qi % 2
            psav = [self.ps[4 + 2 * qpar + h2] for h2 in range(2)]
            psbc = [self.ps[4 + 2 * (1 - qpar) + h2] for h2 in range(2)]
            cmax = min(T, q0 + QT) // 128 if self.cfg.get("causal", True) else T // 128
            ep_prev = (self.c_epV, ep_by_par[qpar])

            def emit_s(c):
                par = c % 2
                for h2 in range(2):
                    pss = self.ps[2 * par + h2]
                    b0 = 64 * h2
                    w = [qkv_all, qk_ready, sw_ready]
                    if (q0, c - 2, h2) in exp_after:
                        w.append((self.c_exp, exp_after[(q0, c - 2, h2)]))
                    else:
                        w.append((self.c_exp, self._exp_last_n))
                    for pi, (kk, qq) in enumerate(((self.k_h, self.q_h), (self.k_h, self.q_l), (self.k_l, self.q_h))):
                        P.do("tensor", lambda e, pss=pss, kk=kk, qq=qq, b0=b0, c=c, q0=q0, pi=pi: e.matmul(
                            pss[:, :], kk[b0:b0 + 64, 128 * c:128 * (c + 1)], qq[b0:b0 + 64, q0:q0 + QT],
                            start=(pi == 0), stop=(pi == 2), tile_position=(b0, 0)),
                            wait=(w if pi == 0 else []), inc=(self.c_s if pi == 2 else None))
                    sn = (self.c_s, self.c_s.n)
                    w = [sn]
                    if (q0, c - 2, h2) in avn_after:
                        w.append((self.c_avn, avn_after[(q0, c - 2, h2)]))
                    else:
                        w.append((self.c_avn, self._avn_last_n))
                    P.do("scalar", lambda e, pss=pss, h2=h2, par=par: e.activation(
                        self.pw[:, h2, par, :], pss[:, :], AF.Exp, scale=self.c0125[:, :]),
                        wait=w, inc=self.c_exp)
                    exp_after[(q0, c, h2)] = self.c_exp.n
                    en = (self.c_exp, self.c_exp.n)
                    u0 = SHIFT + q0 - 128 * c
                    P.do("vector", lambda e, h2=h2, par=par, u0=u0: e.tensor_tensor(
                        self.pw[:, h2, par, :], self.pw[:, h2, par, :], self.Sw[:, h2, u0:u0 + QT], op=ALU.mult),
                        wait=[en], inc=self.c_pw)

            def emit_av(c):
                par = c % 2
                for h2 in range(2):
                    pn = (self.c_pw, pw_n[(c, h2)])
                    w = [pn]
                    if c == 0:
                        w.append(ep_prev)
                        if (q0, h2) in bc_prev:
                            w.append((self.c_epS, bc_prev[(q0, h2)]))
                    P.do("tensor", lambda e, h2=h2, par=par, c=c, q0=q0, cmax=cmax, psav=psav: e.matmul(
                        psav[h2][0:65, :], self.v_tm[:, c, 66 * h2:66 * h2 + 65], self.pw[:, h2, par, :],
                        start=(c == 0), stop=(c == cmax - 1)),
                        wait=w, inc=self.c_avn)
                    avn_after[(q0, c, h2)] = self.c_avn.n

            pw_n = {}
            bc_prev = getattr(self, "_bc_prev", {})
            for c in range(cmax):
                sbase = self.c_pw.n
                emit_s(c)
                pw_n[(c, 0)] = sbase + 1
                pw_n[(c, 1)] = sbase + 2
                if c >= 1:
                    emit_av(c - 1)
            emit_av(cmax - 1)
            bn_prev = None
            for h2 in range(2):
                avn = (self.c_avn, avn_after[(q0, cmax - 1, h2)])
                w = [avn] + ([bn_prev] if bn_prev else [])
                self.srecip(self.rr[0:1, :], psav[h2][64:65, :], self.c_epS, wait=w, nP=1)
                rn = (self.c_epS, self.c_epS.n)
                P.do("tensor", lambda e, h2=h2, psbc=psbc: e.matmul(psbc[h2][0:64, :], self.ones1[0:1, 0:64], self.rr[0:1, :], start=True, stop=True),
                     wait=[rn], inc=self.c_epT)
                bn = (self.c_epT, self.c_epT.n)
                bn_prev = bn
                P.do("scalar", lambda e, h2=h2, psbc=psbc: e.copy(self.bc_sb[:, :], psbc[h2][0:64, :]),
                     wait=[bn], inc=self.c_epS)
                self._bc_prev = getattr(self, "_bc_prev", {})
                self._bc_prev[(q0 + QT, h2)] = self.c_epS.n
                cn = (self.c_epS, self.c_epS.n)
                b0 = 64 * h2
                P.do("vector", lambda e, h2=h2, b0=b0, psav=psav: e.tensor_tensor(self.yf[b0:b0 + 64, :], psav[h2][0:64, :], self.bc_sb[:, :], op=ALU.mult),
                     wait=[cn], inc=self.c_epV)
                yn = (self.c_epV, self.c_epV.n)
                P.do("scalar", lambda e, b0=b0, q0=q0: e.copy(self.y_h[b0:b0 + 64, q0:q0 + QT], self.yf[b0:b0 + 64, :]),
                     wait=[yn, prev_po], inc=self.c_epS)
                hn = (self.c_epS, self.c_epS.n)
                P.do("vector", lambda e, b0=b0, q0=q0: e.tensor_tensor(
                    self.y_l[b0:b0 + 64, q0:q0 + QT], self.yf[b0:b0 + 64, :], self.y_h[b0:b0 + 64, q0:q0 + QT], op=ALU.subtract),
                    wait=[hn], inc=self.c_epV)
            ep_by_par[qpar] = self.c_epV.n
            self._exp_last_n = self.c_exp.n
            self._avn_last_n = self.c_avn.n
        self._ep_by_par = ep_by_par
        self._bc_prev = {(0, h2): self._bc_prev[(T, h2)] for h2 in range(2)}
        self._flash_end = [(self.c_epV, self.c_epV.n), (self.c_epS, self.c_epS.n), (self.c_epT, self.c_epT.n)]
        flash_done = self._flash_end

        if self.cfg.get("no_proj"):
            self.attn_done = [flash_done]
            return
        part = io["part"]
        pev_after = {}
        pst_start = [self.c_pst[0].n, self.c_pst[1].n]
        pst_prev_total = getattr(self, "_pst_totals", [(self.c_pst[0], 0), (self.c_pst[1], 0)])
        for tc in range(NA):
            par = tc % 2
            for hi_, h0 in enumerate(range(0, DM, 512)):
                w512 = min(512, DM - h0)
                pspo = self.ps[2 * par + hi_]
                for pi, (yy, ww) in enumerate(((self.y_h, self.wp_h), (self.y_l, self.wp_h), (self.y_h, self.wp_l))):
                    w = [flash_done]
                    if tc >= 2:
                        w.append((self.c_pev, pev_after[tc - 2]))
                    P.do("tensor", lambda e, pspo=pspo, yy=yy, ww=ww, tc=tc, h0=h0, w512=w512, pi=pi: e.matmul(
                        pspo[:, 0:w512], yy[:, 128 * tc:128 * (tc + 1)], ww[:, h0:h0 + w512],
                        start=(pi == 0), stop=(pi == 2)),
                        wait=(w if pi == 0 else []), inc=(self.c_po if pi == 2 else None))
                pon = (self.c_po, self.c_po.n)
                wv_ = [pon]
                if tc >= 2:
                    wv_.append((self.c_pst[par], pst_start[par] + 16 * (tc // 2)))
                else:
                    wv_.extend(pst_prev_total)
                P.do("vector", lambda e, pspo=pspo, par=par, h0=h0, w512=w512: e.tensor_copy(
                    self.po_sb[:, par, h0:h0 + w512], pspo[:, 0:w512]),
                    wait=wv_, inc=self.c_pev)
            pev_after[tc] = self.c_pev.n
            pvn = (self.c_pev, self.c_pev.n)
            dst = part.rearrange("(a p) d -> p a d", p=128)[:, tc, :]
            w = [pvn]
            if hp > 0:
                w.extend(pst_prev_total)
            if hp == 0:
                P.do("gpsimd", lambda e, dst=dst, par=par: e.dma_start(out=dst, in_=self.po_sb[:, par, :]),
                     wait=w, inc=self.c_pst[par], inc_k=16)
            else:
                P.do("gpsimd", lambda e, dst=dst, par=par: e.dma_start(out=dst, in_=self.po_sb[:, par, :], accum_op=ALU.add),
                     wait=w, inc=self.c_pst[par], inc_k=16)
        self._pst_totals = [(self.c_pst[0], self.c_pst[0].n), (self.c_pst[1], self.c_pst[1].n)]
        self._pev_end = self.c_pev.n
        self._po_end = self.c_po.n
        self.attn_done = self._pst_totals


class Moe:

    def __init__(self, nc, P, ctx, cfg, semctx=None):
        self.nc, self.P, self.cfg = nc, P, cfg
        TC = cfg["TC"]
        DM = cfg["DM"]
        FF = cfg["FF"]
        self.E = cfg["E"]
        self.CAP = cfg["CAP"]
        self.NCORE = cfg["NCORE"]
        self.TC, self.DM, self.FF = TC, DM, FF
        self.NA = TC // 128
        self.NDK = DM // 128
        self.NFB = (FF + 127) // 128
        self.GCH = cfg.get("GCH", 512)
        self.NB = self.CAP // 128
        self.NCOL = 2 * self.NB + 1
        self.NREC = 2 * self.NA
        self.NCH = self.NCORE * self.NREC
        e = ctx.enter_context
        s = lambda *a: e(nc.sbuf_tensor(*a))
        self.t_x = s("m_x", [128, 2, DM], F32)
        self.t_p1 = s("m_p1", [128, 2, DM], F32)
        self.t_ln = s("m_ln", [128, 3, DM], F32)
        self.t_lh = s("m_lh", [128, 3, DM], BF16)
        self.t_ll = s("m_ll", [128, 3, DM], BF16)
        self.stat = s("m_stat", [128, 8], F32)
        self.eps = s("m_eps", [128, 1], F32)
        self.hofft = s("m_hofft", [1, 1], U32)
        self.e0t = s("m_e0t", [1, 1], F32)
        self.e0col = s("m_e0col", [128, 1], F32)
        self.cm05 = s("m_cm05", [128, 1], F32)
        self.c1e9 = s("m_c1e9", [128, 1], F32)
        self.rinv = s("m_rinv", [128, 1], F32)
        self.mcm1 = s("m_mcm1", [128, 1], F32)
        self.rw_f = s("m_rwf", [128, self.NDK, 16], F32)
        self.rw_h = s("m_rwh", [128, self.NDK, 16], BF16)
        self.rw_l = s("m_rwl", [128, self.NDK, 16], BF16)
        self.rb_b = s("m_rbb", [128, 16], F32)
        self.lg = s("m_lg", [128, 16], F32)
        self.prob = s("m_prob", [128, 16], F32)
        self.m8 = s("m_m8", [128, 8], F32)
        self.i8 = s("m_i8", [128, 8], U32)
        self.idxf = s("m_idxf", [128, self.NA, 2], F32)
        self.valn = s("m_valn", [128, self.NA, 2], F32)
        self.posl = s("m_posl", [128, self.NA, 2], F32)
        self.gsc = s("m_gsc", [128, self.NA, 2], F32)
        self.cgi = s("m_cgi", [128, self.NA, 2], U32)
        self.iota16 = s("m_io16", [128, 16], F32)
        self.iota41 = s("m_io41", [128, self.NCOL], F32)
        self.ltri = s("m_ltri", [128, 128], BF16)
        self.ones128 = s("m_ones128", [128, 1], BF16)
        self.runoff = s("m_runoff", [1, 16], F32)
        self.ohk = s("m_ohk", [128, 2, 16], F32)
        self.ohsum_h = s("m_ohsum", [128, 16], BF16)
        self.posb = s("m_posb", [128, 16], F32)
        self.runb = s("m_runb", [128, 16], F32)
        self.xfm_f = s("m_xfmf", [128, DM // 128, 128], F32)
        self.identm = s("m_identm", [128, 128], F32)
        self.cnt_sb = s("m_cnt", [1, 16], F32)
        self.allcnt = s("m_allcnt", [8, 16], F32)
        self.msk = s("m_msk", [8, 1], F32)
        self.off16 = s("m_off16", [1, 16], F32)
        self.offb = s("m_offb", [128, 16], F32)
        self.recs = s("m_recs", [128, self.NCH], F32)
        self.evi = s("m_evi", [128, self.NCH], I32)
        self.evt = s("m_evt", [128, self.NCH], I32)
        self.ef = s("m_ef", [128, self.NCH], F32)
        self.pf = s("m_pf", [128, self.NCH], F32)
        self.colf = s("m_colf", [128, self.NCH], F32)
        self.rrow = s("m_rrow", [128, self.NCH], F32)
        self.recw = s("m_recw", [128, self.NREC], F32)
        self.io128r = s("m_io128r", [128, 128], F32)
        self.tidt = s("m_tid", [128, self.NCH], F32)
        self.oer = s("m_oer", [128, 2, 128], F32)
        self.oeb = s("m_oeb", [128, 2, self.NCOL], F32)
        self.tbl = s("m_tbl", [128, self.NCOL], F32)
        self.tbli = s("m_tbli", [128, self.NCOL], U32)
        self.xg_h = s("m_xgh", [128, max(self.GCH // 256, 1), DM], BF16)
        self.xg_l = s("m_xgl", [128, max(self.GCH // 256, 1), DM], BF16)
        self.xfm_h = s("m_xfmh", [128, 2, self.NDK, self.GCH], BF16)
        self.xfm_l = s("m_xfml", [128, 2, self.NDK, self.GCH], BF16)
        self.b32 = s("m_b32", [128, 3, 2 * self.NA], F32)
        self.bi32 = s("m_bi32", [128, 3, 2 * self.NA], I32)
        self.wsplit_f = s("m_wsf", [128, 2, FF], F32)
        self.wsplit_h = s("m_wsh", [128, 2, FF], BF16)
        self.wsplit_l = s("m_wsl", [128, 2, FF], BF16)
        self.ones1r = s("m_ones1r", [1, 128], F32)
        self.gw_h = s("m_gwh", [128, self.NDK, 2, 256], BF16)
        self.gw_l = s("m_gwl", [128, self.NDK, 2, 256], BF16)
        self.dw_h = s("m_dwh", [128, 2, DM], BF16)
        self.dw_l = s("m_dwl", [128, 2, DM], BF16)
        self.sg = s("m_sg", [128, self.GCH], F32)
        self.hh = s("m_hh", [128, self.NFB, self.GCH], BF16)
        self.hl = s("m_hl", [128, self.NFB, self.GCH], BF16)
        self.hf = s("m_hf", [128, self.GCH], F32)
        self.eo = s("m_eo", [128, self.GCH // 128, DM], F32)
        self.cmb = s("m_cmb", [128, 2, DM], F32)
        self.fin = s("m_fin", [128, 2, DM], F32)
        self.psm = [e(nc.psum_tensor(f"mps{i}", [128, 512], F32)) for i in range(8)]
        se = (semctx or ctx).enter_context
        c = lambda name: P.ctr(se(nc.semaphore(name)), name)
        self.c_ld = [c("m_ld0"), c("m_ld1")]
        self.c_gd = [c("m_gd0"), c("m_gd1")]
        self.c_v = c("m_v"); self.c_s = c("m_s"); self.c_t = c("m_t"); self.c_g = c("m_g")
        self.c_cc = c("m_cc")
        self.c_st = [c("m_st0"), c("m_st1")]
        self.c_ws = [c("m_ws0"), c("m_ws1")]
        self.c_fm = [c("m_fm0"), c("m_fm1")]
        self.c_wL = c("m_wL"); self.c_wG = c("m_wG"); self.c_wS = c("m_wS")
        self.c_mm = c("m_mm"); self.c_ev = c("m_ev"); self.c_evs = c("m_evs")
        self.c_out = c("m_out")
        self._gu_mm, self._dw_mm, self._eo_ev = {}, {}, {}
        self.GCHB = self.GCH // 128

    def dsplit(self, src_ap_rows, n_rows_chunks, hb, lb, wait_root):
        P = self.P
        for ch in range(n_rows_chunks):
            src, hdst, ldst = src_ap_rows(ch)
            par = ch % 2
            wld = [wait_root] if ch < 2 else [(self.c_st[par], self.c_st[par].n)]
            P.do("sync", lambda e, src=src, par=par: e.dma_start(out=self.wsplit_f[:, par, :src.shape[-1]], in_=src),
                 wait=wld, inc=self.c_ld[par], inc_k=16)
            ldn = (self.c_ld[par], self.c_ld[par].n)
            P.do("vector", lambda e, par=par, W=src.shape[-1]: e.tensor_copy(self.wsplit_h[:, par, 0:W], self.wsplit_f[:, par, 0:W]),
                 wait=[ldn, (self.c_v, self.c_v.n)], inc=self.c_v)
            P.do("vector", lambda e, par=par, W=src.shape[-1]: e.tensor_tensor(
                self.wsplit_l[:, par, 0:W], self.wsplit_f[:, par, 0:W], self.wsplit_h[:, par, 0:W], op=ALU.subtract),
                wait=[(self.c_v, self.c_v.n)], inc=self.c_v)
            vn = (self.c_v, self.c_v.n)
            P.do("gpsimd", lambda e, hdst=hdst, par=par, W=src.shape[-1]: e.dma_start(out=hdst, in_=self.wsplit_h[:, par, 0:W]),
                 wait=[vn], inc=self.c_st[par], inc_k=16)
            P.do("gpsimd", lambda e, ldst=ldst, par=par, W=src.shape[-1]: e.dma_start(out=ldst, in_=self.wsplit_l[:, par, 0:W]),
                 inc=self.c_st[par], inc_k=16)

    def emit_consts(self, io, attn_all):
        P = self.P
        W = [attn_all]
        P.do("vector", lambda e: e.memset(self.eps[:, :], 1e-5), wait=W, inc=self.c_v)
        P.do("vector", lambda e: e.memset(self.cm05[:, :], -0.5), inc=self.c_v)
        P.do("vector", lambda e: e.memset(self.c1e9[:, :], 1e-9), inc=self.c_v)
        P.do("vector", lambda e: e.memset(self.rinv[:, :], 1.0 / self.DM), inc=self.c_v)
        P.do("vector", lambda e: e.memset(self.mcm1[:, :], -1.0), inc=self.c_v)
        P.do("vector", lambda e: e.memset(self.runoff[:, :], 0.0), inc=self.c_v)
        P.do("gpsimd", lambda e: e.iota(self.identm[:, :], pattern=[[1, 128]], base=0, channel_multiplier=-1,
                                        allow_small_or_imprecise_dtypes=True), wait=W, inc=self.c_g)
        gn0 = (self.c_g, self.c_g.n)
        P.do("vector", lambda e: e.tensor_scalar(self.identm[:, :], self.identm[:, :], 0.0, None, op0=ALU.is_equal),
             wait=[gn0], inc=self.c_v)
        P.do("gpsimd", lambda e: e.iota(self.iota16[:, :], pattern=[[1, 16]], base=0, channel_multiplier=0,
                                        allow_small_or_imprecise_dtypes=True), inc=self.c_g)
        P.do("gpsimd", lambda e: e.iota(self.iota41[:, :], pattern=[[1, self.NCOL]], base=0, channel_multiplier=0,
                                        allow_small_or_imprecise_dtypes=True), inc=self.c_g)
        P.do("gpsimd", lambda e: e.iota(self.io128r[:, :], pattern=[[1, 128]], base=0, channel_multiplier=0,
                                        allow_small_or_imprecise_dtypes=True), inc=self.c_g)
        P.do("gpsimd", lambda e: e.iota(self.ltri[:, :], pattern=[[1, 128]], base=0, channel_multiplier=-1,
                                        allow_small_or_imprecise_dtypes=True), inc=self.c_g)
        gn = (self.c_g, self.c_g.n)
        P.do("vector", lambda e: e.tensor_scalar(self.ltri[:, :], self.ltri[:, :], 0.0, None, op0=ALU.is_gt),
             wait=[gn], inc=self.c_v)
        P.do("vector", lambda e: e.memset(self.ones128[:, :], 1.0), inc=self.c_v)
        P.do("vector", lambda e: e.memset(self.ones1r[:, :], 1.0), inc=self.c_v)
        P.do("gpsimd", lambda e: e.iota(self.tidt[:, :], pattern=[[128, self.NCH // 2], [0, 2]], base=0,
                                        channel_multiplier=1, allow_small_or_imprecise_dtypes=True),
             inc=self.c_g)
        P.do("sync", lambda e, m=io["msk"]: e.dma_start(out=self.msk[:, :], in_=m), wait=W, inc=self.c_ld[0], inc_k=16)
        P.do("sync", lambda e, m=io["e0in"]: e.dma_start(out=self.e0t[:, :], in_=m), inc=self.c_ld[0], inc_k=16)
        e0n = (self.c_ld[0], self.c_ld[0].n)
        rw = io["router_w"].rearrange("(a p) c -> p a c", p=128)
        P.do("sync", lambda e, rw=rw: e.dma_start(out=self.rw_f[:, :, :], in_=rw), inc=self.c_ld[0], inc_k=16)
        P.do("gpsimd", lambda e, rw=rw: e.dma_start(out=self.rw_h[:, :, :], in_=rw), wait=W, inc=self.c_gd[0], inc_k=16)
        rn = [(self.c_ld[0], self.c_ld[0].n), (self.c_gd[0], self.c_gd[0].n)]
        P.do("vector", lambda e: e.tensor_tensor(self.rw_l[:, :, :], self.rw_f[:, :, :], self.rw_h[:, :, :], op=ALU.subtract),
             wait=rn, inc=self.c_v)
        P.do("sync", lambda e, rb=io["router_b"]: e.dma_start(out=self.lg[0:1, :], in_=rb), inc=self.c_ld[0], inc_k=16)
        bn = (self.c_ld[0], self.c_ld[0].n)
        P.do("tensor", lambda e: e.matmul(self.psm[0][0:128, 0:16], self.ones1r[0:1, :], self.lg[0:1, :], start=True, stop=True),
             wait=[bn, attn_all, (self.c_v, self.c_v.n)], inc=self.c_t)
        tn = (self.c_t, self.c_t.n)
        P.do("vector", lambda e: e.tensor_copy(self.rb_b[:, :], self.psm[0][0:128, 0:16]), wait=[tn], inc=self.c_v)
        P.do("tensor", lambda e: e.matmul(self.psm[1][0:128, 0:1], self.ones1r[0:1, :], self.e0t[0:1, :], start=True, stop=True),
             wait=[e0n], inc=self.c_t)
        t2 = (self.c_t, self.c_t.n)
        P.do("vector", lambda e: e.tensor_copy(self.e0col[:, :], self.psm[1][0:128, 0:1]), wait=[t2], inc=self.c_v)
        self.consts_done = [(self.c_v, self.c_v.n), (self.c_g, self.c_g.n)]

    def emit_ag0_ln_router(self, io, attn_all):
        P, DM, NA = self.P, self.DM, self.NA
        groups = [[2 * i, 2 * i + 1] for i in range(self.NCORE // 2)]
        TBR = 2 * self.TC
        NCHK = 8
        rpc = TBR // NCHK
        tpc = rpc // 128
        ar_base = self.c_cc.n
        order = []
        for i in range(NCHK // 2):
            order += [i, NCHK // 2 + i]
        for ch in order:
            P.do("gpsimd", lambda e, ch=ch, i=io["part"], o=io["partsum"]: e.collective_compute(
                "AllReduce", mybir.AluOpType.add, replica_groups=groups,
                ins=[i[ch * rpc:(ch + 1) * rpc, :]], outs=[o[ch * rpc:(ch + 1) * rpc, :]]),
                wait=[attn_all], inc=self.c_cc)
        ar_tile = lambda a: (self.c_cc, ar_base + min(2 * (a // tpc) + 2, NCHK))
        P.do("sync", lambda e, h=io["hoff"]: e.dma_start(out=self.hofft[:, :], in_=h),
             wait=[attn_all], inc=self.c_ld[0], inc_k=16)
        hn = (self.c_ld[0], self.c_ld[0].n)
        regs = {}
        P.do("sync", lambda e: regs.__setitem__('h', e.value_load(self.hofft[0:1, 0:1])) or None,
             wait=[hn], inc=None)
        xsl = io["x_own"].rearrange("(a p) d -> p a d", p=128)
        p1 = io["partsum"].rearrange("(a p) d -> p a d", p=128)
        xlnf = io["xln_f"].rearrange("(a p) d -> p a d", p=128)
        xlh = io["xln_h"].rearrange("(a p) d -> p a d", p=128)
        xll = io["xln_l"].rearrange("(a p) d -> p a d", p=128)
        cd = self.consts_done
        self.ln_tile_marks = []
        sq_hist = {}
        for a in range(NA):
            par = a % 3
            xpar = a % 2
            spar = a % 2
            wld = [attn_all] if a < 2 else [sq_hist[a - 2]]
            P.do("sync", lambda e, a=a, xpar=xpar, xsl=xsl: e.dma_start(out=self.t_x[:, xpar, :], in_=xsl[:, a, :]),
                 wait=wld + [ar_tile(a)], inc=self.c_ld[spar], inc_k=16)
            P.do("sync", lambda e, a=a, xpar=xpar, p1=p1: e.dma_start(
                out=self.t_p1[:, xpar, :], in_=p1[:, bass.ds(regs['h'] + a, 1), :].rearrange("p a d -> p (a d)")),
                inc=self.c_ld[spar], inc_k=16)
            ldn = [(self.c_ld[spar], self.c_ld[spar].n)]
            slotw = [] if a < 3 else [self.ln_tile_marks[a - 3]]
            P.do("vector", lambda e, par=par, xpar=xpar: e.tensor_tensor(self.t_ln[:, par, :], self.t_x[:, xpar, :], self.t_p1[:, xpar, :], op=ALU.add),
                 wait=ldn + cd + slotw + [(self.c_v, self.c_v.n)], inc=self.c_v)
            sn = (self.c_v, self.c_v.n)
            P.do("vector", lambda e, par=par: e.reduce_sum(self.stat[:, 0:1], self.t_ln[:, par, :], axis=AX.X),
                 wait=[sn], inc=self.c_v)
            P.do("scalar", lambda e, par=par, xpar=xpar: e.activation(self.t_x[:, xpar, :], self.t_ln[:, par, :], AF.Square, accum_out=self.stat[:, 1:2]),
                 wait=[sn, (self.c_s, self.c_s.n)], inc=self.c_s)
            sqn = (self.c_s, self.c_s.n)
            sq_hist[a] = sqn
            P.do("vector", lambda e: e.tensor_scalar(self.stat[:, 2:3], self.stat[:, 0:1], self.rinv[:, :], None, op0=ALU.mult),
                 wait=[(self.c_v, self.c_v.n)], inc=self.c_v)
            P.do("vector", lambda e: e.tensor_scalar(self.stat[:, 3:4], self.stat[:, 1:2], self.rinv[:, :], None, op0=ALU.mult),
                 wait=[sqn, (self.c_v, self.c_v.n)], inc=self.c_v)
            P.do("vector", lambda e: e.tensor_tensor(self.stat[:, 4:5], self.stat[:, 2:3], self.stat[:, 2:3], op=ALU.mult),
                 wait=[(self.c_v, self.c_v.n)], inc=self.c_v)
            P.do("vector", lambda e: e.tensor_tensor(self.stat[:, 4:5], self.stat[:, 3:4], self.stat[:, 4:5], op=ALU.subtract),
                 wait=[(self.c_v, self.c_v.n)], inc=self.c_v)
            vn = (self.c_v, self.c_v.n)
            P.do("scalar", lambda e: e.activation(self.stat[:, 5:6], self.stat[:, 4:5], AF.Ln, bias=self.eps[:, :]),
                 wait=[vn, (self.c_s, self.c_s.n)], inc=self.c_s)
            P.do("scalar", lambda e: e.activation(self.stat[:, 5:6], self.stat[:, 5:6], AF.Exp, scale=self.cm05[:, :]),
                 wait=[(self.c_s, self.c_s.n)], inc=self.c_s)
            rsn = (self.c_s, self.c_s.n)
            P.do("vector", lambda e, par=par: e.tensor_scalar(self.t_ln[:, par, :], self.t_ln[:, par, :], self.stat[:, 2:3], self.stat[:, 5:6], op0=ALU.subtract, op1=ALU.mult),
                 wait=[rsn, (self.c_v, self.c_v.n)], inc=self.c_v)
            lnn = (self.c_v, self.c_v.n)
            P.do("vector", lambda e, par=par: e.tensor_copy(self.t_lh[:, par, :], self.t_ln[:, par, :]),
                 wait=[lnn], inc=self.c_v)
            P.do("vector", lambda e, par=par: e.tensor_tensor(self.t_ll[:, par, :], self.t_ln[:, par, :], self.t_lh[:, par, :], op=ALU.subtract),
                 wait=[(self.c_v, self.c_v.n)], inc=self.c_v)
            hln = (self.c_v, self.c_v.n)
            P.do("gpsimd", lambda e, a=a, par=par, xlnf=xlnf: e.dma_start(out=xlnf[:, a, :], in_=self.t_ln[:, par, :]),
                 wait=[hln], inc=self.c_st[spar], inc_k=16)
            P.do("gpsimd", lambda e, a=a, par=par, xlh=xlh: e.dma_start(out=xlh[:, a, :], in_=self.t_lh[:, par, :]),
                 inc=self.c_st[spar], inc_k=16)
            P.do("gpsimd", lambda e, a=a, par=par, xll=xll: e.dma_start(out=xll[:, a, :], in_=self.t_ll[:, par, :]),
                 inc=self.c_st[spar], inc_k=16)
            self.ln_tile_marks.append((self.c_st[spar], self.c_st[spar].n))
            for dk in range(self.NDK):
                P.do("tensor", lambda e, dk=dk, par=par: e.transpose(
                    self.psm[6][:, 0:128], self.t_ln[:, par, 128 * dk:128 * (dk + 1)], self.identm[:, :]),
                    wait=[lnn, (self.c_v, self.c_v.n), (self.c_mm, self.c_mm.n)], inc=self.c_t)
                ttn = (self.c_t, self.c_t.n)
                P.do("vector", lambda e, dk=dk: e.tensor_copy(self.xfm_f[:, dk, :], self.psm[6][:, 0:128]),
                     wait=[ttn], inc=self.c_v)
            fmn = (self.c_v, self.c_v.n)
            for dk in range(self.NDK):
                P.do("tensor", lambda e, dk=dk: e.matmul(
                    self.psm[7][:, 0:16], self.xfm_f[:, dk, :], self.rw_f[:, dk, :],
                    start=(dk == 0), stop=(dk == self.NDK - 1)),
                    wait=([fmn] if dk == 0 else []), inc=(self.c_mm if dk == self.NDK - 1 else None))
            mmn = (self.c_mm, self.c_mm.n)
            P.do("vector", lambda e: e.tensor_tensor(self.lg[:, :], self.psm[7][:, 0:16], self.rb_b[:, :], op=ALU.add),
                 wait=[mmn, (self.c_v, self.c_v.n)], inc=self.c_v)
            lgn = (self.c_v, self.c_v.n)
            P.do("vector", lambda e: e.reduce_max(self.stat[:, 6:7], self.lg[:, :], axis=AX.X),
                 wait=[lgn], inc=self.c_v)
            P.do("vector", lambda e: e.tensor_scalar(self.stat[:, 7:8], self.stat[:, 6:7], -1.0, None, op0=ALU.mult),
                 wait=[(self.c_v, self.c_v.n)], inc=self.c_v)
            mxn = (self.c_v, self.c_v.n)
            P.do("scalar", lambda e: e.activation(self.prob[:, :], self.lg[:, :], AF.Exp, bias=self.stat[:, 7:8], accum_out=self.stat[:, 6:7]),
                 wait=[mxn, (self.c_s, self.c_s.n)], inc=self.c_s)
            expn = (self.c_s, self.c_s.n)
            self.srecip(self.stat[:, 7:8], self.stat[:, 6:7], self.c_s, wait=[expn])
            rsn2 = (self.c_s, self.c_s.n)
            P.do("vector", lambda e: e.max(self.m8[:, :], self.prob[:, :]), wait=[expn, (self.c_v, self.c_v.n)], inc=self.c_v)
            P.do("vector", lambda e: e.drain(), wait=[], inc=self.c_v)
            P.do("vector", lambda e: e.max_index(self.i8[:, :], self.m8[:, :], self.prob[:, :]),
                 wait=[(self.c_v, self.c_v.n)], inc=self.c_v)
            tpn = (self.c_v, self.c_v.n)
            P.do("vector", lambda e, a=a: e.tensor_scalar(self.valn[:, a, :], self.m8[:, 0:2], self.stat[:, 7:8], None, op0=ALU.mult),
                 wait=[tpn, rsn2], inc=self.c_v)
            P.do("vector", lambda e, a=a: e.tensor_tensor(self.stat[:, 6:7], self.valn[:, a, 0:1], self.valn[:, a, 1:2], op=ALU.add),
                 wait=[(self.c_v, self.c_v.n)], inc=self.c_v)
            P.do("vector", lambda e: e.tensor_tensor(self.stat[:, 6:7], self.stat[:, 6:7], self.c1e9[:, :], op=ALU.add),
                 wait=[(self.c_v, self.c_v.n)], inc=self.c_v)
            dnn = (self.c_v, self.c_v.n)
            self.srecip(self.stat[:, 6:7], self.stat[:, 6:7], self.c_s, wait=[dnn, (self.c_s, self.c_s.n)])
            rdn = (self.c_s, self.c_s.n)
            P.do("vector", lambda e, a=a: e.tensor_scalar(self.valn[:, a, :], self.valn[:, a, :], self.stat[:, 6:7], None, op0=ALU.mult),
                 wait=[rdn, (self.c_v, self.c_v.n)], inc=self.c_v)
            P.do("vector", lambda e, a=a: e.tensor_copy(self.idxf[:, a, :], self.i8[:, 0:2]),
                 wait=[(self.c_v, self.c_v.n)], inc=self.c_v)
            for k in range(2):
                P.do("vector", lambda e, a=a, k=k: e.tensor_scalar(
                    self.ohk[:, k, :], self.iota16[:, :], self.idxf[:, a, k:k + 1], None, op0=ALU.is_equal),
                    wait=[(self.c_v, self.c_v.n)], inc=self.c_v)
            P.do("vector", lambda e: e.tensor_tensor(self.ohsum_h[:, :], self.ohk[:, 0, :], self.ohk[:, 1, :], op=ALU.add),
                 wait=[(self.c_v, self.c_v.n)], inc=self.c_v)
            ohn = (self.c_v, self.c_v.n)
            P.do("tensor", lambda e: e.matmul(self.psm[7][:, 0:16], self.ltri[:, :], self.ohsum_h[:, :], start=True, stop=True),
                 wait=[ohn, (self.c_mm, self.c_mm.n)], inc=self.c_mm)
            P.do("tensor", lambda e: e.matmul(self.psm[6][0:1, 0:16], self.ones128[:, :], self.ohsum_h[:, :], start=True, stop=True),
                 inc=self.c_mm)
            exn = (self.c_mm, self.c_mm.n)
            P.do("tensor", lambda e: e.matmul(self.psm[5][:, 0:16], self.ones1r[0:1, :], self.runoff[0:1, :], start=True, stop=True),
                 wait=[(self.c_v, self.c_v.n)], inc=self.c_mm)
            bcn = (self.c_mm, self.c_mm.n)
            P.do("vector", lambda e: e.tensor_copy(self.runb[:, :], self.psm[5][:, 0:16]),
                 wait=[bcn, (self.c_v, self.c_v.n)], inc=self.c_v)
            P.do("vector", lambda e: e.tensor_tensor(self.posb[:, :], self.psm[7][:, 0:16], self.runb[:, :], op=ALU.add),
                 wait=[exn, (self.c_v, self.c_v.n)], inc=self.c_v)
            pbn = (self.c_v, self.c_v.n)
            for k in range(2):
                P.do("vector", lambda e, k=k: e.tensor_tensor(self.prob[:, :], self.ohk[:, k, :], self.posb[:, :], op=ALU.mult),
                     wait=[pbn, (self.c_v, self.c_v.n)], inc=self.c_v)
                P.do("vector", lambda e, a=a, k=k: e.reduce_sum(self.posl[:, a, k:k + 1], self.prob[:, :], axis=AX.X),
                     wait=[(self.c_v, self.c_v.n)], inc=self.c_v)
            P.do("vector", lambda e: e.tensor_tensor(self.runoff[0:1, :], self.runoff[0:1, :], self.psm[6][0:1, 0:16], op=ALU.add),
                 wait=[(self.c_v, self.c_v.n)], inc=self.c_v)
        self.ln_done = [(self.c_v, self.c_v.n), (self.c_s, self.c_s.n),
                        (self.c_st[0], self.c_st[0].n), (self.c_st[1], self.c_st[1].n)]

    def emit_gates_records(self, io):
        P, NA, CAP, E = self.P, self.NA, self.CAP, self.E
        ln = self.ln_done
        P.do("gpsimd", lambda e, c=io["cnt_d"]: e.dma_start(out=c, in_=self.runoff[0:1, :]),
             wait=ln, inc=self.c_gd[0], inc_k=16)
        P.do("gpsimd", lambda e, i=io["cnt_d"], o=io["agc"]: e.collective_compute(
            "AllGather", mybir.AluOpType.bypass, replica_groups=[list(range(self.NCORE))], ins=[i], outs=[o]),
            wait=[(self.c_gd[0], self.c_gd[0].n)], inc=self.c_cc)
        agcn = (self.c_cc, self.c_cc.n)
        cnt_ap = io["agc"].rearrange("(r f) -> r f", r=self.NCORE)
        P.do("sync", lambda e, cnt_ap=cnt_ap: e.dma_start(out=self.allcnt[0:self.NCORE, :], in_=cnt_ap),
             wait=[agcn], inc=self.c_ld[0], inc_k=16)
        cn = (self.c_ld[0], self.c_ld[0].n)
        P.do("tensor", lambda e: e.matmul(self.psm[4][0:1, 0:16], self.msk[0:self.NCORE, :], self.allcnt[0:self.NCORE, :], start=True, stop=True),
             wait=[cn, (self.c_mm, self.c_mm.n)], inc=self.c_mm)
        on = (self.c_mm, self.c_mm.n)
        P.do("vector", lambda e: e.tensor_copy(self.off16[0:1, :], self.psm[4][0:1, 0:16]),
             wait=[on, (self.c_v, self.c_v.n)], inc=self.c_v)
        o2 = (self.c_v, self.c_v.n)
        P.do("tensor", lambda e: e.matmul(self.psm[4][:, 0:16], self.ones1r[0:1, :], self.off16[0:1, :], start=True, stop=True),
             wait=[o2], inc=self.c_mm)
        ob = (self.c_mm, self.c_mm.n)
        P.do("vector", lambda e: e.tensor_copy(self.offb[:, :], self.psm[4][:, 0:16]),
             wait=[ob, (self.c_v, self.c_v.n)], inc=self.c_v)
        obn = (self.c_v, self.c_v.n)
        P.do("gpsimd", lambda e, i=io["xln_h"], o=io["ag1h"]: e.collective_compute(
            "AllGather", mybir.AluOpType.bypass, replica_groups=[list(range(self.NCORE))], ins=[i], outs=[o]),
            wait=ln, inc=self.c_cc)
        P.do("gpsimd", lambda e, i=io["xln_l"], o=io["ag1l"]: e.collective_compute(
            "AllGather", mybir.AluOpType.bypass, replica_groups=[list(range(self.NCORE))], ins=[i], outs=[o]),
            inc=self.c_cc)
        for a in range(NA):
            for k in range(2):
                P.do("vector", lambda e, a=a, k=k: e.tensor_scalar(
                    self.ohk[:, k, :], self.iota16[:, :], self.idxf[:, a, k:k + 1], None, op0=ALU.is_equal),
                    wait=[obn, (self.c_v, self.c_v.n)], inc=self.c_v)
                P.do("vector", lambda e, k=k: e.tensor_tensor(self.prob[:, :], self.ohk[:, k, :], self.offb[:, :], op=ALU.mult),
                     wait=[(self.c_v, self.c_v.n)], inc=self.c_v)
                P.do("vector", lambda e: e.reduce_sum(self.stat[:, 0:1], self.prob[:, :], axis=AX.X),
                     wait=[(self.c_v, self.c_v.n)], inc=self.c_v)
                P.do("vector", lambda e, a=a, k=k: e.tensor_tensor(self.posl[:, a, k:k + 1], self.posl[:, a, k:k + 1], self.stat[:, 0:1], op=ALU.add),
                     wait=[(self.c_v, self.c_v.n)], inc=self.c_v)
        GCH, R = self.GCH, self.NCORE
        NCHE = CAP // GCH
        ZROW = float(R * 2 * CAP)
        fl = lambda t: t.ap().rearrange("p a k -> p (a k)")
        idf, psf, vlf = fl(self.idxf), fl(self.posl), fl(self.valn)
        gsf, cgf = fl(self.gsc), fl(self.cgi)
        b0, b1, b2 = self.b32[:, 0, :], self.b32[:, 1, :], self.b32[:, 2, :]
        i0, i1 = self.bi32[:, 0, :], self.bi32[:, 1, :]
        shG = GCH.bit_length() - 1
        D = lambda fn, w=None: P.do("vector", fn, wait=([(self.c_v, self.c_v.n)] if w is None else w), inc=self.c_v)
        D(lambda e: e.tensor_scalar(b0, psf, float(CAP), None, op0=ALU.is_lt))
        D(lambda e: e.tensor_tensor(gsf, vlf, b0, op=ALU.mult))
        D(lambda e: e.tensor_copy(i0, idf))
        D(lambda e: e.tensor_scalar(i1, i0, 1, None, op0=ALU.bitwise_and))
        D(lambda e: e.tensor_copy(b1, i1))
        D(lambda e: e.scalar_tensor_tensor(b1, b1, float(NCHE * R * GCH), psf, op0=ALU.mult, op1=ALU.add))
        D(lambda e: e.tensor_scalar(i1, i0, 1, None, op0=ALU.arith_shift_right))
        D(lambda e: e.tensor_copy(b2, i1))
        D(lambda e: e.scalar_tensor_tensor(b2, b2, float(GCH), b1, op0=ALU.mult, op1=ALU.add))
        D(lambda e: e.tensor_copy(i1, psf))
        D(lambda e: e.tensor_scalar(i1, i1, shG, None, op0=ALU.arith_shift_right))
        D(lambda e: e.tensor_copy(b1, i1))
        D(lambda e: e.scalar_tensor_tensor(b1, b1, float((R - 1) * GCH), b2, op0=ALU.mult, op1=ALU.add))
        D(lambda e: e.tensor_tensor(b2, b1, b0, op=ALU.mult))
        D(lambda e: e.tensor_scalar(b1, b0, 1.0, -ZROW, op0=ALU.subtract, op1=ALU.mult))
        D(lambda e: e.tensor_tensor(b2, b2, b1, op=ALU.add))
        D(lambda e: e.tensor_copy(cgf, b2))
        D(lambda e: e.scalar_tensor_tensor(b1, idf, 32768.0, psf, op0=ALU.mult, op1=ALU.add))
        D(lambda e: e.tensor_tensor(b1, b1, b0, op=ALU.mult))
        D(lambda e: e.tensor_scalar(b2, b0, 1.0, -(31.0 * 32768.0), op0=ALU.subtract, op1=ALU.mult))
        D(lambda e: e.tensor_tensor(self.recw[:, :], b1, b2, op=ALU.add))
        self.gates_done = (self.c_v, self.c_v.n)
        recv = io["rec_d"].rearrange("(p j) -> p j", p=128)
        P.do("gpsimd", lambda e, recv=recv: e.dma_start(out=recv, in_=self.recw[:, :]),
             wait=[self.gates_done], inc=self.c_gd[0], inc_k=16)
        pn = (self.c_gd[0], self.c_gd[0].n)
        P.do("gpsimd", lambda e, i=io["rec_d"], o=io["ag1r"]: e.collective_compute(
            "AllGather", mybir.AluOpType.bypass, replica_groups=[list(range(self.NCORE))], ins=[i], outs=[o]),
            wait=[pn], inc=self.c_cc)
        self.ag1_done = (self.c_cc, self.c_cc.n)

    def emit_table(self, io):
        P, CAP, E, NR = self.P, self.CAP, self.E, self.NCORE
        NB = self.NB
        rv = io["ag1r"].rearrange("(r p j) -> p r j", p=128, j=self.NREC)
        P.do("sync", lambda e, rv=rv: e.dma_start(
            out=self.recs.ap().rearrange("p (r j) -> p r j", j=self.NREC), in_=rv),
            wait=[self.ag1_done], inc=self.c_ld[1], inc_k=16)
        rn = (self.c_ld[1], self.c_ld[1].n)
        RJ = self.NCH
        V = lambda n: (self.c_v, self.c_v.n)
        P.do("vector", lambda e, RJ=RJ: e.tensor_copy(self.evi[:, 0:RJ], self.recs[:, 0:RJ]),
             wait=[rn, self.gates_done, (self.c_v, self.c_v.n)], inc=self.c_v)
        P.do("vector", lambda e, RJ=RJ: e.tensor_scalar(self.evt[:, 0:RJ], self.evi[:, 0:RJ], 15, None, op0=ALU.arith_shift_right),
             wait=[V(0)], inc=self.c_v)
        P.do("vector", lambda e, RJ=RJ: e.tensor_copy(self.ef[:, 0:RJ], self.evt[:, 0:RJ]),
             wait=[V(0)], inc=self.c_v)
        P.do("vector", lambda e, RJ=RJ: e.tensor_scalar(self.evt[:, 0:RJ], self.evi[:, 0:RJ], 32767, None, op0=ALU.bitwise_and),
             wait=[V(0)], inc=self.c_v)
        P.do("vector", lambda e, RJ=RJ: e.tensor_copy(self.pf[:, 0:RJ], self.evt[:, 0:RJ]),
             wait=[V(0)], inc=self.c_v)
        P.do("vector", lambda e, RJ=RJ: e.tensor_scalar(self.ef[:, 0:RJ], self.ef[:, 0:RJ], self.e0col[:, :], None, op0=ALU.subtract),
             wait=[V(0)], inc=self.c_v)
        P.do("vector", lambda e, RJ=RJ: e.tensor_scalar(self.colf[:, 0:RJ], self.ef[:, 0:RJ], 0.0, None, op0=ALU.is_ge),
             wait=[V(0)], inc=self.c_v)
        P.do("vector", lambda e, RJ=RJ: e.tensor_scalar(self.rrow[:, 0:RJ], self.ef[:, 0:RJ], 1.0, None, op0=ALU.is_le),
             wait=[V(0)], inc=self.c_v)
        P.do("vector", lambda e, RJ=RJ: e.tensor_tensor(self.colf[:, 0:RJ], self.colf[:, 0:RJ], self.rrow[:, 0:RJ], op=ALU.mult),
             wait=[V(0)], inc=self.c_v)
        inrn = (self.c_v, self.c_v.n)
        P.do("vector", lambda e, RJ=RJ: e.tensor_scalar(self.evt[:, 0:RJ], self.evt[:, 0:RJ], 7, None, op0=ALU.arith_shift_right),
             wait=[V(0)], inc=self.c_v)
        P.do("vector", lambda e, RJ=RJ: e.tensor_copy(self.rrow[:, 0:RJ], self.evt[:, 0:RJ]),
             wait=[V(0)], inc=self.c_v)
        P.do("vector", lambda e, RJ=RJ, NB=NB: e.scalar_tensor_tensor(
            self.ef[:, 0:RJ], self.ef[:, 0:RJ], float(NB), self.rrow[:, 0:RJ], op0=ALU.mult, op1=ALU.add),
            wait=[V(0)], inc=self.c_v)
        P.do("vector", lambda e, RJ=RJ: e.tensor_tensor(self.ef[:, 0:RJ], self.ef[:, 0:RJ], self.colf[:, 0:RJ], op=ALU.mult),
             wait=[V(0)], inc=self.c_v)
        P.do("vector", lambda e, RJ=RJ, NB=NB: e.tensor_scalar(self.colf[:, 0:RJ], self.colf[:, 0:RJ], 1.0, -2.0 * NB, op0=ALU.subtract, op1=ALU.mult),
             wait=[V(0)], inc=self.c_v)
        P.do("vector", lambda e, RJ=RJ: e.tensor_tensor(self.colf[:, 0:RJ], self.colf[:, 0:RJ], self.ef[:, 0:RJ], op=ALU.add),
             wait=[V(0)], inc=self.c_v)
        P.do("vector", lambda e, RJ=RJ: e.tensor_copy(self.evi[:, 0:RJ], self.pf[:, 0:RJ]),
             wait=[V(0)], inc=self.c_v)
        P.do("vector", lambda e, RJ=RJ: e.tensor_scalar(self.evi[:, 0:RJ], self.evi[:, 0:RJ], 127, None, op0=ALU.bitwise_and),
             wait=[V(0)], inc=self.c_v)
        P.do("vector", lambda e, RJ=RJ: e.tensor_copy(self.rrow[:, 0:RJ], self.evi[:, 0:RJ]),
             wait=[V(0)], inc=self.c_v)
        decn = (self.c_v, self.c_v.n)
        mm_after = {}
        for j in range(RJ):
            jp = j % 2
            w = [decn, (self.c_g, self.c_g.n), (self.c_ev, self.c_ev.n)] if j == 0 else []
            if j >= 2:
                w = [(self.c_mm, mm_after[j - 2])]
            P.do("vector", lambda e, j=j, jp=jp: e.tensor_scalar(self.oer[:, jp, :], self.io128r[:, :], self.rrow[:, j:j + 1], None, op0=ALU.is_equal),
                 wait=w + [(self.c_v, self.c_v.n)], inc=self.c_v)
            P.do("vector", lambda e, j=j, jp=jp: e.tensor_scalar(self.oeb[:, jp, :], self.iota41[:, :], self.colf[:, j:j + 1], None, op0=ALU.is_equal),
                 wait=[(self.c_v, self.c_v.n)], inc=self.c_v)
            P.do("vector", lambda e, j=j, jp=jp: e.tensor_scalar(self.oeb[:, jp, :], self.oeb[:, jp, :], self.tidt[:, j:j + 1], None, op0=ALU.mult),
                 wait=[(self.c_v, self.c_v.n)], inc=self.c_v)
            vn = (self.c_v, self.c_v.n)
            P.do("tensor", lambda e, j=j, jp=jp, RJ=RJ: e.matmul(self.psm[7][:, 0:self.NCOL], self.oer[:, jp, :], self.oeb[:, jp, :],
                                                                start=(j == 0), stop=(j == RJ - 1)),
                 wait=[vn], inc=self.c_mm)
            mm_after[j] = self.c_mm.n
        tn = (self.c_mm, self.c_mm.n)
        P.do("vector", lambda e: e.tensor_copy(self.tbl[:, :], self.psm[7][:, 0:self.NCOL]),
             wait=[tn, (self.c_v, self.c_v.n)], inc=self.c_v)
        P.do("vector", lambda e: e.tensor_copy(self.tbli[:, :], self.tbl[:, :]),
             wait=[(self.c_v, self.c_v.n)], inc=self.c_v)
        self.table_done = (self.c_v, self.c_v.n)

    def srecip(self, out, in_, ctrS, wait=(), nP=128):
        P = self.P
        P.do("scalar", lambda e: e.activation(out, in_, AF.Ln), wait=list(wait), inc=ctrS)
        P.do("scalar", lambda e, nP=nP: e.activation(out, out, AF.Exp, scale=self.mcm1[0:nP, :]),
             wait=[(ctrS, ctrS.n)], inc=ctrS)

    def dsplit_one(self, srcf, hid, lod, wait_root, fast=False):
        P = self.P
        R, Wd = srcf.shape
        sv = srcf.rearrange("(c p) w -> p c w", p=128)
        hv = hid.rearrange("(c p) w -> p c w", p=128)
        lv = lod.rearrange("(c p) w -> p c w", p=128)
        seng, stc = ("sync", self.c_ws) if fast else ("gpsimd", self.c_st)
        ceng, cctr = ("vector", self.c_v) if fast else ("gpsimd", self.c_g)
        for ch in range(R // 128):
            for w0 in range(0, Wd, self.FF):
                ww = min(self.FF, Wd - w0)
                k = self._ws_k
                self._ws_k += 1
                fpar = k % 2
                par = k % 2
                wld = [wait_root] if k - 2 not in self._ws_vhist else [self._ws_vhist[k - 2]]
                P.do("sync", lambda e, sv=sv, ch=ch, fpar=fpar, w0=w0, ww=ww: e.dma_start(
                    out=self.wsplit_f[:, fpar, 0:ww], in_=sv[:, ch, w0:w0 + ww]),
                    wait=wld, inc=self.c_ld[fpar], inc_k=16)
                ldn = (self.c_ld[fpar], self.c_ld[fpar].n)
                P.do(ceng, lambda e, par=par, fpar=fpar, ww=ww: e.tensor_copy(self.wsplit_h[:, par, 0:ww], self.wsplit_f[:, fpar, 0:ww]),
                     wait=[ldn, (cctr, cctr.n)] + ([self._ws_hist[k - 2]] if k - 2 in self._ws_hist else []), inc=cctr)
                P.do(ceng, lambda e, par=par, fpar=fpar, ww=ww: e.tensor_tensor(self.wsplit_l[:, par, 0:ww], self.wsplit_f[:, fpar, 0:ww], self.wsplit_h[:, par, 0:ww], op=ALU.subtract),
                     wait=[(cctr, cctr.n)], inc=cctr)
                vn = (cctr, cctr.n)
                self._ws_vhist[k] = vn
                P.do(seng, lambda e, hv=hv, ch=ch, par=par, w0=w0, ww=ww: e.dma_start(out=hv[:, ch, w0:w0 + ww], in_=self.wsplit_h[:, par, 0:ww]),
                     wait=[vn], inc=stc[par % 2], inc_k=16)
                P.do(seng, lambda e, lv=lv, ch=ch, par=par, w0=w0, ww=ww: e.dma_start(out=lv[:, ch, w0:w0 + ww], in_=self.wsplit_l[:, par, 0:ww]),
                     inc=stc[par % 2], inc_k=16)
                self._ws_hist[k] = (stc[par % 2], stc[par % 2].n)

    def emit_wsplits(self, wsplits, wait_root, fast=False):
        stc = self.c_ws if fast else self.c_st
        self._ws_k = getattr(self, "_ws_k", 0)
        self._ws_hist = getattr(self, "_ws_hist", {})
        self._ws_vhist = getattr(self, "_ws_vhist", {})
        for (srcf, hid, lod) in wsplits:
            self.dsplit_one(srcf, hid, lod, wait_root, fast)
        return [(stc[0], stc[0].n), (stc[1], stc[1].n)]

    def emit_mlp(self, tag, nchunks, load_x, wgu_h, wgu_l, wd_hd, wd_ld, sink, extra_root, wsdone, next_load=None):
        P, DM, FF, GCH = self.P, self.DM, self.FF, self.GCH
        NDK, NFB = self.NDK, self.NFB
        guv = wgu_h.rearrange("(c p) w -> p c w", p=128)
        gulv = wgu_l.rearrange("(c p) w -> p c w", p=128)
        dhv = wd_hd.rearrange("(c p) d -> p c d", p=128)
        dlv = wd_ld.rearrange("(c p) d -> p c d", p=128)
        prev_gu_done = getattr(self, "_mlp_gu_done", 0)
        prev_eo_done = getattr(self, "_mlp_eo_done", [(self.c_st[0], 0), (self.c_st[1], 0)])
        self._eo_prev_n = getattr(self, "_eo_prev_n", 0)
        self._gu_hist = getattr(self, "_gu_hist", {})
        self._pidx = getattr(self, "_pidx", 0)
        self._xw_pref = getattr(self, "_xw_pref", {})
        for ci in range(nchunks):
            px = self._pidx % 2
            if self._pidx in self._xw_pref:
                xw = self._xw_pref.pop(self._pidx)
            else:
                xw = load_x(ci, [(self.c_mm, self._gu_hist.get(self._pidx - 2, 0))] + list(extra_root), px)
            split_n = {}
            silu_n = {}
            for fb in range(NFB):
                wpar = fb % 2
                wld = (list(wsdone) + [(self.c_mm, prev_gu_done)]) if (ci == 0 and fb < 2) else [(self.c_mm, self._gu_mm.get((tag, ci, fb - 2), prev_gu_done))]
                P.do("sync", lambda e, fb=fb, wpar=wpar: e.dma_start(out=self.gw_h[:, :, wpar, :], in_=guv[:, :, 256 * fb:256 * (fb + 1)]),
                     wait=wld, inc=self.c_ld[wpar], inc_k=16)
                P.do("sync", lambda e, fb=fb, wpar=wpar: e.dma_start(out=self.gw_l[:, :, wpar, :], in_=gulv[:, :, 256 * fb:256 * (fb + 1)]),
                     wait=wld, inc=self.c_ld[wpar], inc_k=16)
                wn = [(self.c_ld[wpar], self.c_ld[wpar].n)]
                psg, psu = self.psm[2 * wpar], self.psm[2 * wpar + 1]
                pw = []
                if (tag, ci, fb - 2) in silu_n:
                    pw = silu_n[(tag, ci, fb - 2)]
                elif fb < 2:
                    pw = list(getattr(self, "_mlp_last_silu", []))
                    pw.append((self.c_ev, self._eo_prev_n))
                for dk in range(NDK):
                    for pi in range(3):
                        wt = self.gw_h if pi < 2 else self.gw_l
                        xt = self.xfm_h if pi != 1 else self.xfm_l
                        first = (dk == 0 and pi == 0)
                        P.do("tensor", lambda e, wt=wt, xt=xt, dk=dk, pi=pi, wpar=wpar, psg=psg, px=px: e.matmul(
                            psg[:, 0:self.GCH], wt[:, dk, wpar, 0:128], xt[:, px, dk, :],
                            start=(dk == 0 and pi == 0), stop=(dk == NDK - 1 and pi == 2)),
                            wait=(wn + xw + pw if first else []), inc=None)
                        P.do("tensor", lambda e, wt=wt, xt=xt, dk=dk, pi=pi, wpar=wpar, psu=psu, px=px: e.matmul(
                            psu[:, 0:self.GCH], wt[:, dk, wpar, 128:256], xt[:, px, dk, :],
                            start=(dk == 0 and pi == 0), stop=(dk == NDK - 1 and pi == 2)),
                            wait=[], inc=(self.c_mm if (dk == NDK - 1 and pi == 2) else None))
                self._gu_mm[(tag, ci, fb)] = self.c_mm.n
                gun = (self.c_mm, self.c_mm.n)
                wsl = [gun, (self.c_s, self.c_s.n)]
                if (tag, ci, fb - 1) in silu_n:
                    wsl.append(silu_n[(tag, ci, fb - 1)][1])
                if fb == 0:
                    wsl.append((self.c_mm, getattr(self, "_down_prev", 0)))
                    wsl += list(getattr(self, "_mlp_last_silu", []))
                P.do("scalar", lambda e, psg=psg: e.activation(self.sg[:, :], psg[:, 0:self.GCH], AF.Silu),
                     wait=wsl, inc=self.c_s)
                sgn = (self.c_s, self.c_s.n)
                P.do("vector", lambda e, psu=psu: e.tensor_tensor(self.hf[:, :], self.sg[:, :], psu[:, 0:self.GCH], op=ALU.mult),
                     wait=[sgn, (self.c_v, self.c_v.n)], inc=self.c_v)
                P.do("vector", lambda e, fb=fb: e.tensor_copy(self.hh[:, fb, :], self.hf[:, :]),
                     wait=[(self.c_v, self.c_v.n)], inc=self.c_v)
                P.do("vector", lambda e, fb=fb: e.tensor_tensor(self.hl[:, fb, :], self.hf[:, :], self.hh[:, fb, :], op=ALU.subtract),
                     wait=[(self.c_v, self.c_v.n)], inc=self.c_v)
                split_n[fb] = (self.c_v, self.c_v.n)
                silu_n[(tag, ci, fb)] = [sgn, (self.c_v, self.c_v.n)]
            self._mlp_last_silu = silu_n[(tag, ci, NFB - 2)] + silu_n[(tag, ci, NFB - 1)]
            gu_done_ci = self.c_mm.n
            halves = [(hi, h0, min(512, DM - h0)) for hi, h0 in enumerate(range(0, DM, 512))]
            nh = len(halves)
            for fb in range(NFB):
                dpar = fb % 2
                wldd = [split_n[NFB - 1], (self.c_mm, getattr(self, "_down_prev", 0))] if fb < 2 else [(self.c_mm, self._dw_mm.get((tag, ci, fb - 2), 0))]
                P.do("sync", lambda e, fb=fb, dpar=dpar: e.dma_start(out=self.dw_h[:, dpar, :], in_=dhv[:, fb, :]),
                     wait=wldd, inc=self.c_ld[dpar], inc_k=16)
                P.do("sync", lambda e, fb=fb, dpar=dpar: e.dma_start(out=self.dw_l[:, dpar, :], in_=dlv[:, fb, :]),
                     wait=wldd, inc=self.c_ld[dpar], inc_k=16)
                dwn = [(self.c_ld[dpar], self.c_ld[dpar].n)]
                for tc in range(GCH // 128):
                    for (hi, h0, w512) in halves:
                        pso = self.psm[nh * tc + hi]
                        for pi in range(3):
                            ht = self.hh if pi != 1 else self.hl
                            wt = self.dw_h if pi < 2 else self.dw_l
                            first = (pi == 0 and tc == 0 and hi == 0)
                            w = []
                            if first:
                                w = dwn + [split_n[fb]]
                                if fb == 0:
                                    w.append((self.c_ev, self._eo_ev.get((tag, ci - 1), self._eo_prev_n)))
                            P.do("tensor", lambda e, ht=ht, wt=wt, fb=fb, tc=tc, h0=h0, w512=w512, dpar=dpar, pso=pso, pi=pi: e.matmul(
                                pso[:, 0:w512], ht[:, fb, 128 * tc:128 * (tc + 1)], wt[:, dpar, h0:h0 + w512],
                                start=(fb == 0 and pi == 0), stop=(fb == NFB - 1 and pi == 2)),
                                wait=w,
                                inc=(self.c_mm if (pi == 2 and tc == GCH // 128 - 1 and hi == nh - 1) else None))
                self._dw_mm[(tag, ci, fb)] = self.c_mm.n
            dn = (self.c_mm, self.c_mm.n)
            self._down_prev = self.c_mm.n
            nxt = self._pidx + 1
            nw = [(self.c_mm, self._gu_hist.get(nxt - 2, 0))]
            if ci + 1 < nchunks:
                self._xw_pref[nxt] = load_x(ci + 1, nw + list(extra_root), nxt % 2)
            elif next_load is not None:
                self._xw_pref[nxt] = next_load(nw, nxt % 2)
            for tc in range(GCH // 128):
                for (hi, h0, w512) in halves:
                    pso = self.psm[nh * tc + hi]
                    P.do("vector", lambda e, tc=tc, h0=h0, w512=w512, pso=pso: e.tensor_copy(self.eo[:, tc, h0:h0 + w512], pso[:, 0:w512]),
                         wait=[dn, prev_eo_done, (self.c_v, self.c_v.n)], inc=self.c_ev)
            self._eo_ev[(tag, ci)] = self.c_ev.n
            self._eo_prev_n = self.c_ev.n
            evn = (self.c_ev, self.c_ev.n)
            prev_eo_done = sink(ci, [evn])
            prev_gu_done = gu_done_ci
            self._gu_hist[self._pidx] = gu_done_ci
            self._pidx += 1
        self._mlp_gu_done = prev_gu_done
        self._mlp_eo_done = prev_eo_done

    def emit_expert_x(self, io, ci, ex, waits, px):
        P, GCH, NB = self.P, self.GCH, self.CAP // 128
        from bass_rust import IndirectOffsetOnAxis
        w0 = waits + [self.table_done]
        hb = io["xg_hb"].rearrange("(a p) d -> p a d", p=128)
        lb = io["xg_lb"].rearrange("(a p) d -> p a d", p=128)
        nslot = max(GCH // 256, 1)
        for half in range(max(GCH // (128 * nslot), 1)):
            for gs in range(nslot):
                g4 = half * nslot + gs
                col = ex * NB + (GCH * ci) // 128 + g4
                for src, dst in ((io["ag1h"], self.xg_h), (io["ag1l"], self.xg_l)):
                    P.do("gpsimd", lambda e, src=src, dst=dst, gs=gs, col=col: e.indirect_dma_start(
                        dst[:, gs, :], None, src,
                        IndirectOffsetOnAxis(self.tbli[:, col:col + 1], 0)),
                        wait=(w0 if g4 == 0 else []) + [(self.c_st[0], self._xg_st)], inc=self.c_gd[0], inc_k=16)
            gn = (self.c_gd[0], self.c_gd[0].n)
            P.do("gpsimd", lambda e, hb=hb, half=half: e.dma_start(
                out=hb[:, half * nslot:(half + 1) * nslot, :], in_=self.xg_h[:, :, :]),
                wait=[gn, (self.c_fm[0], getattr(self, "_xg_fm", 0))], inc=self.c_st[0], inc_k=16)
            P.do("gpsimd", lambda e, lb=lb, half=half: e.dma_start(
                out=lb[:, half * nslot:(half + 1) * nslot, :], in_=self.xg_l[:, :, :]), inc=self.c_st[0], inc_k=16)
            self._xg_st = self.c_st[0].n
        bn = (self.c_st[0], self.c_st[0].n)
        for dk in range(self.NDK):
            for src, dst in ((io["xg_hb"], self.xfm_h), (io["xg_lb"], self.xfm_l)):
                P.do("sync", lambda e, src=src, dst=dst, dk=dk, px=px: e.dma_start(
                    out=dst[:, px, dk, :], in_=src[0:GCH, 128 * dk:128 * (dk + 1)], transpose=True),
                    wait=([bn] + waits if dk == 0 else []), inc=self.c_fm[0], inc_k=16)
        self._xg_fm = self.c_fm[0].n
        return [(self.c_fm[0], self.c_fm[0].n)]

    def emit_shared_x(self, io, ci, waits, px):
        P, GCH = self.P, self.GCH
        t0 = GCH * ci
        tpc = GCH // 128
        lnw = [self.ln_tile_marks[tpc * (ci + 1) - 1]]
        if tpc * (ci + 1) - 2 >= 0:
            lnw.append(self.ln_tile_marks[tpc * (ci + 1) - 2])
        for dk in range(self.NDK):
            for src, dst in ((io["xln_h_t"], self.xfm_h), (io["xln_l_t"], self.xfm_l)):
                P.do("sync", lambda e, src=src, dst=dst, dk=dk, t0=t0, px=px: e.dma_start(
                    out=dst[:, px, dk, :], in_=src[t0:t0 + GCH, 128 * dk:128 * (dk + 1)], transpose=True),
                    wait=(waits + lnw if dk == 0 else []), inc=self.c_fm[1], inc_k=16)
        return [(self.c_fm[1], self.c_fm[1].n)]

    def emit_moe(self, io, attn_all):
        P = self.P
        self._xg_st = self.c_st[0].n
        shov = io["sho_d"].rearrange("(a p) d -> p a d", p=128)
        def sink_sh(ci, wait):
            for tc in range(self.GCHB):
                P.do("sync", lambda e, ci=ci, tc=tc, shov=shov: e.dma_start(
                    out=shov[:, self.GCHB * ci + tc, :], in_=self.eo[:, tc, :]),
                    wait=(wait if tc == 0 else []), inc=self.c_ws[1], inc_k=16)
            return [(self.c_ws[1], self.c_ws[1].n)]
        self.emit_mlp("sh", self.TC // self.GCH, lambda ci, w, px: self.emit_shared_x(io, ci, w, px),
                      io["sgu_h"], io["sgu_l"], io["sd_h"], io["sd_l"], sink_sh, [], self.wsplit_sh_done)
        sh_done = [(self.c_ws[1], self.c_ws[1].n)]
        self.wsplit_ex_done = self.emit_wsplits(io["wsplits_ex"], attn_all)
        self.emit_table(io)
        eov = io["eo_d"].rearrange("(a p) d -> p a d", p=128)
        NCHE = self.CAP // self.GCH
        for ex in range(2):
            def sink_ex(ci, wait, ex=ex):
                for tc in range(self.GCHB):
                    row = (ex * self.CAP + self.GCH * ci) // 128 + tc
                    P.do("gpsimd", lambda e, row=row, tc=tc, eov=eov: e.dma_start(
                        out=eov[:, row, :], in_=self.eo[:, tc, :]),
                        wait=(wait if tc == 0 else []), inc=self.c_st[1], inc_k=16)
                snn = (self.c_st[1], self.c_st[1].n)
                r0 = ex * self.CAP + ci * self.GCH
                P.do("gpsimd", lambda e, r0=r0, i=io["eo_d"], o=io["ag2_parts"][ex * NCHE + ci]: e.collective_compute(
                    "AllGather", mybir.AluOpType.bypass, replica_groups=[list(range(self.NCORE))],
                    ins=[i[r0:r0 + self.GCH, :]], outs=[o]),
                    wait=[snn], inc=self.c_cc)
                return [snn]
            nxtl = None
            if ex == 0:
                nxtl = lambda w, px: self.emit_expert_x(io, 0, 1, w + [self.table_done], px)
            self.emit_mlp(f"ex{ex}", self.CAP // self.GCH,
                          lambda ci, w, px, ex=ex: self.emit_expert_x(io, ci, ex, w, px),
                          io["gu_h"][ex], io["gu_l"][ex], io["d_h"][ex], io["d_l"][ex], sink_ex,
                          [self.table_done], self.wsplit_ex_done, next_load=nxtl)
        ag2n = (self.c_cc, self.c_cc.n)
        P.do("vector", lambda e: e.memset(self.fin[0:1, 0, :], 0.0), wait=[attn_all], inc=self.c_v)
        zn = (self.c_v, self.c_v.n)
        P.do("gpsimd", lambda e, z=io["ag2_zero"]: e.dma_start(out=z, in_=self.fin[0:1, 0, :]),
             wait=[zn], inc=self.c_gd[1], inc_k=16)
        zdn = (self.c_gd[1], self.c_gd[1].n)
        from bass_rust import IndirectOffsetOnAxis
        xlnf = io["xln_f"].rearrange("(a p) d -> p a d", p=128)
        outv = io["out"].rearrange("(a p) d -> p a d", p=128)
        gts = self.gates_done
        st_hist = {}
        cmb_hist = {}
        for a in range(self.NA):
            par = a % 2
            wld = [self.ln_done] if a < 2 else [(self.c_out, st_hist[a - 2])]
            P.do("sync", lambda e, a=a, par=par, xlnf=xlnf: e.dma_start(out=self.t_x[:, par, :], in_=xlnf[:, a, :]),
                 wait=wld, inc=self.c_ld[par], inc_k=16)
            P.do("sync", lambda e, a=a, par=par, shov=shov: e.dma_start(out=self.t_p1[:, par, :], in_=shov[:, a, :]),
                 wait=sh_done, inc=self.c_ld[par], inc_k=16)
            ldn = (self.c_ld[par], self.c_ld[par].n)
            for k in range(2):
                P.do("gpsimd", lambda e, a=a, k=k, par=par, src=io["ag2_full"]: e.indirect_dma_start(
                    self.cmb[:, k, :], None, src, IndirectOffsetOnAxis(self.cgi[:, a, k:k + 1], 0)),
                    wait=[ag2n, zdn, gts] + ([] if a < 1 else [(self.c_v, cmb_hist.get(a - 1, 0))]),
                    inc=self.c_gd[par], inc_k=16)
            gn = (self.c_gd[par], self.c_gd[par].n)
            P.do("vector", lambda e, par=par: e.tensor_tensor(self.fin[:, par, :], self.t_x[:, par, :], self.t_p1[:, par, :], op=ALU.add),
                 wait=[ldn, gn, (self.c_v, self.c_v.n)], inc=self.c_v)
            P.do("vector", lambda e, par=par: e.tensor_tensor(self.fin[:, par, :], self.fin[:, par, :], self.t_x[:, par, :], op=ALU.add),
                 wait=[(self.c_v, self.c_v.n)], inc=self.c_v)
            for k in range(2):
                P.do("vector", lambda e, a=a, k=k, par=par: e.tensor_scalar(
                    self.cmb[:, k, :], self.cmb[:, k, :], self.gsc[:, a, k:k + 1], None, op0=ALU.mult),
                    wait=[(self.c_v, self.c_v.n)], inc=self.c_v)
                P.do("vector", lambda e, k=k, par=par: e.tensor_tensor(self.fin[:, par, :], self.fin[:, par, :], self.cmb[:, k, :], op=ALU.add),
                     wait=[(self.c_v, self.c_v.n)], inc=self.c_v)
            sn = (self.c_v, self.c_v.n)
            cmb_hist[a] = self.c_v.n
            P.do("vector", lambda e, par=par: e.reduce_sum(self.stat[:, 0:1], self.fin[:, par, :], axis=AX.X),
                 wait=[sn], inc=self.c_v)
            P.do("scalar", lambda e, par=par: e.activation(self.t_x[:, par, :], self.fin[:, par, :], AF.Square, accum_out=self.stat[:, 1:2]),
                 wait=[sn, (self.c_s, self.c_s.n)], inc=self.c_s)
            sqn = (self.c_s, self.c_s.n)
            P.do("vector", lambda e: e.tensor_scalar(self.stat[:, 2:3], self.stat[:, 0:1], self.rinv[:, :], None, op0=ALU.mult),
                 wait=[(self.c_v, self.c_v.n)], inc=self.c_v)
            P.do("vector", lambda e: e.tensor_scalar(self.stat[:, 3:4], self.stat[:, 1:2], self.rinv[:, :], None, op0=ALU.mult),
                 wait=[sqn, (self.c_v, self.c_v.n)], inc=self.c_v)
            P.do("vector", lambda e: e.tensor_tensor(self.stat[:, 4:5], self.stat[:, 2:3], self.stat[:, 2:3], op=ALU.mult),
                 wait=[(self.c_v, self.c_v.n)], inc=self.c_v)
            P.do("vector", lambda e: e.tensor_tensor(self.stat[:, 4:5], self.stat[:, 3:4], self.stat[:, 4:5], op=ALU.subtract),
                 wait=[(self.c_v, self.c_v.n)], inc=self.c_v)
            vn = (self.c_v, self.c_v.n)
            P.do("scalar", lambda e: e.activation(self.stat[:, 5:6], self.stat[:, 4:5], AF.Ln, bias=self.eps[:, :]),
                 wait=[vn, (self.c_s, self.c_s.n)], inc=self.c_s)
            P.do("scalar", lambda e: e.activation(self.stat[:, 5:6], self.stat[:, 5:6], AF.Exp, scale=self.cm05[:, :]),
                 wait=[(self.c_s, self.c_s.n)], inc=self.c_s)
            rsn = (self.c_s, self.c_s.n)
            P.do("vector", lambda e, par=par: e.tensor_scalar(self.fin[:, par, :], self.fin[:, par, :], self.stat[:, 2:3], self.stat[:, 5:6], op0=ALU.subtract, op1=ALU.mult),
                 wait=[rsn, (self.c_v, self.c_v.n)], inc=self.c_v)
            fn = (self.c_v, self.c_v.n)
            P.do("gpsimd", lambda e, a=a, par=par, outv=outv: e.dma_start(out=outv[:, a, :], in_=self.fin[:, par, :]),
                 wait=[fn], inc=self.c_out, inc_k=16)
            st_hist[a] = self.c_out.n
        P.do("gpsimd", lambda e: None, wait=[(self.c_out, self.c_out.n)])


def build_program(cfg):
    from contextlib import ExitStack
    nc = bass.Bass("TRN2", target_bir_lowering=False)
    TB, TC, DM, HPC = cfg["TB"], cfg["TC"], cfg["DM"], cfg["HPC"]
    FF, E, CAP, NCORE = cfg["FF"], cfg["E"], cfg["CAP"], cfg["NCORE"]
    d = lambda n, sh, dt, k=None: nc.dram_tensor(n, sh, dt, kind=k) if k else nc.dram_tensor(n, sh, dt)
    x_b = d("x_b", [TB, DM], F32, "ExternalInput").ap()
    x_own = d("x_own", [TC, DM], F32, "ExternalInput").ap()
    wq = d("wq", [DM, HPC * 64], F32, "ExternalInput").ap()
    wk = d("wk", [DM, HPC * 64], F32, "ExternalInput").ap()
    wv = d("wv", [DM, HPC * 64], F32, "ExternalInput").ap()
    wp = d("wp", [HPC * 64, DM], F32, "ExternalInput").ap()
    draw = d("draw", [1, HPC], F32, "ExternalInput").ap()
    router_w = d("router_w", [DM, E], F32, "ExternalInput").ap()
    router_b = d("router_b", [1, E], F32, "ExternalInput").ap()
    msk = d("msk", [NCORE, 1], F32, "ExternalInput").ap()
    hoff = d("hoff", [1, 1], U32, "ExternalInput").ap()
    e0in = d("e0in", [1, 1], F32, "ExternalInput").ap()
    wgu = d("wgu", [2 * DM, 2 * FF], F32, "ExternalInput").ap()
    wdn = d("wdn", [2 * FF, DM], F32, "ExternalInput").ap()
    sgu = d("sgu", [DM, 2 * FF], F32, "ExternalInput").ap()
    sdn = d("sdn", [FF, DM], F32, "ExternalInput").ap()
    out = d("out", [TC, DM], F32, "ExternalOutput").ap()
    xhb = d("xhb", [TB, DM], BF16).ap()
    xlb = d("xlb", [TB, DM], BF16).ap()
    part = d("part", [TB, DM], F32).ap()
    partsum = d("partsum", [TB, DM], F32).ap()
    xln_f = d("xln_f", [TC, DM], F32).ap()
    xln_h = d("xln_h", [TC, DM], BF16).ap()
    xln_l = d("xln_l", [TC, DM], BF16).ap()
    cnt_d = d("cnt_d", [1, E], F32).ap()
    agc = d("agc", [NCORE * E], F32).ap()
    rec_d = d("rec_d", [2 * TC], F32).ap()
    ag1r = d("ag1r", [NCORE * 2 * TC], F32).ap()
    ag1h = d("ag1h", [NCORE * TC, DM], BF16).ap()
    ag1l = d("ag1l", [NCORE * TC, DM], BF16).ap()
    GCH = cfg.get("GCH", 512)
    xg_hb = d("xg_hb", [GCH, DM], BF16).ap()
    xg_lb = d("xg_lb", [GCH, DM], BF16).ap()
    gu_h = d("gu_h", [2 * DM, 2 * FF], BF16).ap()
    gu_l = d("gu_l", [2 * DM, 2 * FF], BF16).ap()
    d_h = d("d_h", [2 * FF, DM], BF16).ap()
    d_l = d("d_l", [2 * FF, DM], BF16).ap()
    sgu_h = d("sgu_h", [DM, 2 * FF], BF16).ap()
    sgu_l = d("sgu_l", [DM, 2 * FF], BF16).ap()
    sd_h = d("sd_h", [FF, DM], BF16).ap()
    sd_l = d("sd_l", [FF, DM], BF16).ap()
    sho_d = d("sho_d", [TC, DM], F32).ap()
    eo_d = d("eo_d", [2 * CAP, DM], F32).ap()
    ag2 = d("ag2", [NCORE * 2 * CAP + 1, DM], F32).ap()
    P = Prog(nc)
    with ExitStack() as semctx:
        with nc.Block() as block:
            with ExitStack() as actx:
                A = Attn(nc, P, actx, {"T": TB, "DM": DM, "HPC": HPC, "causal": True}, semctx)
                A.emit_xsplit(x_b, xhb, xlb)
                A.emit_tables(draw)
                ioA = {"wq": wq, "wk": wk, "wv": wv, "wp": wp, "xhb": xhb, "xlb": xlb, "part": part}
                for hp in range(HPC // 2):
                    A.emit_hp(hp, ioA)
            attn_all = [(c, c.n) for c in P._ctrs]
            with ExitStack() as mctx:
                M = Moe(nc, P, mctx, {"TC": TC, "DM": DM, "FF": FF, "E": E, "CAP": CAP,
                                      "NCORE": NCORE, "GCH": GCH}, semctx)
                ioM = {
                    "x_own": x_own, "part": part, "partsum": partsum, "hoff": hoff,
                    "xln_f": xln_f, "xln_h": xln_h, "xln_l": xln_l,
                    "router_w": router_w, "router_b": router_b, "msk": msk,
                    "cnt_d": cnt_d, "agc": agc, "rec_d": rec_d, "ag1r": ag1r,
                    "ag1h": ag1h, "ag1l": ag1l, "xg_hb": xg_hb, "xg_lb": xg_lb,
                    "e0in": e0in,
                    "wsplits_ex": [(wgu, gu_h, gu_l), (wdn, d_h, d_l)],
                    "wsplits_sh": [(sgu, sgu_h, sgu_l), (sdn, sd_h, sd_l)],
                    "gu_h": [gu_h[DM * ex:DM * (ex + 1), :] for ex in range(2)],
                    "gu_l": [gu_l[DM * ex:DM * (ex + 1), :] for ex in range(2)],
                    "d_h": [d_h[FF * ex:FF * (ex + 1), :] for ex in range(2)],
                    "d_l": [d_l[FF * ex:FF * (ex + 1), :] for ex in range(2)],
                    "sgu_h": sgu_h, "sgu_l": sgu_l, "sd_h": sd_h, "sd_l": sd_l,
                    "sho_d": sho_d, "eo_d": eo_d,
                    "ag2_parts": [ag2[j * NCORE * cfg["GCH"]:(j + 1) * NCORE * cfg["GCH"], :]
                                  for j in range(2 * (CAP // cfg["GCH"]))],
                    "ag2_zero": ag2[NCORE * 2 * CAP:NCORE * 2 * CAP + 1, :],
                    "ag2_full": ag2, "out": out,
                    "xln_h_t": xln_h, "xln_l_t": xln_l,
                }
                M.emit_consts(ioM, attn_all)
                M.wsplit_sh_done = M.emit_wsplits(ioM["wsplits_sh"], attn_all, fast=True)
                M.emit_ag0_ln_router(ioM, attn_all)
                M.emit_gates_records(ioM)
                M.emit_moe(ioM, attn_all)
                if cfg.get("debug"):
                    dbg = {}
                    fin_w = [(c, c.n) for c in P._ctrs]
                    def dump_dram(name, src_ap):
                        t = nc.dram_tensor("o_" + name, list(src_ap.shape), src_ap.dtype, kind="ExternalOutput").ap()
                        P.do("gpsimd", lambda e, t=t, s=src_ap: e.dma_start(out=t, in_=s), wait=fin_w, inc=M.c_g, inc_k=16)
                    def dump_sb(name, t_sb):
                        t = nc.dram_tensor("o_" + name, list(t_sb.shape), t_sb.dtype, kind="ExternalOutput").ap()
                        P.do("gpsimd", lambda e, t=t, s=t_sb: e.dma_start(out=t, in_=s[:]), wait=fin_w, inc=M.c_g, inc_k=16)
                    dump_dram("xln", xln_f)
                    dump_dram("psum", partsum)
                    dump_dram("sho", sho_d)
                    dump_dram("eo", eo_d)
                    dump_dram("agc", agc)
                    dump_dram("ag1r", ag1r)
                    dump_sb("idxf", M.idxf.ap())
                    dump_sb("valn", M.valn.ap())
                    dump_sb("posl", M.posl.ap())
                    dump_sb("gsc", M.gsc.ap())
                    dump_sb("cgi", M.cgi.ap())
                    dump_sb("tbl", M.tbl.ap())
                    dump_sb("recs", M.recs.ap())
                    dump_sb("ef", M.ef.ap())
                    dump_sb("pf", M.pf.ap())
                    dump_sb("colf", M.colf.ap())
                    dump_sb("rrow", M.rrow.ap())
                    dump_sb("tid", M.tidt.ap())
                    dump_sb("hh", M.hh.ap())
                    dump_sb("hl", M.hl.ap())
                    P.do("gpsimd", lambda e: None, wait=[(M.c_g, M.c_g.n)])
            P.emit(block)
    return nc


_CFG = {"TB": 4096, "TC": 2048, "DM": 1024, "HPC": 8, "FF": 1408, "E": 16,
        "CAP": 2560, "NCORE": 8, "GCH": 512}


def _interleave_gu(wg, wu):
    blocks = []
    for j in range(wg.shape[1] // 128):
        blocks.append(wg[:, 128 * j:128 * (j + 1)])
        blocks.append(wu[:, 128 * j:128 * (j + 1)])
    return np.concatenate(blocks, axis=1)


def kernel(L0_attn_x, L0_attn_w_attn, L0_attn_w_proj, L0_attn_decay_raw,
           L0_ffn_router_w, L0_ffn_router_b, L0_ffn_w_gate, L0_ffn_w_up,
           L0_ffn_w_down, L0_ffn_sw_gate, L0_ffn_sw_up, L0_ffn_sw_down):
    from concourse.bass_utils import run_bass_kernel_spmd
    cfg = _CFG
    TB, TC, DM, HPC, NCORE = cfg["TB"], cfg["TC"], cfg["DM"], cfg["HPC"], cfg["NCORE"]
    x = np.asarray(L0_attn_x)
    wa = np.asarray(L0_attn_w_attn)
    wpj = np.asarray(L0_attn_w_proj)
    draw = np.asarray(L0_attn_decay_raw)
    rw = np.asarray(L0_ffn_router_w)
    rb = np.asarray(L0_ffn_router_b)
    wg = np.asarray(L0_ffn_w_gate)
    wu = np.asarray(L0_ffn_w_up)
    wd = np.asarray(L0_ffn_w_down)
    sg_ = np.asarray(L0_ffn_sw_gate)
    su_ = np.asarray(L0_ffn_sw_up)
    sd_ = np.asarray(L0_ffn_sw_down)
    nc = build_program(cfg)
    sgu = np.ascontiguousarray(_interleave_gu(sg_, su_))
    in_maps = []
    for c in range(NCORE):
        b, half = c // 2, c % 2
        heads = [half * HPC + i for i in range(HPC)]
        wq = np.concatenate([wa[:, 64 * h:64 * h + 64] for h in heads], axis=1)
        wk = np.concatenate([wa[:, DM + 64 * h:DM + 64 * h + 64] for h in heads], axis=1)
        wv = np.concatenate([wa[:, 2 * DM + 64 * h:2 * DM + 64 * h + 64] for h in heads], axis=1)
        wpc = np.concatenate([wpj[64 * h:64 * h + 64, :] for h in heads], axis=0)
        e0, e1 = 2 * c, 2 * c + 1
        wgu = np.concatenate([_interleave_gu(wg[e0], wu[e0]), _interleave_gu(wg[e1], wu[e1])], axis=0)
        wdn = np.concatenate([wd[e0], wd[e1]], axis=0)
        in_maps.append({
            "x_b": np.ascontiguousarray(x[b]),
            "x_own": np.ascontiguousarray(x[b][half * TC:(half + 1) * TC]),
            "wq": np.ascontiguousarray(wq), "wk": np.ascontiguousarray(wk),
            "wv": np.ascontiguousarray(wv), "wp": np.ascontiguousarray(wpc),
            "draw": np.ascontiguousarray(draw[heads]).reshape(1, HPC),
            "router_w": np.ascontiguousarray(rw),
            "router_b": np.ascontiguousarray(rb).reshape(1, cfg["E"]),
            "msk": (np.arange(NCORE) < c).astype(np.float32).reshape(NCORE, 1),
            "hoff": np.array([[half * (TC // 128)]], dtype=np.uint32),
            "e0in": np.array([[2.0 * c]], dtype=np.float32),
            "wgu": np.ascontiguousarray(wgu), "wdn": np.ascontiguousarray(wdn),
            "sgu": sgu, "sdn": np.ascontiguousarray(sd_),
        })
    res = run_bass_kernel_spmd(nc, in_maps, core_ids=list(range(NCORE)))
    out = np.concatenate([np.asarray(res.results[c]["out"]) for c in range(NCORE)], axis=0)
    return out.reshape(x.shape[0], TB, DM).astype(np.float32)
